# version 06
# baseline (speedup 1.0000x reference)
import numpy as np
import ml_dtypes
from contextlib import ExitStack

import concourse.bass as bass
import concourse.mybir as mybir
from concourse.bass import IndirectOffsetOnAxis
from concourse.bass_utils import run_bass_kernel_spmd

F32 = mybir.dt.float32
BF16 = mybir.dt.bfloat16
U32 = mybir.dt.uint32
AF = mybir.ActivationFunctionType
ALU = mybir.AluOpType
AX = mybir.AxisListType

NC = 8
B, T, D, H, DH = 16, 4096, 1024, 16, 64
NPAIR = 8
BPC = B // NC
NT = T // 128
NLOC = BPC * T
NTL = NLOC // 128
E, FF = 8, 4096
CAP = 2 * B * T // E
PEXP = 2304
RH = PEXP // 2
NRH = RH // 128
THRESH, SHARP = 0.29514, 15.0
LN_EPS, NORM_EPS = 1e-5, 1e-12
KT = D // 128


class Ctr:
    def __init__(self, sem, step=1):
        self.sem, self.val, self.step = sem, 0, step

    def inc(self, instr, n=None):
        n = self.step if n is None else n
        instr.then_inc(self.sem, n)
        self.val += n
        return instr


class Dep:
    __slots__ = ("w", "r")
    def __init__(self):
        self.w, self.r = None, []


class DmaPool:
    def __init__(self, subs):
        self.subs = subs
        self.i = 0

    def next(self):
        c = self.subs[self.i % len(self.subs)]
        self.i += 1
        return c


class Builder:
    def __init__(self, block, ctrs, pools):
        self.block = block
        self.ctrs = ctrs
        self.pools = pools
        self.seen = {}
        self.queues = {}
        self.pending = {}

    def barrier(self):
        snap = [(c, c.val) for c in self.ctrs.values()]
        for pl in self.pools.values():
            snap += [(c, c.val) for c in pl.subs]
        for ename in ("sync", "scalar", "vector", "tensor", "gpsimd"):
            self.pending[ename] = list(snap)

    def _want(self, ename, ctr, val, waits, own):
        if val <= 0:
            return
        if ename == "tensor" and own is not None and ctr is own:
            return
        key = id(ctr.sem)
        seen = self.seen.setdefault(ename, {})
        if seen.get(key, -1) >= val:
            return
        seen[key] = val
        waits.append((ctr.sem, val))

    def emit(self, ename, cname, fn, reads=(), writes=()):
        if cname in self.pools:
            ctr = self.pools[cname].next()
        else:
            ctr = self.ctrs[cname]
        waits = []
        for (c, v) in self.pending.pop(ename, ()):
            self._want(ename, c, v, waits, None)
        if ctr.step == 16:
            self._want(ename, ctr, ctr.val, waits, None)
        for d in reads:
            if d.w is not None:
                self._want(ename, d.w[0], d.w[1], waits, ctr)
        for d in writes:
            if d.w is not None:
                self._want(ename, d.w[0], d.w[1], waits, ctr)
            for (c, v) in d.r:
                self._want(ename, c, v, waits, ctr)
        n = ctr.step
        self.queues.setdefault(ename, []).append((waits, fn, ctr.sem, n))
        ctr.val += n
        for d in reads:
            d.r.append((ctr, ctr.val))
        for d in writes:
            d.w = (ctr, ctr.val)
            d.r = []

    def flush(self):
        for ename, q in self.queues.items():
            def body(eng, q=q):
                for (waits, fn, sem, n) in q:
                    for (s, v) in waits:
                        eng.wait_ge(s, v)
                    fn(eng).then_inc(sem, n)
            getattr(self.block, ename)(body)
        self.queues = {}


def build_nc():
    nc = bass.Bass("TRN2", target_bir_lowering=False)

    x_in = nc.dram_tensor("x_in", [BPC, T, D], F32, kind="ExternalInput").ap()
    wq_in = nc.dram_tensor("wq_in", [D, D], F32, kind="ExternalInput").ap()
    wk_in = nc.dram_tensor("wk_in", [D, D], F32, kind="ExternalInput").ap()
    wv_in = nc.dram_tensor("wv_in", [D, D], F32, kind="ExternalInput").ap()
    wo_in = nc.dram_tensor("wo_in", [D, D], F32, kind="ExternalInput").ap()
    gq_in = nc.dram_tensor("gq_in", [1, DH], F32, kind="ExternalInput").ap()
    gk_in = nc.dram_tensor("gk_in", [1, DH], F32, kind="ExternalInput").ap()
    bo_in = nc.dram_tensor("bo_in", [1, D], F32, kind="ExternalInput").ap()
    lnw_in = nc.dram_tensor("lnw_in", [D, 1], F32, kind="ExternalInput").ap()
    lnb_in = nc.dram_tensor("lnb_in", [D, 1], F32, kind="ExternalInput").ap()
    wr_in = nc.dram_tensor("wr_in", [D, E], F32, kind="ExternalInput").ap()
    w1_in = nc.dram_tensor("w1_in", [E, D, FF], F32, kind="ExternalInput").ap()
    w2_in = nc.dram_tensor("w2_in", [E, FF, D], F32, kind="ExternalInput").ap()
    fb_in = nc.dram_tensor("fb_in", [1, D], F32, kind="ExternalInput").ap()
    ident_in = nc.dram_tensor("ident_in", [128, 128], BF16, kind="ExternalInput").ap()
    utri_in = nc.dram_tensor("utri_in", [128, 128], BF16, kind="ExternalInput").ap()
    iota8_in = nc.dram_tensor("iota8_in", [1, E], F32, kind="ExternalInput").ap()
    rowmask_in = nc.dram_tensor("rowmask_in", [1, 128], F32, kind="ExternalInput").ap()
    selcol_in = nc.dram_tensor("selcol_in", [128, 1], F32, kind="ExternalInput").ap()
    e2map_in = nc.dram_tensor("e2map_in", [2, 128], BF16, kind="ExternalInput").ap()

    out_t = nc.dram_tensor("out_t", [BPC, T, D], F32, kind="ExternalOutput").ap()

    wqkv_d = nc.dram_tensor("wqkv_d", [KT, 128, NPAIR, 384], BF16, kind="Internal").ap()
    collT_d = nc.dram_tensor("collT_d", [D, T], BF16, kind="Internal").ap()
    z2_d = nc.dram_tensor("z2_d", [NLOC, D], BF16, kind="Internal").ap()
    xdisp_d = nc.dram_tensor("xdisp_d", [E * PEXP, D], BF16, kind="Internal").ap()
    ybuf_d = nc.dram_tensor("ybuf_d", [E * PEXP, D], F32, kind="Internal").ap()
    ccs_d = nc.dram_tensor("ccs_d", [128, 128], F32, kind="Internal").ap()
    ccd_d = nc.dram_tensor("ccd_d", [128, 128], F32, kind="Internal").ap()

    with ExitStack() as top:
        TE = top.enter_context

        KPOOL = 8
        ctrs = {
            "tC": Ctr(TE(nc.semaphore("tC"))),
            "vC": Ctr(TE(nc.semaphore("vC"))),
            "sC": Ctr(TE(nc.semaphore("sC"))),
            "cc": Ctr(TE(nc.semaphore("ccsm"))),
        }
        pools = {
            q: DmaPool([Ctr(TE(nc.semaphore(f"{q}_{j}")), 16) for j in range(KPOOL)])
            for q in ("qS", "qA", "qG")
        }

        cq_b = TE(nc.sbuf_tensor("cq_b", [128, D], BF16))
        ck_b = TE(nc.sbuf_tensor("ck_b", [128, D], BF16))
        sv_b = TE(nc.sbuf_tensor("sv_b", [128, D], BF16))
        bo_b = TE(nc.sbuf_tensor("bo_b", [128, D], BF16))
        fb_b = TE(nc.sbuf_tensor("fb_b", [128, D], BF16))
        gq_b = TE(nc.sbuf_tensor("gq_b", [128, 128], F32))
        gk_b = TE(nc.sbuf_tensor("gk_b", [128, 128], F32))
        io8_b = TE(nc.sbuf_tensor("io8_b", [128, E], F32))
        ident = TE(nc.sbuf_tensor("ident", [128, 128], BF16))
        utri = TE(nc.sbuf_tensor("utri", [128, 128], BF16))
        ones_f = TE(nc.sbuf_tensor("ones_f", [1, 128], F32))
        ones_bf = TE(nc.sbuf_tensor("ones_bf", [1, 128], BF16))
        ones_bcol = TE(nc.sbuf_tensor("ones_bcol", [128, 1], BF16))
        selcol = TE(nc.sbuf_tensor("selcol", [128, 1], F32))
        rowmask = TE(nc.sbuf_tensor("rowmask", [1, 128], F32))
        sigb = TE(nc.sbuf_tensor("sigb", [128, 1], F32))
        lnw_s = TE(nc.sbuf_tensor("lnw_s", [128, KT], F32))
        lnb_s = TE(nc.sbuf_tensor("lnb_s", [128, KT], BF16))
        wr_s = TE(nc.sbuf_tensor("wr_s", [128, KT, E], BF16))
        es_t = TE(nc.sbuf_tensor("es_t", [128, NTL, 2], F32))
        ws_t = TE(nc.sbuf_tensor("ws_t", [128, NTL, 2], F32))
        rs_t = TE(nc.sbuf_tensor("rs_t", [128, NTL, 2], F32))
        run_row = TE(nc.sbuf_tensor("run_row", [1, E], F32))
        runb_s = TE(nc.sbuf_tensor("runb_s", [128, E], F32))
        cpx_b = TE(nc.sbuf_tensor("cpx_b", [128, E], F32))
        var2_t = TE(nc.sbuf_tensor("var2_t", [128, NTL], F32))
        rstd2_t = TE(nc.sbuf_tensor("rstd2_t", [128, NTL], F32))
        var3_t = TE(nc.sbuf_tensor("var3_t", [128, NTL], F32))
        rstd3_t = TE(nc.sbuf_tensor("rstd3_t", [128, NTL], F32))
        lgz_t = TE(nc.sbuf_tensor("lgz_t", [128, NTL, E], F32))
        msc = TE(nc.sbuf_tensor("msc", [128, 16], F32))
        m1x = TE(nc.sbuf_tensor("m1x", [128, E], F32))
        m2x = TE(nc.sbuf_tensor("m2x", [128, E], F32))
        cab = TE(nc.sbuf_tensor("cab", [128, E], F32))
        prb = TE(nc.sbuf_tensor("prb", [128, E], F32))
        top8 = TE(nc.sbuf_tensor("top8", [128, 8], F32))
        idx8 = TE(nc.sbuf_tensor("idx8", [128, 8], U32))
        idxu = TE(nc.sbuf_tensor("idxu", [128, 2], U32))
        mm8 = TE(nc.sbuf_tensor("mm8", [128, E], BF16))
        cnt128 = TE(nc.sbuf_tensor("cnt128", [128, 128], F32))

        dp = {n: Dep() for n in [
            "consts", "lnw", "wr", "route", "run", "runb", "cpx",
            "var2", "rstd2", "var3", "rstd3", "lgz",
            "wqkvd", "collTd", "z2d", "xdispd", "ybufd", "ccsd", "ccdd", "outd",
            "msc", "m12", "cab", "prb", "top", "idxu", "mm8", "cnt"]}

        block = TE(nc.Block())
        bld = Builder(block, ctrs, pools)
        EM = bld.emit

        def bcast_row_f32(dst, d_dst, row_ap, d_row, ps, d_ps, width=D):
            nch = (width + 511) // 512
            for ch in range(nch):
                w = min(512, width - ch * 512)
                sl = slice(ch * 512, ch * 512 + w)
                EM("tensor", "tC", lambda e, sl=sl, w=w: e.matmul(
                    ps[:, 0:w], ones_f[:], row_ap[:, sl], start=True, stop=True),
                    reads=[dp["consts"], d_row], writes=[d_ps])
                EM("vector", "vC", lambda e, sl=sl, w=w: e.tensor_copy(
                    out=dst[:, sl], in_=ps[:, 0:w]), reads=[d_ps], writes=[d_dst])

        with ExitStack() as pre:
            PP = pre.enter_context
            stage = PP(nc.sbuf_tensor("stage", [128, KT, D], BF16))
            row_s = PP(nc.sbuf_tensor("row_s", [1, D], F32))
            zro = PP(nc.sbuf_tensor("zro", [128, 8192], BF16))
            ps_p = PP(nc.psum_tensor("ps_p", [128, 512], F32))
            ps_q = PP(nc.psum_tensor("ps_q", [1, 512], F32))
            d_stage, d_row, d_psp, d_psq, d_zro = Dep(), Dep(), Dep(), Dep(), Dep()

            EM("sync", "qS", lambda e: e.dma_start(out=ident[:], in_=ident_in), writes=[dp["consts"]])
            EM("sync", "qS", lambda e: e.dma_start(out=utri[:], in_=utri_in), writes=[dp["consts"]])
            EM("sync", "qS", lambda e: e.dma_start(out=selcol[:], in_=selcol_in), writes=[dp["consts"]])
            EM("sync", "qS", lambda e: e.dma_start(out=rowmask[:], in_=rowmask_in), writes=[dp["consts"]])
            EM("sync", "qS", lambda e: e.dma_start(out=gq_b[0:1, 0:DH], in_=gq_in), writes=[dp["consts"]])
            EM("sync", "qS", lambda e: e.dma_start(out=gq_b[0:1, DH:128], in_=gq_in), writes=[dp["consts"]])
            EM("sync", "qS", lambda e: e.dma_start(out=gk_b[0:1, 0:DH], in_=gk_in), writes=[dp["consts"]])
            EM("sync", "qS", lambda e: e.dma_start(out=gk_b[0:1, DH:128], in_=gk_in), writes=[dp["consts"]])
            EM("sync", "qS", lambda e: e.dma_start(out=io8_b[0:1, :], in_=iota8_in), writes=[dp["consts"]])
            for kt in range(KT):
                EM("sync", "qS", lambda e, kt=kt: e.dma_start(
                    out=lnw_s[:, kt:kt + 1], in_=lnw_in[kt * 128:(kt + 1) * 128, :]), writes=[dp["lnw"]])
                EM("gpsimd", "qG", lambda e, kt=kt: e.dma_start(
                    out=lnb_s[:, kt:kt + 1], in_=lnb_in[kt * 128:(kt + 1) * 128, :]), writes=[dp["lnw"]])
                EM("gpsimd", "qG", lambda e, kt=kt: e.dma_start(
                    out=wr_s[:, kt, :], in_=wr_in[kt * 128:(kt + 1) * 128, :]), writes=[dp["wr"]])

            EM("vector", "vC", lambda e: e.memset(ones_f[:], 1.0), writes=[dp["consts"]])
            EM("vector", "vC", lambda e: e.memset(ones_bf[:], 1.0), writes=[dp["consts"]])
            EM("vector", "vC", lambda e: e.memset(ones_bcol[:], 1.0), writes=[dp["consts"]])
            EM("vector", "vC", lambda e: e.memset(sigb[:], -SHARP * THRESH), writes=[dp["consts"]])
            EM("vector", "vC", lambda e: e.memset(run_row[:], 0.0), writes=[dp["run"]])
            EM("vector", "vC", lambda e: e.memset(cnt128[:], 0.0), writes=[dp["cnt"]])
            EM("vector", "vC", lambda e: e.memset(zro[:], 0.0), writes=[d_zro])

            EM("sync", "qS", lambda e: e.dma_start(out=row_s[:], in_=bo_in), writes=[d_row])
            bcast_row_f32(bo_b, dp["consts"], row_s, d_row, ps_p, d_psp)
            EM("sync", "qS", lambda e: e.dma_start(out=row_s[:], in_=fb_in), writes=[d_row])
            bcast_row_f32(fb_b, dp["consts"], row_s, d_row, ps_p, d_psp)

            for (gb, w) in ((gq_b, 128), (gk_b, 128), (io8_b, E)):
                EM("tensor", "tC", lambda e, gb=gb, w=w: e.matmul(
                    ps_p[:, 0:w], ones_f[:], gb[0:1, 0:w], start=True, stop=True),
                    reads=[dp["consts"]], writes=[d_psp])
                EM("vector", "vC", lambda e, gb=gb, w=w: e.tensor_copy(
                    out=gb[:, 0:w], in_=ps_p[:, 0:w]), reads=[d_psp], writes=[dp["consts"]])

            xd_r = xdisp_d.rearrange("(n p) d -> n p d", p=128)
            for i in range(18):
                EM("gpsimd", "qG", lambda e, i=i: e.dma_start(
                    out=xd_r[8 * i:8 * (i + 1)].rearrange("n p d -> p n d"),
                    in_=zro[:].rearrange("p (n x) -> p n x", x=1024)),
                    reads=[d_zro], writes=[dp["xdispd"]])

            def stage_weight(w_src, off, fold, row_kind, dst_b):
                for kt in range(KT):
                    EM("gpsimd", "qG", lambda e, kt=kt: e.dma_start(
                        out=stage[:, kt, :], in_=w_src[kt * 128:(kt + 1) * 128, :]), writes=[d_stage])
                if row_kind is not None:
                    for ch in range(2):
                        sl = slice(ch * 512, (ch + 1) * 512)
                        for kt in range(KT):
                            lhs = (lnb_s[:, kt:kt + 1] if row_kind == "lnb" else ones_bcol[:])
                            EM("tensor", "tC", lambda e, kt=kt, sl=sl, lhs=lhs: e.matmul(
                                ps_q[:, 0:512], lhs, stage[:, kt, sl],
                                start=(kt == 0), stop=(kt == KT - 1)),
                                reads=[d_stage, dp["lnw"], dp["consts"]], writes=[d_psq])
                        EM("vector", "vC", lambda e, sl=sl: e.tensor_copy(
                            out=row_s[0:1, sl], in_=ps_q[0:1, 0:512]),
                            reads=[d_psq], writes=[d_row])
                    bcast_row_f32(dst_b, dp["consts"], row_s, d_row, ps_p, d_psp)
                if fold:
                    for kt in range(KT):
                        EM("vector", "vC", lambda e, kt=kt: e.tensor_scalar(
                            out=stage[:, kt, :], in0=stage[:, kt, :],
                            scalar1=lnw_s[:, kt:kt + 1], scalar2=None, op0=ALU.mult),
                            reads=[dp["lnw"], d_stage], writes=[d_stage])
                for kt in range(KT):
                    EM("gpsimd", "qG", lambda e, kt=kt, off=off: e.dma_start(
                        out=wqkv_d[kt, :, :, off:off + 128],
                        in_=stage[:, kt, :].rearrange("p (pr c) -> p pr c", c=128)),
                        reads=[d_stage], writes=[dp["wqkvd"]])

            stage_weight(wq_in, 0, True, "lnb", cq_b)
            stage_weight(wk_in, 128, True, "lnb", ck_b)
            stage_weight(wv_in, 256, False, "ones", sv_b)

        bld.barrier()
        with ExitStack() as att:
            AE = att.enter_context
            zT = AE(nc.sbuf_tensor("zT", [128, KT, T], BF16))
            qkv = AE(nc.sbuf_tensor("qkv", [128, NT, 384], BF16))
            vbuf = AE(nc.sbuf_tensor("vbuf", [128, 2, NT, 128], BF16))
            qnT = AE(nc.sbuf_tensor("qnT", [128, 2, T], BF16))
            knT = AE(nc.sbuf_tensor("knT", [128, 2, T], BF16))
            modT = AE(nc.sbuf_tensor("modT", [128, 3, 1024], BF16))
            sc8k = AE(nc.sbuf_tensor("sc8k", [128, 4096], BF16))
            gqbc = AE(nc.sbuf_tensor("gqbc", [128, T], BF16))
            xa2 = AE(nc.sbuf_tensor("xa2", [128, 2, D], F32))
            wpair = AE(nc.sbuf_tensor("wpair", [128, 2, KT, 384], BF16))
            mu_t = AE(nc.sbuf_tensor("mu_t", [128, NT], F32))
            rstd_t = AE(nc.sbuf_tensor("rstd_t", [128, NT], F32))
            var_t = AE(nc.sbuf_tensor("var_t", [128, NT], F32))
            ssq = AE(nc.sbuf_tensor("ssq", [128, NT, 2], F32))
            ssk = AE(nc.sbuf_tensor("ssk", [128, NT, 2], F32))
            gqv = AE(nc.sbuf_tensor("gqv", [128, NT, 2], F32))
            gkv = AE(nc.sbuf_tensor("gkv", [128, NT, 2], F32))
            gq16 = AE(nc.sbuf_tensor("gq16", [128, NT, 2], BF16))
            gqT = AE(nc.sbuf_tensor("gqT", [2, T], BF16))
            e2map = AE(nc.sbuf_tensor("e2map", [2, 128], BF16))

            ps_al = AE(nc.psum_tensor("ps_al", [128, 3, 1024], F32))
            ps_co = AE(nc.psum_tensor("ps_co", [128, 2, 512], F32))
            ps_proj = ps_co[:, 0, 0:384]
            ps_tr2 = ps_co[0:2, 1, 0:64].bitcast(BF16)

            d = {n: Dep() for n in [
                "zT", "mu", "qkv", "sc8k", "gqbc", "ss", "gv", "gqT", "e2",
                "psproj", "pstr"]}
            d_vbuf = [Dep(), Dep()]
            d_qnT = [Dep(), Dep()]
            d_knT = [Dep(), Dep()]
            d_modT = [Dep(), Dep(), Dep()]
            d_xa = [Dep(), Dep()]
            d_wpair = [Dep(), Dep()]
            d_psal = [Dep(), Dep(), Dep()]
            d_psco = [Dep(), Dep()]

            EM("sync", "qS", lambda e: e.dma_start(out=e2map[:], in_=e2map_in), writes=[d["e2"]])

            def phaseA(b):
                ztile = qkv[:].rearrange("p a c -> p (a c)")[:, 0:1024]
                for i in range(NT):
                    pp = i % 2
                    EM("sync", "qS", lambda e, i=i, pp=pp: e.dma_start(
                        out=xa2[:, pp, :], in_=x_in[b, i * 128:(i + 1) * 128, :]),
                        writes=[d_xa[pp]])
                    EM("vector", "vC", lambda e, i=i, pp=pp: e.tensor_reduce(
                        out=mu_t[:, i:i + 1], in_=xa2[:, pp, :], axis=AX.X, op=ALU.add),
                        reads=[d_xa[pp]], writes=[d["mu"]])
                    EM("scalar", "sC", lambda e, i=i, pp=pp: e.activation(
                        out=gqbc[:, 0:1024], in_=xa2[:, pp, :], func=AF.Square,
                        accum_out=var_t[:, i:i + 1]),
                        reads=[d_xa[pp]], writes=[d["gqbc"], d["mu"]])
                    EM("vector", "vC", lambda e, i=i: e.tensor_scalar(
                        out=mu_t[:, i:i + 1], in0=mu_t[:, i:i + 1], scalar1=1.0 / D,
                        scalar2=None, op0=ALU.mult), reads=[d["mu"]], writes=[d["mu"]])
                    EM("vector", "vC", lambda e, i=i: e.tensor_tensor(
                        out=msc[:, 0:1], in0=mu_t[:, i:i + 1], in1=mu_t[:, i:i + 1], op=ALU.mult),
                        reads=[d["mu"]], writes=[dp["msc"]])
                    EM("vector", "vC", lambda e, i=i: e.scalar_tensor_tensor(
                        out=var_t[:, i:i + 1], in0=var_t[:, i:i + 1], scalar=1.0 / D,
                        in1=msc[:, 0:1], op0=ALU.mult, op1=ALU.subtract),
                        reads=[d["mu"], dp["msc"]], writes=[d["mu"]])
                    EM("vector", "vC", lambda e, i=i: e.tensor_scalar(
                        out=var_t[:, i:i + 1], in0=var_t[:, i:i + 1], scalar1=LN_EPS,
                        scalar2=None, op0=ALU.add), reads=[d["mu"]], writes=[d["mu"]])
                    EM("vector", "vC", lambda e, i=i, pp=pp: e.tensor_scalar(
                        out=ztile, in0=xa2[:, pp, :], scalar1=mu_t[:, i:i + 1],
                        scalar2=None, op0=ALU.subtract),
                        reads=[d_xa[pp], d["mu"]], writes=[d["qkv"]])
                    for kt in range(KT):
                        j = kt % 2
                        trv = ps_co[:, j, 0:64].bitcast(BF16)
                        EM("tensor", "tC", lambda e, kt=kt, trv=trv: e.transpose(
                            trv, ztile[:, kt * 128:(kt + 1) * 128], ident[:]),
                            reads=[d["qkv"], dp["consts"]], writes=[d_psco[j]])
                        EM("vector", "vC", lambda e, kt=kt, i=i, trv=trv: e.tensor_copy(
                            out=zT[:, kt, i * 128:(i + 1) * 128], in_=trv),
                            reads=[d_psco[j]], writes=[d["zT"]])
                EM("scalar", "sC", lambda e: e.activation(out=rstd_t[:], in_=var_t[:], func=AF.Sqrt),
                   reads=[d["mu"]], writes=[d["mu"]])
                EM("vector", "vC", lambda e: e.reciprocal(out=rstd_t[:], in_=rstd_t[:]),
                   reads=[d["mu"]], writes=[d["mu"]])

            def phaseB(b):
                q3 = qkv
                sst = sc8k[:].rearrange("p (n c) -> p n c", c=128)
                for p in range(NPAIR):
                    par = p % 2
                    EM("sync", "qS", lambda e, p=p, par=par: e.dma_start(
                        out=wpair[:, par],
                        in_=wqkv_d[:, :, p, :].rearrange("k p c -> p k c")),
                        reads=[dp["wqkvd"]], writes=[d_wpair[par]])
                    for i in range(NT):
                        for kt in range(KT):
                            EM("tensor", "tC", lambda e, i=i, kt=kt, par=par: e.matmul(
                                ps_proj, zT[:, kt, i * 128:(i + 1) * 128],
                                wpair[:, par, kt, :], start=(kt == 0), stop=(kt == KT - 1)),
                                reads=[d["zT"], d_wpair[par]], writes=[d_psco[0]])
                        EM("vector", "vC", lambda e, i=i: e.tensor_copy(
                            out=qkv[:, i, :], in_=ps_proj),
                            reads=[d_psco[0]], writes=[d["qkv"]])
                    rstd3 = rstd_t[:].unsqueeze(2).to_broadcast([128, NT, 128])
                    mu3 = mu_t[:].unsqueeze(2).to_broadcast([128, NT, 128])
                    for (off, cb) in ((0, cq_b), (128, ck_b)):
                        EM("vector", "vC", lambda e, off=off, rstd3=rstd3: e.tensor_tensor(
                            out=q3[:, :, off:off + 128], in0=q3[:, :, off:off + 128],
                            in1=rstd3, op=ALU.mult), reads=[d["qkv"], d["mu"]], writes=[d["qkv"]])
                        cb3 = cb[:, p * 128:(p + 1) * 128].unsqueeze(1).to_broadcast([128, NT, 128])
                        EM("vector", "vC", lambda e, off=off, cb3=cb3: e.tensor_tensor(
                            out=q3[:, :, off:off + 128], in0=q3[:, :, off:off + 128],
                            in1=cb3, op=ALU.add), reads=[d["qkv"], dp["consts"]], writes=[d["qkv"]])
                    sv3 = sv_b[:, p * 128:(p + 1) * 128].unsqueeze(1).to_broadcast([128, NT, 128])
                    EM("vector", "vC", lambda e, sv3=sv3, mu3=mu3: e.tensor_tensor(
                        out=sst, in0=sv3, in1=mu3, op=ALU.mult),
                        reads=[dp["consts"], d["mu"]], writes=[d["sc8k"]])
                    EM("vector", "vC", lambda e: e.tensor_tensor(
                        out=q3[:, :, 256:384], in0=q3[:, :, 256:384], in1=sst, op=ALU.add),
                        reads=[d["sc8k"], d["qkv"]], writes=[d["qkv"]])
                    for (off, ssx) in ((0, ssq), (128, ssk)):
                        EM("vector", "vC", lambda e, off=off: e.tensor_tensor(
                            out=sst, in0=q3[:, :, off:off + 128], in1=q3[:, :, off:off + 128],
                            op=ALU.mult), reads=[d["qkv"]], writes=[d["sc8k"]])
                        EM("vector", "vC", lambda e, ssx=ssx: e.tensor_reduce(
                            out=ssx[:], in_=sst.rearrange("p n (h x) -> p n h x", x=DH),
                            axis=AX.X, op=ALU.add), reads=[d["sc8k"]], writes=[d["ss"]])
                        EM("scalar", "sC", lambda e, ssx=ssx: e.activation(
                            out=ssx[:].rearrange("p a c -> p (a c)"),
                            in_=ssx[:].rearrange("p a c -> p (a c)"), func=AF.Sqrt),
                            reads=[d["ss"]], writes=[d["ss"]])
                        EM("vector", "vC", lambda e, ssx=ssx: e.tensor_scalar(
                            out=ssx[:].rearrange("p a c -> p (a c)"),
                            in0=ssx[:].rearrange("p a c -> p (a c)"),
                            scalar1=NORM_EPS, scalar2=None, op0=ALU.max),
                            reads=[d["ss"]], writes=[d["ss"]])
                        EM("vector", "vC", lambda e, ssx=ssx: e.reciprocal(
                            out=ssx[:].rearrange("p a c -> p (a c)"),
                            in_=ssx[:].rearrange("p a c -> p (a c)")),
                            reads=[d["ss"]], writes=[d["ss"]])
                        rb = ssx[:].unsqueeze(3).to_broadcast([128, NT, 2, DH])
                        EM("vector", "vC", lambda e, off=off, rb=rb: e.tensor_tensor(
                            out=q3[:, :, off:off + 128].rearrange("p n (h x) -> p n h x", x=DH),
                            in0=q3[:, :, off:off + 128].rearrange("p n (h x) -> p n h x", x=DH),
                            in1=rb, op=ALU.mult), reads=[d["qkv"], d["ss"]], writes=[d["qkv"]])
                    gk4 = gk_b[:].rearrange("p (h x) -> p h x", x=DH).unsqueeze(1).to_broadcast([128, NT, 2, DH])
                    gq4 = gq_b[:].rearrange("p (h x) -> p h x", x=DH).unsqueeze(1).to_broadcast([128, NT, 2, DH])
                    EM("vector", "vC", lambda e, gk4=gk4: e.tensor_tensor(
                        out=sst.rearrange("p n (h x) -> p n h x", x=DH),
                        in0=q3[:, :, 128:256].rearrange("p n (h x) -> p n h x", x=DH),
                        in1=gk4, op=ALU.mult), reads=[d["qkv"], dp["consts"]], writes=[d["sc8k"]])
                    EM("vector", "vC", lambda e: e.tensor_reduce(
                        out=gkv[:], in_=sst.rearrange("p n (h x) -> p n h x", x=DH),
                        axis=AX.X, op=ALU.add), reads=[d["sc8k"]], writes=[d["gv"]])
                    EM("vector", "vC", lambda e, gq4=gq4: e.tensor_tensor(
                        out=sst.rearrange("p n (h x) -> p n h x", x=DH),
                        in0=q3[:, :, 0:128].rearrange("p n (h x) -> p n h x", x=DH),
                        in1=gq4, op=ALU.mult), reads=[d["qkv"], dp["consts"]], writes=[d["sc8k"]])
                    EM("vector", "vC", lambda e: e.tensor_reduce(
                        out=gqv[:], in_=sst.rearrange("p n (h x) -> p n h x", x=DH),
                        axis=AX.X, op=ALU.add), reads=[d["sc8k"]], writes=[d["gv"]])
                    gkb4 = gkv[:].unsqueeze(3).to_broadcast([128, NT, 2, DH])
                    EM("vector", "vC", lambda e, par=par, gkb4=gkb4: e.tensor_tensor(
                        out=vbuf[:, par].rearrange("p n (h x) -> p n h x", x=DH),
                        in0=q3[:, :, 256:384].rearrange("p n (h x) -> p n h x", x=DH),
                        in1=gkb4, op=ALU.mult), reads=[d["qkv"], d["gv"]], writes=[d_vbuf[par]])
                    EM("vector", "vC", lambda e: e.tensor_copy(out=gq16[:], in_=gqv[:]),
                       reads=[d["gv"]], writes=[d["gv"]])
                    trv0 = ps_co[:, 0, 0:64].bitcast(BF16)
                    trv1 = ps_co[:, 1, 0:64].bitcast(BF16)
                    for i in range(NT):
                        EM("tensor", "tC", lambda e, i=i: e.transpose(
                            trv0, qkv[:, i, 0:128], ident[:]),
                            reads=[d["qkv"], dp["consts"]], writes=[d_psco[0]])
                        EM("vector", "vC", lambda e, i=i, par=par: e.tensor_copy(
                            out=qnT[:, par, i * 128:(i + 1) * 128], in_=trv0),
                            reads=[d_psco[0]], writes=[d_qnT[par]])
                        EM("tensor", "tC", lambda e, i=i: e.transpose(
                            trv1, qkv[:, i, 128:256], ident[:]),
                            reads=[d["qkv"], dp["consts"]], writes=[d_psco[1]])
                        EM("vector", "vC", lambda e, i=i, par=par: e.tensor_copy(
                            out=knT[:, par, i * 128:(i + 1) * 128], in_=trv1),
                            reads=[d_psco[1]], writes=[d_knT[par]])
                        EM("tensor", "tC", lambda e, i=i: e.transpose(
                            ps_tr2, gq16[:, i, :], ident[:]),
                            reads=[d["gv"], dp["consts"]], writes=[d_psco[1]])
                        EM("vector", "vC", lambda e, i=i, par=par: e.tensor_copy(
                            out=gqT[0:2, i * 128:(i + 1) * 128], in_=ps_tr2),
                            reads=[d_psco[1]], writes=[d["gqT"]])
                    for ch in range(T // 512):
                        EM("tensor", "tC", lambda e, ch=ch, par=par: e.matmul(
                            ps_co[:, 0, 0:512], e2map[:],
                            gqT[0:2, ch * 512:(ch + 1) * 512], start=True, stop=True),
                            reads=[d["gqT"], d["e2"]], writes=[d_psco[0]])
                        EM("vector", "vC", lambda e, ch=ch: e.tensor_copy(
                            out=gqbc[:, ch * 512:(ch + 1) * 512], in_=ps_co[:, 0, 0:512]),
                            reads=[d_psco[0]], writes=[d["gqbc"]])
                    for tch in range(T // 512):
                        cop = tch % 2
                        for s in range(NT + 2):
                            if s < NT:
                                ap = s % 3
                                EM("tensor", "tC", lambda e, s=s, ap=ap, tch=tch, par=par: e.matmul(
                                    ps_al[:, ap, 0:512], knT[0:64, par, s * 128:(s + 1) * 128],
                                    qnT[0:64, par, tch * 512:(tch + 1) * 512],
                                    start=True, stop=True, tile_position=(0, 0)),
                                    reads=[d_knT[par], d_qnT[par]], writes=[d_psal[ap]])
                                EM("tensor", "tC", lambda e, s=s, ap=ap, tch=tch, par=par: e.matmul(
                                    ps_al[:, ap, 512:1024], knT[64:128, par, s * 128:(s + 1) * 128],
                                    qnT[64:128, par, tch * 512:(tch + 1) * 512],
                                    start=True, stop=True, tile_position=(64, 0)),
                                    reads=[d_knT[par], d_qnT[par]], writes=[d_psal[ap]])
                                EM("scalar", "sC", lambda e, ap=ap: e.activation(
                                    out=modT[:, ap, :], in_=ps_al[:, ap, :],
                                    func=AF.Sigmoid, scale=SHARP, bias=sigb[:]),
                                    reads=[d_psal[ap], dp["consts"]], writes=[d_modT[ap]])
                            if s >= 2:
                                sc = s - 2
                                ac = sc % 3
                                EM("tensor", "tC", lambda e, sc=sc, ac=ac, cop=cop, par=par: e.matmul(
                                    ps_co[0:64, cop, :], vbuf[:, par, sc, 0:64],
                                    modT[:, ac, 0:512], start=(sc == 0), stop=(sc == NT - 1),
                                    tile_position=(0, 0)),
                                    reads=[d_vbuf[par], d_modT[ac]], writes=[d_psco[cop]])
                                EM("tensor", "tC", lambda e, sc=sc, ac=ac, cop=cop, par=par: e.matmul(
                                    ps_co[64:128, cop, :], vbuf[:, par, sc, 64:128],
                                    modT[:, ac, 512:1024], start=(sc == 0), stop=(sc == NT - 1),
                                    tile_position=(0, 64), skip_group_check=True),
                                    reads=[d_vbuf[par], d_modT[ac]], writes=[d_psco[cop]])
                        EM("vector", "vC", lambda e, tch=tch, cop=cop: e.tensor_tensor(
                            out=sc8k[:, tch * 512:(tch + 1) * 512], in0=ps_co[:, cop, :],
                            in1=gqbc[:, tch * 512:(tch + 1) * 512], op=ALU.mult),
                            reads=[d_psco[cop], d["gqbc"]], writes=[d["sc8k"]])
                    EM("scalar", "qA", lambda e, p=p: e.dma_start(
                        out=collT_d[p * 128:(p + 1) * 128, :], in_=sc8k[:]),
                        reads=[d["sc8k"]], writes=[dp["collTd"]])

            def phaseC1(b):
                wo_al = zT[:, :, 0:1024]
                colsb = [qkv[:].rearrange("p a c -> p (a c)")[:, 0:4096]
                         .rearrange("p (k c) -> p k c", c=512),
                         qkv[:].rearrange("p a c -> p (a c)")[:, 4096:8192]
                         .rearrange("p (k c) -> p k c", c=512)]
                r2_al = vbuf[:].rearrange("p a n c -> p (a n c)").bitcast(F32)[:, 0:1024]
                z2t_al = sc8k[:, 0:1024]
                z2T_al = modT[:].rearrange("p a c -> p (a c)")[:, 0:1024] \
                    .rearrange("p (k c) -> p k c", c=128)
                xsq_al = gqbc[:, 0:1024]
                d_r2 = d_vbuf
                for kt in range(KT):
                    EM("gpsimd", "qG", lambda e, kt=kt: e.dma_start(
                        out=wo_al[:, kt, :], in_=wo_in[kt * 128:(kt + 1) * 128, :]),
                        writes=[d["zT"]])
                for ci in range(8):
                    cb = ci % 2
                    for dt in range(KT):
                        EM("sync", "qS", lambda e, ci=ci, cb=cb, dt=dt: e.dma_start(
                            out=colsb[cb][:, dt, :],
                            in_=collT_d[dt * 128:(dt + 1) * 128, ci * 512:(ci + 1) * 512]),
                            reads=[dp["collTd"]], writes=[d["qkv"]])
                    for tsub in range(4):
                        i = ci * 4 + tsub
                        gi = b * NT + i
                        pp = i % 3
                        xp = i % 2
                        EM("sync", "qS", lambda e, i=i, xp=xp: e.dma_start(
                            out=xa2[:, xp, :], in_=x_in[b, i * 128:(i + 1) * 128, :]),
                            writes=[d_xa[xp]])
                        for dch in range(2):
                            for dt in range(KT):
                                EM("tensor", "tC", lambda e, cb=cb, dt=dt, tsub=tsub, dch=dch, pp=pp: e.matmul(
                                    ps_al[:, pp, dch * 512:(dch + 1) * 512],
                                    colsb[cb][:, dt, tsub * 128:(tsub + 1) * 128],
                                    wo_al[:, dt, dch * 512:(dch + 1) * 512],
                                    start=(dt == 0), stop=(dt == KT - 1)),
                                    reads=[d["qkv"], d["zT"]], writes=[d_psal[pp]])
                        EM("vector", "vC", lambda e, pp=pp: e.tensor_tensor(
                            out=r2_al, in0=ps_al[:, pp, :], in1=bo_b[:], op=ALU.add),
                            reads=[d_psal[pp], dp["consts"]], writes=d_r2)
                        EM("vector", "vC", lambda e, xp=xp: e.tensor_tensor(
                            out=r2_al, in0=r2_al, in1=xa2[:, xp, :], op=ALU.add),
                            reads=[d_xa[xp]] + d_r2, writes=d_r2)
                        EM("vector", "vC", lambda e, gi=gi: e.tensor_reduce(
                            out=msc[:, 1:2], in_=r2_al, axis=AX.X, op=ALU.add),
                            reads=d_r2, writes=[dp["msc"]])
                        EM("vector", "vC", lambda e: e.tensor_scalar(
                            out=msc[:, 1:2], in0=msc[:, 1:2], scalar1=1.0 / D,
                            scalar2=None, op0=ALU.mult), reads=[dp["msc"]], writes=[dp["msc"]])
                        EM("scalar", "sC", lambda e, gi=gi: e.activation(
                            out=xsq_al, in_=r2_al, func=AF.Square,
                            accum_out=var2_t[:, gi:gi + 1]),
                            reads=d_r2, writes=[d["gqbc"], dp["var2"]])
                        EM("vector", "vC", lambda e: e.tensor_tensor(
                            out=msc[:, 2:3], in0=msc[:, 1:2], in1=msc[:, 1:2], op=ALU.mult),
                            reads=[dp["msc"]], writes=[dp["msc"]])
                        EM("vector", "vC", lambda e, gi=gi: e.scalar_tensor_tensor(
                            out=var2_t[:, gi:gi + 1], in0=var2_t[:, gi:gi + 1], scalar=1.0 / D,
                            in1=msc[:, 2:3], op0=ALU.mult, op1=ALU.subtract),
                            reads=[dp["msc"], dp["var2"]], writes=[dp["var2"]])
                        EM("vector", "vC", lambda e, gi=gi: e.tensor_scalar(
                            out=var2_t[:, gi:gi + 1], in0=var2_t[:, gi:gi + 1],
                            scalar1=LN_EPS, scalar2=None, op0=ALU.add),
                            reads=[dp["var2"]], writes=[dp["var2"]])
                        EM("vector", "vC", lambda e: e.tensor_scalar(
                            out=z2t_al, in0=r2_al, scalar1=msc[:, 1:2],
                            scalar2=None, op0=ALU.subtract),
                            reads=d_r2 + [dp["msc"]], writes=[d["sc8k"]])
                        EM("scalar", "qA", lambda e, gi=gi: e.dma_start(
                            out=z2_d[gi * 128:(gi + 1) * 128, :], in_=z2t_al),
                            reads=[d["sc8k"]], writes=[dp["z2d"]])
                        for kt in range(KT):
                            j = kt % 2
                            trv = ps_co[:, j, 0:64].bitcast(BF16)
                            EM("tensor", "tC", lambda e, kt=kt, trv=trv: e.transpose(
                                trv, z2t_al[:, kt * 128:(kt + 1) * 128], ident[:]),
                                reads=[d["sc8k"], dp["consts"]], writes=[d_psco[j]])
                            EM("vector", "vC", lambda e, kt=kt, trv=trv: e.tensor_copy(
                                out=z2T_al[:, kt, :], in_=trv),
                                reads=[d_psco[j]], writes=d_modT)
                        for kt in range(KT):
                            EM("tensor", "tC", lambda e, kt=kt: e.matmul(
                                ps_co[:, 0, 0:E], z2T_al[:, kt, :], wr_s[:, kt, :],
                                start=(kt == 0), stop=(kt == KT - 1)),
                                reads=d_modT + [dp["wr"]], writes=[d_psco[0]])
                        EM("vector", "vC", lambda e, gi=gi: e.tensor_copy(
                            out=lgz_t[:, gi, :], in_=ps_co[:, 0, 0:E]),
                            reads=[d_psco[0]], writes=[dp["lgz"]])

            for b in range(BPC):
                phaseA(b)
                phaseB(b)
                phaseC1(b)

            EM("scalar", "sC", lambda e: e.activation(out=rstd2_t[:], in_=var2_t[:], func=AF.Sqrt),
               reads=[dp["var2"]], writes=[dp["rstd2"]])
            EM("vector", "vC", lambda e: e.reciprocal(out=rstd2_t[:], in_=rstd2_t[:]),
               reads=[dp["rstd2"]], writes=[dp["rstd2"]])

            z2c_al = sc8k[:, 0:1024]
            x2c_al = sc8k[:, 1024:2048]
            for gi in range(NTL):
                EM("sync", "qS", lambda e, gi=gi: e.dma_start(
                    out=z2c_al, in_=z2_d[gi * 128:(gi + 1) * 128, :]),
                    reads=[dp["z2d"]], writes=[d["sc8k"]])
                EM("vector", "vC", lambda e, gi=gi: e.tensor_scalar(
                    out=x2c_al, in0=z2c_al, scalar1=rstd2_t[:, gi:gi + 1],
                    scalar2=None, op0=ALU.mult),
                    reads=[d["sc8k"], dp["rstd2"]], writes=[d["sc8k"]])
                EM("vector", "vC", lambda e, gi=gi: e.tensor_scalar(
                    out=prb[:], in0=lgz_t[:, gi, :], scalar1=rstd2_t[:, gi:gi + 1],
                    scalar2=None, op0=ALU.mult),
                    reads=[dp["lgz"], dp["rstd2"]], writes=[dp["prb"]])
                EM("vector", "vC", lambda e: e.tensor_reduce(
                    out=msc[:, 3:4], in_=prb[:], axis=AX.X, op=ALU.max, negate=True),
                    reads=[dp["prb"]], writes=[dp["msc"]])
                EM("scalar", "sC", lambda e: e.activation(
                    out=prb[:], in_=prb[:], func=AF.Exp, bias=msc[:, 3:4]),
                    reads=[dp["prb"], dp["msc"]], writes=[dp["prb"]])
                EM("vector", "vC", lambda e: e.tensor_reduce(
                    out=msc[:, 4:5], in_=prb[:], axis=AX.X, op=ALU.add),
                    reads=[dp["prb"]], writes=[dp["msc"]])
                EM("vector", "vC", lambda e: e.reciprocal(out=msc[:, 5:6], in_=msc[:, 4:5]),
                   reads=[dp["msc"]], writes=[dp["msc"]])
                EM("vector", "vC", lambda e: e.tensor_scalar(
                    out=prb[:], in0=prb[:], scalar1=msc[:, 5:6], scalar2=None, op0=ALU.mult),
                    reads=[dp["prb"], dp["msc"]], writes=[dp["prb"]])
                EM("vector", "vC", lambda e: e.max(out=top8[:], in_=prb[:]),
                   reads=[dp["prb"]], writes=[dp["top"]])
                EM("vector", "vC", lambda e: e.max_index(out=idx8[:], in_max=top8[:], in_values=prb[:]),
                   reads=[dp["top"], dp["prb"]], writes=[dp["top"]])
                EM("vector", "vC", lambda e, gi=gi: e.tensor_copy(
                    out=ws_t[:, gi, :], in_=top8[:, 0:2]), reads=[dp["top"]], writes=[dp["route"]])
                EM("vector", "vC", lambda e, gi=gi: e.tensor_copy(
                    out=es_t[:, gi, :], in_=idx8[:, 0:2]), reads=[dp["top"]], writes=[dp["route"]])
                EM("vector", "vC", lambda e, gi=gi: e.tensor_scalar(
                    out=m1x[:], in0=io8_b[:], scalar1=es_t[:, gi, 0:1], scalar2=None,
                    op0=ALU.is_equal), reads=[dp["consts"], dp["route"]], writes=[dp["m12"]])
                EM("vector", "vC", lambda e, gi=gi: e.tensor_scalar(
                    out=m2x[:], in0=io8_b[:], scalar1=es_t[:, gi, 1:2], scalar2=None,
                    op0=ALU.is_equal), reads=[dp["consts"], dp["route"]], writes=[dp["m12"]])
                EM("vector", "vC", lambda e: e.tensor_tensor(
                    out=cab[:], in0=m1x[:], in1=m2x[:], op=ALU.add),
                    reads=[dp["m12"]], writes=[dp["cab"]])
                EM("vector", "vC", lambda e: e.tensor_copy(out=mm8[:], in_=cab[:]),
                   reads=[dp["cab"]], writes=[dp["mm8"]])
                EM("tensor", "tC", lambda e: e.matmul(
                    ps_co[:, 0, 0:E], utri[:], mm8[:], start=True, stop=True),
                    reads=[dp["consts"], dp["mm8"]], writes=[d_psco[0]])
                EM("tensor", "tC", lambda e: e.matmul(
                    ps_co[:, 1, 0:E], ones_f[:], run_row[:], start=True, stop=True),
                    reads=[dp["consts"], dp["run"]], writes=[d_psco[1]])
                EM("vector", "vC", lambda e: e.tensor_copy(out=runb_s[:], in_=ps_co[:, 1, 0:E]),
                   reads=[d_psco[1]], writes=[dp["runb"]])
                EM("vector", "vC", lambda e: e.tensor_tensor(
                    out=cab[:], in0=ps_co[:, 0, 0:E], in1=runb_s[:], op=ALU.add),
                    reads=[d_psco[0], dp["runb"], dp["cab"]], writes=[dp["cab"]])
                for j in range(2):
                    EM("vector", "vC", lambda e, gi=gi, j=j: e.scalar_tensor_tensor(
                        out=prb[:], in0=cab[:], scalar=1.0, in1=(m1x[:] if j == 0 else m2x[:]),
                        op0=ALU.mult, op1=ALU.mult, accum_out=rs_t[:, gi, j:j + 1]),
                        reads=[dp["cab"], dp["m12"]], writes=[dp["prb"], dp["route"]])
                EM("tensor", "tC", lambda e: e.matmul(
                    ps_co[0:1, 1, 0:E], ones_bcol[:], mm8[:], start=True, stop=True),
                    reads=[dp["consts"], dp["mm8"]], writes=[d_psco[1]])
                EM("vector", "vC", lambda e: e.tensor_tensor(
                    out=run_row[:], in0=run_row[:], in1=ps_co[0:1, 1, 0:E], op=ALU.add),
                    reads=[d_psco[1], dp["run"]], writes=[dp["run"]])
                for j in range(2):
                    EM("vector", "vC", lambda e, gi=gi, j=j: e.tensor_scalar(
                        out=msc[:, 8 + j:9 + j], in0=rs_t[:, gi, j:j + 1],
                        scalar1=float(PEXP - 1), scalar2=None, op0=ALU.min),
                        reads=[dp["route"]], writes=[dp["msc"]])
                    EM("vector", "vC", lambda e, gi=gi, j=j: e.scalar_tensor_tensor(
                        out=msc[:, 10 + j:11 + j], in0=es_t[:, gi, j:j + 1],
                        scalar=float(PEXP), in1=msc[:, 8 + j:9 + j],
                        op0=ALU.mult, op1=ALU.add),
                        reads=[dp["route"], dp["msc"]], writes=[dp["msc"]])
                    EM("vector", "vC", lambda e, j=j: e.tensor_copy(
                        out=idxu[:, j:j + 1], in_=msc[:, 10 + j:11 + j]),
                        reads=[dp["msc"]], writes=[dp["idxu"]])
                    EM("gpsimd", "qG", lambda e, j=j: e.indirect_dma_start(
                        out=xdisp_d, out_offset=IndirectOffsetOnAxis(idxu[:, j:j + 1], 0),
                        in_=x2c_al, in_offset=None),
                        reads=[dp["idxu"], d["sc8k"]], writes=[dp["xdispd"]])

            EM("tensor", "tC", lambda e: e.matmul(
                ps_co[:, 0, 0:E], rowmask[:], run_row[:], start=True, stop=True),
                reads=[dp["consts"], dp["run"]], writes=[d_psco[0]])
            EM("vector", "vC", lambda e: e.tensor_copy(out=cnt128[:, 0:E], in_=ps_co[:, 0, 0:E]),
               reads=[d_psco[0]], writes=[dp["cnt"]])
            EM("gpsimd", "qG", lambda e: e.dma_start(out=ccs_d, in_=cnt128[:]),
               reads=[dp["cnt"]], writes=[dp["ccsd"]])
            EM("gpsimd", "cc", lambda e: e.collective_compute(
                "AllReduce", ALU.add, replica_groups=[list(range(NC))],
                ins=[ccs_d.opt()], outs=[ccd_d.opt()]),
                reads=[dp["ccsd"]], writes=[dp["ccdd"]])
            EM("gpsimd", "qG", lambda e: e.dma_start(out=cnt128[:], in_=ccd_d),
               reads=[dp["ccdd"], dp["cnt"]], writes=[dp["cnt"]])
            EM("tensor", "tC", lambda e: e.matmul(
                ps_co[0:1, 0, 0:E], selcol[:], cnt128[:, 0:E], start=True, stop=True),
                reads=[dp["consts"], dp["cnt"]], writes=[d_psco[0]])
            EM("vector", "vC", lambda e: e.tensor_copy(out=cpx_b[0:1, :], in_=ps_co[0:1, 0, 0:E]),
               reads=[d_psco[0]], writes=[dp["cpx"]])
            EM("tensor", "tC", lambda e: e.matmul(
                ps_co[:, 1, 0:E], ones_f[:], cpx_b[0:1, :], start=True, stop=True),
                reads=[dp["consts"], dp["cpx"]], writes=[d_psco[1]])
            EM("vector", "vC", lambda e: e.tensor_copy(out=cpx_b[:], in_=ps_co[:, 1, 0:E]),
               reads=[d_psco[1]], writes=[dp["cpx"]])

        bld.barrier()
        with ExitStack() as ffn:
            FE = ffn.enter_context
            xeT = FE(nc.sbuf_tensor("xeT", [128, 2, KT, RH], BF16))
            hT = FE(nc.sbuf_tensor("hT", [128, 8, RH], BF16))
            w1s = FE(nc.sbuf_tensor("w1s", [128, 2, KT, 1024], BF16))
            w2s = FE(nc.sbuf_tensor("w2s", [128, 2, 8, 1024], BF16))
            wst = FE(nc.sbuf_tensor("wst", [128, 4, 1024], F32))
            yac = FE(nc.sbuf_tensor("yac", [128, NRH, 1024], F32))
            ps_h = FE(nc.psum_tensor("ps_h", [128, 2, 512], F32))
            ps_y = FE(nc.psum_tensor("ps_y", [128, 3, 1024], F32))

            d_xeT = [Dep(), Dep()]
            d_hT = Dep()
            d_w1 = [Dep(), Dep()]
            d_w2 = [Dep(), Dep()]
            d_wst = Dep()
            d_yac = Dep()
            d_psh = [Dep(), Dep()]
            d_psy = [Dep(), Dep(), Dep()]

            for e_i in range(E):
                for rh in range(2):
                    base = e_i * PEXP + rh * RH
                    xp = (e_i * 2 + rh) % 2
                    for kt in range(KT):
                        EM("scalar", "qA", lambda e, base=base, kt=kt, xp=xp: e.dma_start(
                            out=xeT[:, xp, kt, :],
                            in_=xdisp_d[base:base + RH, kt * 128:(kt + 1) * 128],
                            transpose=True),
                            reads=[dp["xdispd"]], writes=[d_xeT[xp]])
                    def emit_wload(we, wfb):
                        wb = wfb % 2
                        for half in range(2):
                            for k4 in range(4):
                                kt = half * 4 + k4
                                EM("sync", "qS", lambda e, we=we, wfb=wfb, kt=kt, k4=k4: e.dma_start(
                                    out=wst[:, k4, :],
                                    in_=w1_in[we, kt * 128:(kt + 1) * 128, wfb * 1024:(wfb + 1) * 1024]),
                                    writes=[d_wst])
                            EM("vector", "vC", lambda e, wb=wb, half=half: e.tensor_copy(
                                out=w1s[:, wb, half * 4:half * 4 + 4, :], in_=wst[:]),
                                reads=[d_wst], writes=[d_w1[wb]])
                        for half in range(2):
                            for k4 in range(4):
                                ft = half * 4 + k4
                                EM("scalar", "qA", lambda e, we=we, wfb=wfb, ft=ft, k4=k4: e.dma_start(
                                    out=wst[:, k4, :],
                                    in_=w2_in[we, wfb * 1024 + ft * 128:wfb * 1024 + (ft + 1) * 128, :]),
                                    writes=[d_wst])
                            EM("vector", "vC", lambda e, wb=wb, half=half: e.tensor_copy(
                                out=w2s[:, wb, half * 4:half * 4 + 4, :], in_=wst[:]),
                                reads=[d_wst], writes=[d_w2[wb]])

                    for fb in range(4):
                        wb = fb % 2
                        if e_i == 0 and rh == 0 and fb == 0:
                            emit_wload(0, 0)
                        if fb < 3:
                            emit_wload(e_i, fb + 1)
                        elif rh == 0:
                            emit_wload(e_i, 0)
                        elif e_i < E - 1:
                            emit_wload(e_i + 1, 0)
                        for ft in range(8):
                            for rc in range(3):
                                w = 512 if rc < 2 else 128
                                hp = (ft * 3 + rc) % 2
                                for kt in range(KT):
                                    EM("tensor", "tC", lambda e, ft=ft, rc=rc, kt=kt, wb=wb, hp=hp, w=w, xp=xp: e.matmul(
                                        ps_h[:, hp, 0:w], w1s[:, wb, kt, ft * 128:(ft + 1) * 128],
                                        xeT[:, xp, kt, rc * 512:rc * 512 + w],
                                        start=(kt == 0), stop=(kt == KT - 1)),
                                        reads=[d_w1[wb], d_xeT[xp]], writes=[d_psh[hp]])
                                EM("scalar", "sC", lambda e, ft=ft, rc=rc, hp=hp, w=w: e.activation(
                                    out=hT[:, ft, rc * 512:rc * 512 + w], in_=ps_h[:, hp, 0:w],
                                    func=AF.Gelu_apprx_tanh),
                                    reads=[d_psh[hp]], writes=[d_hT])
                        for rt in range(NRH):
                            yp = rt % 3
                            for dch in range(2):
                                for ft in range(8):
                                    EM("tensor", "tC", lambda e, rt=rt, dch=dch, ft=ft, wb=wb, yp=yp: e.matmul(
                                        ps_y[:, yp, dch * 512:(dch + 1) * 512],
                                        hT[:, ft, rt * 128:(rt + 1) * 128],
                                        w2s[:, wb, ft, dch * 512:(dch + 1) * 512],
                                        start=(ft == 0), stop=(ft == 7)),
                                        reads=[d_hT, d_w2[wb]], writes=[d_psy[yp]])
                            if fb == 0:
                                EM("vector", "vC", lambda e, rt=rt, yp=yp: e.tensor_copy(
                                    out=yac[:, rt, :], in_=ps_y[:, yp, :]),
                                    reads=[d_psy[yp]], writes=[d_yac])
                            else:
                                EM("vector", "vC", lambda e, rt=rt, yp=yp: e.tensor_tensor(
                                    out=yac[:, rt, :], in0=yac[:, rt, :], in1=ps_y[:, yp, :],
                                    op=ALU.add),
                                    reads=[d_psy[yp], d_yac], writes=[d_yac])
                    for rt in range(NRH):
                        EM("sync", "qS", lambda e, base=base, rt=rt: e.dma_start(
                            out=ybuf_d[base + rt * 128:base + (rt + 1) * 128, :],
                            in_=yac[:, rt, :]),
                            reads=[d_yac], writes=[dp["ybufd"]])

        bld.barrier()
        with ExitStack() as comb:
            CE = comb.enter_context
            z3 = CE(nc.sbuf_tensor("z3", [128, NTL, D], BF16))
            g1 = CE(nc.sbuf_tensor("g1", [128, D], F32))
            g2 = CE(nc.sbuf_tensor("g2", [128, D], F32))
            z2c = CE(nc.sbuf_tensor("z2c", [128, D], BF16))
            x2c = CE(nc.sbuf_tensor("x2c", [128, D], BF16))
            acc = CE(nc.sbuf_tensor("acc", [128, D], F32))
            xsq3 = CE(nc.sbuf_tensor("xsq3", [128, D], BF16))
            outf = CE(nc.sbuf_tensor("outf", [128, 2, D], F32))

            d_z3 = Dep()
            d_g = Dep()
            d_z2c = Dep()
            d_x2c = Dep()
            d_acc = Dep()
            d_xsq3 = Dep()
            d_outf = [Dep(), Dep()]

            for gi in range(NTL):
                EM("sync", "qS", lambda e, gi=gi: e.dma_start(
                    out=z2c[:], in_=z2_d[gi * 128:(gi + 1) * 128, :]),
                    reads=[dp["z2d"]], writes=[d_z2c])
                EM("vector", "vC", lambda e, gi=gi: e.tensor_scalar(
                    out=x2c[:], in0=z2c[:], scalar1=rstd2_t[:, gi:gi + 1],
                    scalar2=None, op0=ALU.mult),
                    reads=[d_z2c, dp["rstd2"]], writes=[d_x2c])
                for j in range(2):
                    mj = m1x if j == 0 else m2x
                    EM("vector", "vC", lambda e, gi=gi, j=j, mj=mj: e.tensor_scalar(
                        out=mj[:], in0=io8_b[:], scalar1=es_t[:, gi, j:j + 1], scalar2=None,
                        op0=ALU.is_equal), reads=[dp["consts"], dp["route"]], writes=[dp["m12"]])
                    EM("vector", "vC", lambda e, j=j, mj=mj: e.scalar_tensor_tensor(
                        out=prb[:], in0=cpx_b[:], scalar=1.0, in1=mj[:],
                        op0=ALU.mult, op1=ALU.mult, accum_out=msc[:, j:j + 1]),
                        reads=[dp["cpx"], dp["m12"]], writes=[dp["prb"], dp["msc"]])
                    EM("vector", "vC", lambda e, gi=gi, j=j: e.tensor_tensor(
                        out=msc[:, 2 + j:3 + j], in0=rs_t[:, gi, j:j + 1], in1=msc[:, j:j + 1],
                        op=ALU.add), reads=[dp["route"], dp["msc"]], writes=[dp["msc"]])
                    EM("vector", "vC", lambda e, j=j: e.tensor_scalar(
                        out=msc[:, 4 + j:5 + j], in0=msc[:, 2 + j:3 + j],
                        scalar1=CAP - 0.5, scalar2=None, op0=ALU.is_lt),
                        reads=[dp["msc"]], writes=[dp["msc"]])
                    EM("vector", "vC", lambda e, gi=gi, j=j: e.tensor_tensor(
                        out=msc[:, 6 + j:7 + j], in0=ws_t[:, gi, j:j + 1],
                        in1=msc[:, 4 + j:5 + j], op=ALU.mult),
                        reads=[dp["route"], dp["msc"]], writes=[dp["msc"]])
                    EM("vector", "vC", lambda e, gi=gi, j=j: e.tensor_scalar(
                        out=msc[:, 8 + j:9 + j], in0=rs_t[:, gi, j:j + 1],
                        scalar1=float(PEXP - 1), scalar2=None, op0=ALU.min),
                        reads=[dp["route"]], writes=[dp["msc"]])
                    EM("vector", "vC", lambda e, gi=gi, j=j: e.scalar_tensor_tensor(
                        out=msc[:, 10 + j:11 + j], in0=es_t[:, gi, j:j + 1],
                        scalar=float(PEXP), in1=msc[:, 8 + j:9 + j],
                        op0=ALU.mult, op1=ALU.add),
                        reads=[dp["route"], dp["msc"]], writes=[dp["msc"]])
                    EM("vector", "vC", lambda e, j=j: e.tensor_copy(
                        out=idxu[:, j:j + 1], in_=msc[:, 10 + j:11 + j]),
                        reads=[dp["msc"]], writes=[dp["idxu"]])
                    EM("gpsimd", "qG", lambda e, j=j: e.indirect_dma_start(
                        out=(g1 if j == 0 else g2)[:], out_offset=None,
                        in_=ybuf_d, in_offset=IndirectOffsetOnAxis(idxu[:, j:j + 1], 0)),
                        reads=[dp["idxu"], dp["ybufd"]], writes=[d_g])
                EM("vector", "vC", lambda e: e.scalar_tensor_tensor(
                    out=acc[:], in0=g1[:], scalar=msc[:, 6:7], in1=x2c[:],
                    op0=ALU.mult, op1=ALU.add),
                    reads=[d_g, dp["msc"], d_x2c], writes=[d_acc])
                EM("vector", "vC", lambda e: e.scalar_tensor_tensor(
                    out=acc[:], in0=g2[:], scalar=msc[:, 7:8], in1=acc[:],
                    op0=ALU.mult, op1=ALU.add),
                    reads=[d_g, dp["msc"], d_acc], writes=[d_acc])
                EM("vector", "vC", lambda e: e.tensor_tensor(
                    out=acc[:], in0=acc[:], in1=fb_b[:], op=ALU.add),
                    reads=[dp["consts"], d_acc], writes=[d_acc])
                EM("vector", "vC", lambda e: e.tensor_reduce(
                    out=msc[:, 12:13], in_=acc[:], axis=AX.X, op=ALU.add),
                    reads=[d_acc], writes=[dp["msc"]])
                EM("vector", "vC", lambda e: e.tensor_scalar(
                    out=msc[:, 12:13], in0=msc[:, 12:13], scalar1=1.0 / D,
                    scalar2=None, op0=ALU.mult), reads=[dp["msc"]], writes=[dp["msc"]])
                EM("scalar", "sC", lambda e, gi=gi: e.activation(
                    out=xsq3[:], in_=acc[:], func=AF.Square,
                    accum_out=var3_t[:, gi:gi + 1]),
                    reads=[d_acc], writes=[d_xsq3, dp["var3"]])
                EM("vector", "vC", lambda e: e.tensor_tensor(
                    out=msc[:, 13:14], in0=msc[:, 12:13], in1=msc[:, 12:13], op=ALU.mult),
                    reads=[dp["msc"]], writes=[dp["msc"]])
                EM("vector", "vC", lambda e, gi=gi: e.scalar_tensor_tensor(
                    out=var3_t[:, gi:gi + 1], in0=var3_t[:, gi:gi + 1], scalar=1.0 / D,
                    in1=msc[:, 13:14], op0=ALU.mult, op1=ALU.subtract),
                    reads=[dp["msc"], dp["var3"]], writes=[dp["var3"]])
                EM("vector", "vC", lambda e, gi=gi: e.tensor_scalar(
                    out=var3_t[:, gi:gi + 1], in0=var3_t[:, gi:gi + 1],
                    scalar1=LN_EPS, scalar2=None, op0=ALU.add),
                    reads=[dp["var3"]], writes=[dp["var3"]])
                EM("vector", "vC", lambda e, gi=gi: e.tensor_scalar(
                    out=z3[:, gi, :], in0=acc[:], scalar1=msc[:, 12:13],
                    scalar2=None, op0=ALU.subtract),
                    reads=[d_acc, dp["msc"]], writes=[d_z3])

            EM("scalar", "sC", lambda e: e.activation(out=rstd3_t[:], in_=var3_t[:], func=AF.Sqrt),
               reads=[dp["var3"]], writes=[dp["rstd3"]])
            EM("vector", "vC", lambda e: e.reciprocal(out=rstd3_t[:], in_=rstd3_t[:]),
               reads=[dp["rstd3"]], writes=[dp["rstd3"]])

            for gi in range(NTL):
                b, i = gi // NT, gi % NT
                op_ = gi % 2
                EM("vector", "vC", lambda e, gi=gi, op_=op_: e.tensor_scalar(
                    out=outf[:, op_, :], in0=z3[:, gi, :], scalar1=rstd3_t[:, gi:gi + 1],
                    scalar2=None, op0=ALU.mult),
                    reads=[d_z3, dp["rstd3"]], writes=[d_outf[op_]])
                EM("sync", "qS", lambda e, b=b, i=i, op_=op_: e.dma_start(
                    out=out_t[b, i * 128:(i + 1) * 128, :], in_=outf[:, op_, :]),
                    reads=[d_outf[op_]], writes=[dp["outd"]])

            snap = [(c.sem, c.val) for c in ctrs.values()]
            for pl in pools.values():
                snap += [(c.sem, c.val) for c in pl.subs]
            for ename in ("sync", "gpsimd", "vector", "scalar", "tensor"):
                self_q = bld.queues.setdefault(ename, [])
            bld.flush()
            for ename in ("sync", "gpsimd"):
                def fini(eng, snap=snap):
                    for (s, v) in snap:
                        eng.wait_ge(s, v)
                getattr(block, ename)(fini)

    return nc


def kernel(L0_attn_x, L0_attn_Wq, L0_attn_Wk, L0_attn_Wv, L0_attn_gq, L0_attn_gk,
           L0_attn_Wo, L0_attn_bo, L0_attn_ln_w, L0_attn_ln_b,
           L0_ffn_w_router, L0_ffn_w1, L0_ffn_w2, L0_ffn_bias):
    x = np.ascontiguousarray(np.asarray(L0_attn_x, dtype=np.float32))
    ident = np.eye(128, dtype=ml_dtypes.bfloat16)
    utri = np.triu(np.ones((128, 128), np.float32), 1).astype(ml_dtypes.bfloat16)
    iota8 = np.arange(E, dtype=np.float32).reshape(1, E)

    shared = {
        "wq_in": np.ascontiguousarray(np.asarray(L0_attn_Wq, np.float32)),
        "wk_in": np.ascontiguousarray(np.asarray(L0_attn_Wk, np.float32)),
        "wv_in": np.ascontiguousarray(np.asarray(L0_attn_Wv, np.float32)),
        "wo_in": np.ascontiguousarray(np.asarray(L0_attn_Wo, np.float32)),
        "gq_in": np.asarray(L0_attn_gq, np.float32).reshape(1, DH),
        "gk_in": np.asarray(L0_attn_gk, np.float32).reshape(1, DH),
        "bo_in": np.asarray(L0_attn_bo, np.float32).reshape(1, D),
        "lnw_in": np.asarray(L0_attn_ln_w, np.float32).reshape(D, 1),
        "lnb_in": np.asarray(L0_attn_ln_b, np.float32).reshape(D, 1),
        "wr_in": np.ascontiguousarray(np.asarray(L0_ffn_w_router, np.float32)),
        "w1_in": np.ascontiguousarray(np.asarray(L0_ffn_w1, np.float32)),
        "w2_in": np.ascontiguousarray(np.asarray(L0_ffn_w2, np.float32)),
        "fb_in": np.asarray(L0_ffn_bias, np.float32).reshape(1, D),
        "ident_in": ident,
        "utri_in": utri,
        "iota8_in": iota8,
        "e2map_in": np.kron(np.eye(2, dtype=np.float32), np.ones((1, 64), np.float32)).astype(ml_dtypes.bfloat16),
    }

    nc = build_nc()
    in_maps = []
    for c in range(NC):
        m = dict(shared)
        m["x_in"] = np.ascontiguousarray(x[c * BPC:(c + 1) * BPC])
        rm = np.zeros((1, 128), np.float32)
        rm[0, c] = 1.0
        m["rowmask_in"] = rm
        sc = np.zeros((128, 1), np.float32)
        sc[:c, 0] = 1.0
        m["selcol_in"] = sc
        in_maps.append(m)

    res = run_bass_kernel_spmd(nc, in_maps, core_ids=list(range(NC)))
    out = np.concatenate([np.asarray(r["out_t"]) for r in res.results], axis=0)
    return out.astype(np.float32)


# version 07
# speedup vs baseline: 1.0701x; 1.0701x over previous
import numpy as np
import ml_dtypes
from contextlib import ExitStack

import concourse.bass as bass
import concourse.mybir as mybir
from concourse.bass import IndirectOffsetOnAxis
from concourse.bass_utils import run_bass_kernel_spmd

F32 = mybir.dt.float32
BF16 = mybir.dt.bfloat16
U32 = mybir.dt.uint32
AF = mybir.ActivationFunctionType
ALU = mybir.AluOpType
AX = mybir.AxisListType

NC = 8
B, T, D, H, DH = 16, 4096, 1024, 16, 64
NPAIR = 8
BPC = B // NC
NT = T // 128
NLOC = BPC * T
NTL = NLOC // 128
E, FF = 8, 4096
CAP = 2 * B * T // E
PEXP = 2304
RH = PEXP // 2
NRH = RH // 128
THRESH, SHARP = 0.29514, 15.0
LN_EPS, NORM_EPS = 1e-5, 1e-12
KT = D // 128


class Ctr:
    def __init__(self, sem, step=1):
        self.sem, self.val, self.step = sem, 0, step

    def inc(self, instr, n=None):
        n = self.step if n is None else n
        instr.then_inc(self.sem, n)
        self.val += n
        return instr


class Dep:
    __slots__ = ("w", "r")
    def __init__(self):
        self.w, self.r = None, []


class DmaPool:
    def __init__(self, subs):
        self.subs = subs
        self.i = 0

    def next(self):
        c = self.subs[self.i % len(self.subs)]
        self.i += 1
        return c


class Builder:
    def __init__(self, block, ctrs, pools):
        self.block = block
        self.ctrs = ctrs
        self.pools = pools
        self.seen = {}
        self.queues = {}
        self.pending = {}

    def barrier(self):
        snap = [(c, c.val) for c in self.ctrs.values()]
        for pl in self.pools.values():
            snap += [(c, c.val) for c in pl.subs]
        for ename in ("sync", "scalar", "vector", "tensor", "gpsimd"):
            self.pending[ename] = list(snap)

    def _want(self, ename, ctr, val, waits, own):
        if val <= 0:
            return
        if ename == "tensor" and own is not None and ctr is own:
            return
        key = id(ctr.sem)
        seen = self.seen.setdefault(ename, {})
        if seen.get(key, -1) >= val:
            return
        seen[key] = val
        waits.append((ctr.sem, val))

    def emit(self, ename, cname, fn, reads=(), writes=()):
        if cname in self.pools:
            ctr = self.pools[cname].next()
        else:
            ctr = self.ctrs[cname]
        waits = []
        for (c, v) in self.pending.pop(ename, ()):
            self._want(ename, c, v, waits, None)
        if ctr.step == 16:
            self._want(ename, ctr, ctr.val, waits, None)
        for d in reads:
            if d.w is not None:
                self._want(ename, d.w[0], d.w[1], waits, ctr)
        for d in writes:
            if d.w is not None:
                self._want(ename, d.w[0], d.w[1], waits, ctr)
            for (c, v) in d.r:
                self._want(ename, c, v, waits, ctr)
        n = ctr.step
        self.queues.setdefault(ename, []).append((waits, fn, ctr.sem, n))
        ctr.val += n
        for d in reads:
            d.r.append((ctr, ctr.val))
        for d in writes:
            d.w = (ctr, ctr.val)
            d.r = []

    def flush(self):
        for ename, q in self.queues.items():
            def body(eng, q=q):
                for (waits, fn, sem, n) in q:
                    for (s, v) in waits:
                        eng.wait_ge(s, v)
                    fn(eng).then_inc(sem, n)
            getattr(self.block, ename)(body)
        self.queues = {}


def build_nc():
    nc = bass.Bass("TRN2", target_bir_lowering=False)

    x_in = nc.dram_tensor("x_in", [BPC, T, D], F32, kind="ExternalInput").ap()
    wq_in = nc.dram_tensor("wq_in", [D, D], F32, kind="ExternalInput").ap()
    wk_in = nc.dram_tensor("wk_in", [D, D], F32, kind="ExternalInput").ap()
    wv_in = nc.dram_tensor("wv_in", [D, D], F32, kind="ExternalInput").ap()
    wo_in = nc.dram_tensor("wo_in", [D, D], F32, kind="ExternalInput").ap()
    gq_in = nc.dram_tensor("gq_in", [1, DH], F32, kind="ExternalInput").ap()
    gk_in = nc.dram_tensor("gk_in", [1, DH], F32, kind="ExternalInput").ap()
    bo_in = nc.dram_tensor("bo_in", [1, D], F32, kind="ExternalInput").ap()
    lnw_in = nc.dram_tensor("lnw_in", [D, 1], F32, kind="ExternalInput").ap()
    lnb_in = nc.dram_tensor("lnb_in", [D, 1], F32, kind="ExternalInput").ap()
    wr_in = nc.dram_tensor("wr_in", [D, E], F32, kind="ExternalInput").ap()
    w1_in = nc.dram_tensor("w1_in", [E, D, FF], F32, kind="ExternalInput").ap()
    w2_in = nc.dram_tensor("w2_in", [E, FF, D], F32, kind="ExternalInput").ap()
    fb_in = nc.dram_tensor("fb_in", [1, D], F32, kind="ExternalInput").ap()
    ident_in = nc.dram_tensor("ident_in", [128, 128], BF16, kind="ExternalInput").ap()
    utri_in = nc.dram_tensor("utri_in", [128, 128], BF16, kind="ExternalInput").ap()
    iota8_in = nc.dram_tensor("iota8_in", [1, E], F32, kind="ExternalInput").ap()
    rowmask_in = nc.dram_tensor("rowmask_in", [1, 128], F32, kind="ExternalInput").ap()
    selcol_in = nc.dram_tensor("selcol_in", [128, 1], F32, kind="ExternalInput").ap()
    e2map_in = nc.dram_tensor("e2map_in", [2, 128], BF16, kind="ExternalInput").ap()

    out_t = nc.dram_tensor("out_t", [BPC, T, D], F32, kind="ExternalOutput").ap()

    wqkv_d = nc.dram_tensor("wqkv_d", [KT, 128, NPAIR, 384], BF16, kind="Internal").ap()
    collT_d = nc.dram_tensor("collT_d", [D, T], BF16, kind="Internal").ap()
    z2_d = nc.dram_tensor("z2_d", [NLOC, D], BF16, kind="Internal").ap()
    xdisp_d = nc.dram_tensor("xdisp_d", [E * PEXP, D], BF16, kind="Internal").ap()
    ybuf_d = nc.dram_tensor("ybuf_d", [E * PEXP, D], F32, kind="Internal").ap()
    ccs_d = nc.dram_tensor("ccs_d", [128, 128], F32, kind="Internal").ap()
    ccd_d = nc.dram_tensor("ccd_d", [128, 128], F32, kind="Internal").ap()

    with ExitStack() as top:
        TE = top.enter_context

        KPOOL = 8
        ctrs = {
            "tC": Ctr(TE(nc.semaphore("tC"))),
            "vC": Ctr(TE(nc.semaphore("vC"))),
            "sC": Ctr(TE(nc.semaphore("sC"))),
            "cc": Ctr(TE(nc.semaphore("ccsm"))),
        }
        pools = {
            q: DmaPool([Ctr(TE(nc.semaphore(f"{q}_{j}")), 16) for j in range(KPOOL)])
            for q in ("qS", "qA", "qG")
        }

        cq_b = TE(nc.sbuf_tensor("cq_b", [128, D], BF16))
        ck_b = TE(nc.sbuf_tensor("ck_b", [128, D], BF16))
        sv_b = TE(nc.sbuf_tensor("sv_b", [128, D], BF16))
        bo_b = TE(nc.sbuf_tensor("bo_b", [128, D], BF16))
        fb_b = TE(nc.sbuf_tensor("fb_b", [128, D], BF16))
        gq_b = TE(nc.sbuf_tensor("gq_b", [128, 128], F32))
        gk_b = TE(nc.sbuf_tensor("gk_b", [128, 128], F32))
        io8_b = TE(nc.sbuf_tensor("io8_b", [128, E], F32))
        ident = TE(nc.sbuf_tensor("ident", [128, 128], BF16))
        utri = TE(nc.sbuf_tensor("utri", [128, 128], BF16))
        ones_f = TE(nc.sbuf_tensor("ones_f", [1, 128], F32))
        ones_bf = TE(nc.sbuf_tensor("ones_bf", [1, 128], BF16))
        ones_bcol = TE(nc.sbuf_tensor("ones_bcol", [128, 1], BF16))
        selcol = TE(nc.sbuf_tensor("selcol", [128, 1], F32))
        rowmask = TE(nc.sbuf_tensor("rowmask", [1, 128], F32))
        sigb = TE(nc.sbuf_tensor("sigb", [128, 1], F32))
        lnw_s = TE(nc.sbuf_tensor("lnw_s", [128, KT], F32))
        lnb_s = TE(nc.sbuf_tensor("lnb_s", [128, KT], BF16))
        wr_s = TE(nc.sbuf_tensor("wr_s", [128, KT, E], BF16))
        es_t = TE(nc.sbuf_tensor("es_t", [128, NTL, 2], F32))
        ws_t = TE(nc.sbuf_tensor("ws_t", [128, NTL, 2], F32))
        rs_t = TE(nc.sbuf_tensor("rs_t", [128, NTL, 2], F32))
        run_row = TE(nc.sbuf_tensor("run_row", [1, E], F32))
        runb_s = TE(nc.sbuf_tensor("runb_s", [128, E], F32))
        cpx_b = TE(nc.sbuf_tensor("cpx_b", [128, E], F32))
        var2_t = TE(nc.sbuf_tensor("var2_t", [128, NTL], F32))
        rstd2_t = TE(nc.sbuf_tensor("rstd2_t", [128, NTL], F32))
        var3_t = TE(nc.sbuf_tensor("var3_t", [128, NTL], F32))
        rstd3_t = TE(nc.sbuf_tensor("rstd3_t", [128, NTL], F32))
        lgz_t = TE(nc.sbuf_tensor("lgz_t", [128, NTL, E], F32))
        msc = TE(nc.sbuf_tensor("msc", [128, 16], F32))
        m1x = TE(nc.sbuf_tensor("m1x", [128, E], F32))
        m2x = TE(nc.sbuf_tensor("m2x", [128, E], F32))
        cab = TE(nc.sbuf_tensor("cab", [128, E], F32))
        prb = TE(nc.sbuf_tensor("prb", [128, E], F32))
        top8 = TE(nc.sbuf_tensor("top8", [128, 8], F32))
        idx8 = TE(nc.sbuf_tensor("idx8", [128, 8], U32))
        idxu = TE(nc.sbuf_tensor("idxu", [128, 2], U32))
        mm8 = TE(nc.sbuf_tensor("mm8", [128, E], BF16))
        cnt128 = TE(nc.sbuf_tensor("cnt128", [128, 128], F32))

        dp = {n: Dep() for n in [
            "consts", "lnw", "wr", "route", "run", "runb", "cpx",
            "var2", "rstd2", "var3", "rstd3", "lgz",
            "wqkvd", "collTd", "z2d", "xdispd", "ybufd", "ccsd", "ccdd", "outd",
            "msc", "m12", "cab", "prb", "top", "idxu", "mm8", "cnt"]}

        block = TE(nc.Block())
        bld = Builder(block, ctrs, pools)
        EM = bld.emit

        def bcast_row_f32(dst, d_dst, row_ap, d_row, ps, d_ps, width=D):
            nch = (width + 511) // 512
            for ch in range(nch):
                w = min(512, width - ch * 512)
                sl = slice(ch * 512, ch * 512 + w)
                EM("tensor", "tC", lambda e, sl=sl, w=w: e.matmul(
                    ps[:, 0:w], ones_f[:], row_ap[:, sl], start=True, stop=True),
                    reads=[dp["consts"], d_row], writes=[d_ps])
                EM("vector", "vC", lambda e, sl=sl, w=w: e.tensor_copy(
                    out=dst[:, sl], in_=ps[:, 0:w]), reads=[d_ps], writes=[d_dst])

        with ExitStack() as pre:
            PP = pre.enter_context
            stage = PP(nc.sbuf_tensor("stage", [128, KT, D], BF16))
            row_s = PP(nc.sbuf_tensor("row_s", [1, D], F32))
            zro = PP(nc.sbuf_tensor("zro", [128, 8192], BF16))
            ps_p = PP(nc.psum_tensor("ps_p", [128, 512], F32))
            ps_q = PP(nc.psum_tensor("ps_q", [1, 512], F32))
            d_stage, d_row, d_psp, d_psq, d_zro = Dep(), Dep(), Dep(), Dep(), Dep()

            EM("sync", "qS", lambda e: e.dma_start(out=ident[:], in_=ident_in), writes=[dp["consts"]])
            EM("sync", "qS", lambda e: e.dma_start(out=utri[:], in_=utri_in), writes=[dp["consts"]])
            EM("sync", "qS", lambda e: e.dma_start(out=selcol[:], in_=selcol_in), writes=[dp["consts"]])
            EM("sync", "qS", lambda e: e.dma_start(out=rowmask[:], in_=rowmask_in), writes=[dp["consts"]])
            EM("sync", "qS", lambda e: e.dma_start(out=gq_b[0:1, 0:DH], in_=gq_in), writes=[dp["consts"]])
            EM("sync", "qS", lambda e: e.dma_start(out=gq_b[0:1, DH:128], in_=gq_in), writes=[dp["consts"]])
            EM("sync", "qS", lambda e: e.dma_start(out=gk_b[0:1, 0:DH], in_=gk_in), writes=[dp["consts"]])
            EM("sync", "qS", lambda e: e.dma_start(out=gk_b[0:1, DH:128], in_=gk_in), writes=[dp["consts"]])
            EM("sync", "qS", lambda e: e.dma_start(out=io8_b[0:1, :], in_=iota8_in), writes=[dp["consts"]])
            for kt in range(KT):
                EM("sync", "qS", lambda e, kt=kt: e.dma_start(
                    out=lnw_s[:, kt:kt + 1], in_=lnw_in[kt * 128:(kt + 1) * 128, :]), writes=[dp["lnw"]])
                EM("gpsimd", "qG", lambda e, kt=kt: e.dma_start(
                    out=lnb_s[:, kt:kt + 1], in_=lnb_in[kt * 128:(kt + 1) * 128, :]), writes=[dp["lnw"]])
                EM("gpsimd", "qG", lambda e, kt=kt: e.dma_start(
                    out=wr_s[:, kt, :], in_=wr_in[kt * 128:(kt + 1) * 128, :]), writes=[dp["wr"]])

            EM("vector", "vC", lambda e: e.memset(ones_f[:], 1.0), writes=[dp["consts"]])
            EM("vector", "vC", lambda e: e.memset(ones_bf[:], 1.0), writes=[dp["consts"]])
            EM("vector", "vC", lambda e: e.memset(ones_bcol[:], 1.0), writes=[dp["consts"]])
            EM("vector", "vC", lambda e: e.memset(sigb[:], -SHARP * THRESH), writes=[dp["consts"]])
            EM("vector", "vC", lambda e: e.memset(run_row[:], 0.0), writes=[dp["run"]])
            EM("vector", "vC", lambda e: e.memset(cnt128[:], 0.0), writes=[dp["cnt"]])
            EM("vector", "vC", lambda e: e.memset(zro[:], 0.0), writes=[d_zro])

            EM("sync", "qS", lambda e: e.dma_start(out=row_s[:], in_=bo_in), writes=[d_row])
            bcast_row_f32(bo_b, dp["consts"], row_s, d_row, ps_p, d_psp)
            EM("sync", "qS", lambda e: e.dma_start(out=row_s[:], in_=fb_in), writes=[d_row])
            bcast_row_f32(fb_b, dp["consts"], row_s, d_row, ps_p, d_psp)

            for (gb, w) in ((gq_b, 128), (gk_b, 128), (io8_b, E)):
                EM("tensor", "tC", lambda e, gb=gb, w=w: e.matmul(
                    ps_p[:, 0:w], ones_f[:], gb[0:1, 0:w], start=True, stop=True),
                    reads=[dp["consts"]], writes=[d_psp])
                EM("vector", "vC", lambda e, gb=gb, w=w: e.tensor_copy(
                    out=gb[:, 0:w], in_=ps_p[:, 0:w]), reads=[d_psp], writes=[dp["consts"]])

            xd_r = xdisp_d.rearrange("(n p) d -> n p d", p=128)
            for i in range(18):
                EM("gpsimd", "qG", lambda e, i=i: e.dma_start(
                    out=xd_r[8 * i:8 * (i + 1)].rearrange("n p d -> p n d"),
                    in_=zro[:].rearrange("p (n x) -> p n x", x=1024)),
                    reads=[d_zro], writes=[dp["xdispd"]])

            def stage_weight(w_src, off, fold, row_kind, dst_b):
                for kt in range(KT):
                    EM("gpsimd", "qG", lambda e, kt=kt: e.dma_start(
                        out=stage[:, kt, :], in_=w_src[kt * 128:(kt + 1) * 128, :]), writes=[d_stage])
                if row_kind is not None:
                    for ch in range(2):
                        sl = slice(ch * 512, (ch + 1) * 512)
                        for kt in range(KT):
                            lhs = (lnb_s[:, kt:kt + 1] if row_kind == "lnb" else ones_bcol[:])
                            EM("tensor", "tC", lambda e, kt=kt, sl=sl, lhs=lhs: e.matmul(
                                ps_q[:, 0:512], lhs, stage[:, kt, sl],
                                start=(kt == 0), stop=(kt == KT - 1)),
                                reads=[d_stage, dp["lnw"], dp["consts"]], writes=[d_psq])
                        EM("vector", "vC", lambda e, sl=sl: e.tensor_copy(
                            out=row_s[0:1, sl], in_=ps_q[0:1, 0:512]),
                            reads=[d_psq], writes=[d_row])
                    bcast_row_f32(dst_b, dp["consts"], row_s, d_row, ps_p, d_psp)
                if fold:
                    for kt in range(KT):
                        EM("vector", "vC", lambda e, kt=kt: e.tensor_scalar(
                            out=stage[:, kt, :], in0=stage[:, kt, :],
                            scalar1=lnw_s[:, kt:kt + 1], scalar2=None, op0=ALU.mult),
                            reads=[dp["lnw"], d_stage], writes=[d_stage])
                for kt in range(KT):
                    EM("gpsimd", "qG", lambda e, kt=kt, off=off: e.dma_start(
                        out=wqkv_d[kt, :, :, off:off + 128],
                        in_=stage[:, kt, :].rearrange("p (pr c) -> p pr c", c=128)),
                        reads=[d_stage], writes=[dp["wqkvd"]])

            stage_weight(wq_in, 0, True, "lnb", cq_b)
            stage_weight(wk_in, 128, True, "lnb", ck_b)
            stage_weight(wv_in, 256, False, "ones", sv_b)

        bld.barrier()
        with ExitStack() as att:
            AE = att.enter_context
            zT = AE(nc.sbuf_tensor("zT", [128, KT, T], BF16))
            qkv = AE(nc.sbuf_tensor("qkv", [128, NT, 384], BF16))
            vbuf = AE(nc.sbuf_tensor("vbuf", [128, 2, NT, 128], BF16))
            qnT = AE(nc.sbuf_tensor("qnT", [128, 2, T], BF16))
            knT = AE(nc.sbuf_tensor("knT", [128, 2, T], BF16))
            modT = AE(nc.sbuf_tensor("modT", [128, 3, 1024], BF16))
            sc8k = AE(nc.sbuf_tensor("sc8k", [128, 4096], BF16))
            gqbc = AE(nc.sbuf_tensor("gqbc", [128, T], BF16))
            xa2 = AE(nc.sbuf_tensor("xa2", [128, 2, D], F32))
            wpair = AE(nc.sbuf_tensor("wpair", [128, 2, KT, 384], BF16))
            mu_t = AE(nc.sbuf_tensor("mu_t", [128, NT], F32))
            rstd_t = AE(nc.sbuf_tensor("rstd_t", [128, NT], F32))
            var_t = AE(nc.sbuf_tensor("var_t", [128, NT], F32))
            ssq = AE(nc.sbuf_tensor("ssq", [128, NT, 2], F32))
            ssk = AE(nc.sbuf_tensor("ssk", [128, NT, 2], F32))
            gqv = AE(nc.sbuf_tensor("gqv", [128, NT, 2], F32))
            gkv = AE(nc.sbuf_tensor("gkv", [128, NT, 2], F32))
            gq16 = AE(nc.sbuf_tensor("gq16", [128, NT, 2], BF16))
            gqT = AE(nc.sbuf_tensor("gqT", [2, T], BF16))
            e2map = AE(nc.sbuf_tensor("e2map", [2, 128], BF16))

            ps_al = AE(nc.psum_tensor("ps_al", [128, 3, 1024], F32))
            ps_co = AE(nc.psum_tensor("ps_co", [128, 2, 512], F32))
            ps_proj = ps_co[:, 0, 0:384]
            ps_tr2 = ps_co[0:2, 1, 0:64].bitcast(BF16)

            d = {n: Dep() for n in [
                "zT", "mu", "qkv", "sc8k", "gqbc", "ss", "gv", "gqT", "e2",
                "psproj", "pstr"]}
            d_vbuf = [Dep(), Dep()]
            d_qnT = [Dep(), Dep()]
            d_knT = [Dep(), Dep()]
            d_modT = [Dep(), Dep(), Dep()]
            d_xa = [Dep(), Dep()]
            d_wpair = [Dep(), Dep()]
            d_psal = [Dep(), Dep(), Dep()]
            d_psco = [Dep(), Dep()]

            EM("sync", "qS", lambda e: e.dma_start(out=e2map[:], in_=e2map_in), writes=[d["e2"]])

            def phaseA(b):
                ztile = qkv[:].rearrange("p a c -> p (a c)")[:, 0:1024]
                for i in range(NT):
                    pp = i % 2
                    EM("sync", "qS", lambda e, i=i, pp=pp: e.dma_start(
                        out=xa2[:, pp, :], in_=x_in[b, i * 128:(i + 1) * 128, :]),
                        writes=[d_xa[pp]])
                    EM("vector", "vC", lambda e, i=i, pp=pp: e.tensor_reduce(
                        out=mu_t[:, i:i + 1], in_=xa2[:, pp, :], axis=AX.X, op=ALU.add),
                        reads=[d_xa[pp]], writes=[d["mu"]])
                    EM("scalar", "sC", lambda e, i=i, pp=pp: e.activation(
                        out=gqbc[:, 0:1024], in_=xa2[:, pp, :], func=AF.Square,
                        accum_out=var_t[:, i:i + 1]),
                        reads=[d_xa[pp]], writes=[d["gqbc"], d["mu"]])
                    EM("vector", "vC", lambda e, i=i: e.tensor_scalar(
                        out=mu_t[:, i:i + 1], in0=mu_t[:, i:i + 1], scalar1=1.0 / D,
                        scalar2=None, op0=ALU.mult), reads=[d["mu"]], writes=[d["mu"]])
                    EM("vector", "vC", lambda e, i=i: e.tensor_tensor(
                        out=msc[:, 0:1], in0=mu_t[:, i:i + 1], in1=mu_t[:, i:i + 1], op=ALU.mult),
                        reads=[d["mu"]], writes=[dp["msc"]])
                    EM("vector", "vC", lambda e, i=i: e.scalar_tensor_tensor(
                        out=var_t[:, i:i + 1], in0=var_t[:, i:i + 1], scalar=1.0 / D,
                        in1=msc[:, 0:1], op0=ALU.mult, op1=ALU.subtract),
                        reads=[d["mu"], dp["msc"]], writes=[d["mu"]])
                    EM("vector", "vC", lambda e, i=i: e.tensor_scalar(
                        out=var_t[:, i:i + 1], in0=var_t[:, i:i + 1], scalar1=LN_EPS,
                        scalar2=None, op0=ALU.add), reads=[d["mu"]], writes=[d["mu"]])
                    EM("vector", "vC", lambda e, i=i, pp=pp: e.tensor_scalar(
                        out=ztile, in0=xa2[:, pp, :], scalar1=mu_t[:, i:i + 1],
                        scalar2=None, op0=ALU.subtract),
                        reads=[d_xa[pp], d["mu"]], writes=[d["qkv"]])
                    for kt in range(KT):
                        j = kt % 2
                        trv = ps_co[:, j, 0:64].bitcast(BF16)
                        EM("tensor", "tC", lambda e, kt=kt, trv=trv: e.transpose(
                            trv, ztile[:, kt * 128:(kt + 1) * 128], ident[:]),
                            reads=[d["qkv"], dp["consts"]], writes=[d_psco[j]])
                        EM("vector", "vC", lambda e, kt=kt, i=i, trv=trv: e.tensor_copy(
                            out=zT[:, kt, i * 128:(i + 1) * 128], in_=trv),
                            reads=[d_psco[j]], writes=[d["zT"]])
                EM("scalar", "sC", lambda e: e.activation(out=rstd_t[:], in_=var_t[:], func=AF.Sqrt),
                   reads=[d["mu"]], writes=[d["mu"]])
                EM("vector", "vC", lambda e: e.reciprocal(out=rstd_t[:], in_=rstd_t[:]),
                   reads=[d["mu"]], writes=[d["mu"]])

            def phaseB(b):
                q3 = qkv
                sst = sc8k[:].rearrange("p (n c) -> p n c", c=128)
                for p in range(NPAIR):
                    par = p % 2
                    EM("sync", "qS", lambda e, p=p, par=par: e.dma_start(
                        out=wpair[:, par],
                        in_=wqkv_d[:, :, p, :].rearrange("k p c -> p k c")),
                        reads=[dp["wqkvd"]], writes=[d_wpair[par]])
                    for i in range(NT):
                        for kt in range(KT):
                            EM("tensor", "tC", lambda e, i=i, kt=kt, par=par: e.matmul(
                                ps_proj, zT[:, kt, i * 128:(i + 1) * 128],
                                wpair[:, par, kt, :], start=(kt == 0), stop=(kt == KT - 1)),
                                reads=[d["zT"], d_wpair[par]], writes=[d_psco[0]])
                        EM("vector", "vC", lambda e, i=i: e.tensor_copy(
                            out=qkv[:, i, :], in_=ps_proj),
                            reads=[d_psco[0]], writes=[d["qkv"]])
                    rstd3 = rstd_t[:].unsqueeze(2).to_broadcast([128, NT, 128])
                    mu3 = mu_t[:].unsqueeze(2).to_broadcast([128, NT, 128])
                    for (off, cb) in ((0, cq_b), (128, ck_b)):
                        EM("vector", "vC", lambda e, off=off, rstd3=rstd3: e.tensor_tensor(
                            out=q3[:, :, off:off + 128], in0=q3[:, :, off:off + 128],
                            in1=rstd3, op=ALU.mult), reads=[d["qkv"], d["mu"]], writes=[d["qkv"]])
                        cb3 = cb[:, p * 128:(p + 1) * 128].unsqueeze(1).to_broadcast([128, NT, 128])
                        EM("vector", "vC", lambda e, off=off, cb3=cb3: e.tensor_tensor(
                            out=q3[:, :, off:off + 128], in0=q3[:, :, off:off + 128],
                            in1=cb3, op=ALU.add), reads=[d["qkv"], dp["consts"]], writes=[d["qkv"]])
                    sv3 = sv_b[:, p * 128:(p + 1) * 128].unsqueeze(1).to_broadcast([128, NT, 128])
                    EM("vector", "vC", lambda e, sv3=sv3, mu3=mu3: e.tensor_tensor(
                        out=sst, in0=sv3, in1=mu3, op=ALU.mult),
                        reads=[dp["consts"], d["mu"]], writes=[d["sc8k"]])
                    EM("vector", "vC", lambda e: e.tensor_tensor(
                        out=q3[:, :, 256:384], in0=q3[:, :, 256:384], in1=sst, op=ALU.add),
                        reads=[d["sc8k"], d["qkv"]], writes=[d["qkv"]])
                    for (off, ssx) in ((0, ssq), (128, ssk)):
                        EM("vector", "vC", lambda e, off=off: e.tensor_tensor(
                            out=sst, in0=q3[:, :, off:off + 128], in1=q3[:, :, off:off + 128],
                            op=ALU.mult), reads=[d["qkv"]], writes=[d["sc8k"]])
                        EM("vector", "vC", lambda e, ssx=ssx: e.tensor_reduce(
                            out=ssx[:], in_=sst.rearrange("p n (h x) -> p n h x", x=DH),
                            axis=AX.X, op=ALU.add), reads=[d["sc8k"]], writes=[d["ss"]])
                        EM("scalar", "sC", lambda e, ssx=ssx: e.activation(
                            out=ssx[:].rearrange("p a c -> p (a c)"),
                            in_=ssx[:].rearrange("p a c -> p (a c)"), func=AF.Sqrt),
                            reads=[d["ss"]], writes=[d["ss"]])
                        EM("vector", "vC", lambda e, ssx=ssx: e.tensor_scalar(
                            out=ssx[:].rearrange("p a c -> p (a c)"),
                            in0=ssx[:].rearrange("p a c -> p (a c)"),
                            scalar1=NORM_EPS, scalar2=None, op0=ALU.max),
                            reads=[d["ss"]], writes=[d["ss"]])
                        EM("vector", "vC", lambda e, ssx=ssx: e.reciprocal(
                            out=ssx[:].rearrange("p a c -> p (a c)"),
                            in_=ssx[:].rearrange("p a c -> p (a c)")),
                            reads=[d["ss"]], writes=[d["ss"]])
                        rb = ssx[:].unsqueeze(3).to_broadcast([128, NT, 2, DH])
                        EM("vector", "vC", lambda e, off=off, rb=rb: e.tensor_tensor(
                            out=q3[:, :, off:off + 128].rearrange("p n (h x) -> p n h x", x=DH),
                            in0=q3[:, :, off:off + 128].rearrange("p n (h x) -> p n h x", x=DH),
                            in1=rb, op=ALU.mult), reads=[d["qkv"], d["ss"]], writes=[d["qkv"]])
                    gk4 = gk_b[:].rearrange("p (h x) -> p h x", x=DH).unsqueeze(1).to_broadcast([128, NT, 2, DH])
                    gq4 = gq_b[:].rearrange("p (h x) -> p h x", x=DH).unsqueeze(1).to_broadcast([128, NT, 2, DH])
                    EM("vector", "vC", lambda e, gk4=gk4: e.tensor_tensor(
                        out=sst.rearrange("p n (h x) -> p n h x", x=DH),
                        in0=q3[:, :, 128:256].rearrange("p n (h x) -> p n h x", x=DH),
                        in1=gk4, op=ALU.mult), reads=[d["qkv"], dp["consts"]], writes=[d["sc8k"]])
                    EM("vector", "vC", lambda e: e.tensor_reduce(
                        out=gkv[:], in_=sst.rearrange("p n (h x) -> p n h x", x=DH),
                        axis=AX.X, op=ALU.add), reads=[d["sc8k"]], writes=[d["gv"]])
                    EM("vector", "vC", lambda e, gq4=gq4: e.tensor_tensor(
                        out=sst.rearrange("p n (h x) -> p n h x", x=DH),
                        in0=q3[:, :, 0:128].rearrange("p n (h x) -> p n h x", x=DH),
                        in1=gq4, op=ALU.mult), reads=[d["qkv"], dp["consts"]], writes=[d["sc8k"]])
                    EM("vector", "vC", lambda e: e.tensor_reduce(
                        out=gqv[:], in_=sst.rearrange("p n (h x) -> p n h x", x=DH),
                        axis=AX.X, op=ALU.add), reads=[d["sc8k"]], writes=[d["gv"]])
                    gkb4 = gkv[:].unsqueeze(3).to_broadcast([128, NT, 2, DH])
                    EM("vector", "vC", lambda e, par=par, gkb4=gkb4: e.tensor_tensor(
                        out=vbuf[:, par].rearrange("p n (h x) -> p n h x", x=DH),
                        in0=q3[:, :, 256:384].rearrange("p n (h x) -> p n h x", x=DH),
                        in1=gkb4, op=ALU.mult), reads=[d["qkv"], d["gv"]], writes=[d_vbuf[par]])
                    EM("vector", "vC", lambda e: e.tensor_copy(out=gq16[:], in_=gqv[:]),
                       reads=[d["gv"]], writes=[d["gv"]])
                    trv0 = ps_co[:, 0, 0:64].bitcast(BF16)
                    trv1 = ps_co[:, 1, 0:64].bitcast(BF16)
                    for i in range(NT):
                        EM("tensor", "tC", lambda e, i=i: e.transpose(
                            trv0, qkv[:, i, 0:128], ident[:]),
                            reads=[d["qkv"], dp["consts"]], writes=[d_psco[0]])
                        EM("vector", "vC", lambda e, i=i, par=par: e.tensor_copy(
                            out=qnT[:, par, i * 128:(i + 1) * 128], in_=trv0),
                            reads=[d_psco[0]], writes=[d_qnT[par]])
                        EM("tensor", "tC", lambda e, i=i: e.transpose(
                            trv1, qkv[:, i, 128:256], ident[:]),
                            reads=[d["qkv"], dp["consts"]], writes=[d_psco[1]])
                        EM("vector", "vC", lambda e, i=i, par=par: e.tensor_copy(
                            out=knT[:, par, i * 128:(i + 1) * 128], in_=trv1),
                            reads=[d_psco[1]], writes=[d_knT[par]])
                        EM("tensor", "tC", lambda e, i=i: e.transpose(
                            ps_tr2, gq16[:, i, :], ident[:]),
                            reads=[d["gv"], dp["consts"]], writes=[d_psco[1]])
                        EM("vector", "vC", lambda e, i=i, par=par: e.tensor_copy(
                            out=gqT[0:2, i * 128:(i + 1) * 128], in_=ps_tr2),
                            reads=[d_psco[1]], writes=[d["gqT"]])
                    for ch in range(T // 512):
                        EM("tensor", "tC", lambda e, ch=ch, par=par: e.matmul(
                            ps_co[:, 0, 0:512], e2map[:],
                            gqT[0:2, ch * 512:(ch + 1) * 512], start=True, stop=True),
                            reads=[d["gqT"], d["e2"]], writes=[d_psco[0]])
                        EM("vector", "vC", lambda e, ch=ch: e.tensor_copy(
                            out=gqbc[:, ch * 512:(ch + 1) * 512], in_=ps_co[:, 0, 0:512]),
                            reads=[d_psco[0]], writes=[d["gqbc"]])
                    for tch in range(T // 512):
                        cop = tch % 2
                        for s in range(NT + 2):
                            if s < NT:
                                ap = s % 3
                                EM("tensor", "tC", lambda e, s=s, ap=ap, tch=tch, par=par: e.matmul(
                                    ps_al[:, ap, 0:512], knT[0:64, par, s * 128:(s + 1) * 128],
                                    qnT[0:64, par, tch * 512:(tch + 1) * 512],
                                    start=True, stop=True, tile_position=(0, 0)),
                                    reads=[d_knT[par], d_qnT[par]], writes=[d_psal[ap]])
                                EM("tensor", "tC", lambda e, s=s, ap=ap, tch=tch, par=par: e.matmul(
                                    ps_al[:, ap, 512:1024], knT[64:128, par, s * 128:(s + 1) * 128],
                                    qnT[64:128, par, tch * 512:(tch + 1) * 512],
                                    start=True, stop=True, tile_position=(64, 0)),
                                    reads=[d_knT[par], d_qnT[par]], writes=[d_psal[ap]])
                                EM("scalar", "sC", lambda e, ap=ap: e.activation(
                                    out=modT[:, ap, :], in_=ps_al[:, ap, :],
                                    func=AF.Sigmoid, scale=SHARP, bias=sigb[:]),
                                    reads=[d_psal[ap], dp["consts"]], writes=[d_modT[ap]])
                            if s >= 2:
                                sc = s - 2
                                ac = sc % 3
                                EM("tensor", "tC", lambda e, sc=sc, ac=ac, cop=cop, par=par: e.matmul(
                                    ps_co[0:64, cop, :], vbuf[:, par, sc, 0:64],
                                    modT[:, ac, 0:512], start=(sc == 0), stop=(sc == NT - 1),
                                    tile_position=(0, 0)),
                                    reads=[d_vbuf[par], d_modT[ac]], writes=[d_psco[cop]])
                                EM("tensor", "tC", lambda e, sc=sc, ac=ac, cop=cop, par=par: e.matmul(
                                    ps_co[64:128, cop, :], vbuf[:, par, sc, 64:128],
                                    modT[:, ac, 512:1024], start=(sc == 0), stop=(sc == NT - 1),
                                    tile_position=(0, 64), skip_group_check=True),
                                    reads=[d_vbuf[par], d_modT[ac]], writes=[d_psco[cop]])
                        EM("vector", "vC", lambda e, tch=tch, cop=cop: e.tensor_tensor(
                            out=sc8k[:, tch * 512:(tch + 1) * 512], in0=ps_co[:, cop, :],
                            in1=gqbc[:, tch * 512:(tch + 1) * 512], op=ALU.mult),
                            reads=[d_psco[cop], d["gqbc"]], writes=[d["sc8k"]])
                    EM("scalar", "qA", lambda e, p=p: e.dma_start(
                        out=collT_d[p * 128:(p + 1) * 128, :], in_=sc8k[:]),
                        reads=[d["sc8k"]], writes=[dp["collTd"]])

            def phaseC1(b):
                wo_al = zT[:, :, 0:1024]
                colsb = [qkv[:].rearrange("p a c -> p (a c)")[:, 0:4096]
                         .rearrange("p (k c) -> p k c", c=512),
                         qkv[:].rearrange("p a c -> p (a c)")[:, 4096:8192]
                         .rearrange("p (k c) -> p k c", c=512)]
                r2_al = vbuf[:].rearrange("p a n c -> p (a n c)").bitcast(F32)[:, 0:1024]
                z2t_al = sc8k[:, 0:1024]
                z2T_al = modT[:].rearrange("p a c -> p (a c)")[:, 0:1024] \
                    .rearrange("p (k c) -> p k c", c=128)
                xsq_al = gqbc[:, 0:1024]
                d_r2 = d_vbuf
                for kt in range(KT):
                    EM("gpsimd", "qG", lambda e, kt=kt: e.dma_start(
                        out=wo_al[:, kt, :], in_=wo_in[kt * 128:(kt + 1) * 128, :]),
                        writes=[d["zT"]])
                for ci in range(8):
                    cb = ci % 2
                    for dt in range(KT):
                        EM("sync", "qS", lambda e, ci=ci, cb=cb, dt=dt: e.dma_start(
                            out=colsb[cb][:, dt, :],
                            in_=collT_d[dt * 128:(dt + 1) * 128, ci * 512:(ci + 1) * 512]),
                            reads=[dp["collTd"]], writes=[d["qkv"]])
                    for tsub in range(4):
                        i = ci * 4 + tsub
                        gi = b * NT + i
                        pp = i % 3
                        xp = i % 2
                        EM("sync", "qS", lambda e, i=i, xp=xp: e.dma_start(
                            out=xa2[:, xp, :], in_=x_in[b, i * 128:(i + 1) * 128, :]),
                            writes=[d_xa[xp]])
                        for dch in range(2):
                            for dt in range(KT):
                                EM("tensor", "tC", lambda e, cb=cb, dt=dt, tsub=tsub, dch=dch, pp=pp: e.matmul(
                                    ps_al[:, pp, dch * 512:(dch + 1) * 512],
                                    colsb[cb][:, dt, tsub * 128:(tsub + 1) * 128],
                                    wo_al[:, dt, dch * 512:(dch + 1) * 512],
                                    start=(dt == 0), stop=(dt == KT - 1)),
                                    reads=[d["qkv"], d["zT"]], writes=[d_psal[pp]])
                        EM("vector", "vC", lambda e, pp=pp: e.tensor_tensor(
                            out=r2_al, in0=ps_al[:, pp, :], in1=bo_b[:], op=ALU.add),
                            reads=[d_psal[pp], dp["consts"]], writes=d_r2)
                        EM("vector", "vC", lambda e, xp=xp: e.tensor_tensor(
                            out=r2_al, in0=r2_al, in1=xa2[:, xp, :], op=ALU.add),
                            reads=[d_xa[xp]] + d_r2, writes=d_r2)
                        EM("vector", "vC", lambda e, gi=gi: e.tensor_reduce(
                            out=msc[:, 1:2], in_=r2_al, axis=AX.X, op=ALU.add),
                            reads=d_r2, writes=[dp["msc"]])
                        EM("vector", "vC", lambda e: e.tensor_scalar(
                            out=msc[:, 1:2], in0=msc[:, 1:2], scalar1=1.0 / D,
                            scalar2=None, op0=ALU.mult), reads=[dp["msc"]], writes=[dp["msc"]])
                        EM("scalar", "sC", lambda e, gi=gi: e.activation(
                            out=xsq_al, in_=r2_al, func=AF.Square,
                            accum_out=var2_t[:, gi:gi + 1]),
                            reads=d_r2, writes=[d["gqbc"], dp["var2"]])
                        EM("vector", "vC", lambda e: e.tensor_tensor(
                            out=msc[:, 2:3], in0=msc[:, 1:2], in1=msc[:, 1:2], op=ALU.mult),
                            reads=[dp["msc"]], writes=[dp["msc"]])
                        EM("vector", "vC", lambda e, gi=gi: e.scalar_tensor_tensor(
                            out=var2_t[:, gi:gi + 1], in0=var2_t[:, gi:gi + 1], scalar=1.0 / D,
                            in1=msc[:, 2:3], op0=ALU.mult, op1=ALU.subtract),
                            reads=[dp["msc"], dp["var2"]], writes=[dp["var2"]])
                        EM("vector", "vC", lambda e, gi=gi: e.tensor_scalar(
                            out=var2_t[:, gi:gi + 1], in0=var2_t[:, gi:gi + 1],
                            scalar1=LN_EPS, scalar2=None, op0=ALU.add),
                            reads=[dp["var2"]], writes=[dp["var2"]])
                        EM("vector", "vC", lambda e: e.tensor_scalar(
                            out=z2t_al, in0=r2_al, scalar1=msc[:, 1:2],
                            scalar2=None, op0=ALU.subtract),
                            reads=d_r2 + [dp["msc"]], writes=[d["sc8k"]])
                        EM("scalar", "qA", lambda e, gi=gi: e.dma_start(
                            out=z2_d[gi * 128:(gi + 1) * 128, :], in_=z2t_al),
                            reads=[d["sc8k"]], writes=[dp["z2d"]])
                        for kt in range(KT):
                            j = kt % 2
                            trv = ps_co[:, j, 0:64].bitcast(BF16)
                            EM("tensor", "tC", lambda e, kt=kt, trv=trv: e.transpose(
                                trv, z2t_al[:, kt * 128:(kt + 1) * 128], ident[:]),
                                reads=[d["sc8k"], dp["consts"]], writes=[d_psco[j]])
                            EM("vector", "vC", lambda e, kt=kt, trv=trv: e.tensor_copy(
                                out=z2T_al[:, kt, :], in_=trv),
                                reads=[d_psco[j]], writes=d_modT)
                        for kt in range(KT):
                            EM("tensor", "tC", lambda e, kt=kt: e.matmul(
                                ps_co[:, 0, 0:E], z2T_al[:, kt, :], wr_s[:, kt, :],
                                start=(kt == 0), stop=(kt == KT - 1)),
                                reads=d_modT + [dp["wr"]], writes=[d_psco[0]])
                        EM("vector", "vC", lambda e, gi=gi: e.tensor_copy(
                            out=lgz_t[:, gi, :], in_=ps_co[:, 0, 0:E]),
                            reads=[d_psco[0]], writes=[dp["lgz"]])

            for b in range(BPC):
                phaseA(b)
                phaseB(b)
                phaseC1(b)

            EM("scalar", "sC", lambda e: e.activation(out=rstd2_t[:], in_=var2_t[:], func=AF.Sqrt),
               reads=[dp["var2"]], writes=[dp["rstd2"]])
            EM("vector", "vC", lambda e: e.reciprocal(out=rstd2_t[:], in_=rstd2_t[:]),
               reads=[dp["rstd2"]], writes=[dp["rstd2"]])

            z2c_al = sc8k[:, 0:1024]
            x2c_al = sc8k[:, 1024:2048]
            for gi in range(NTL):
                EM("sync", "qS", lambda e, gi=gi: e.dma_start(
                    out=z2c_al, in_=z2_d[gi * 128:(gi + 1) * 128, :]),
                    reads=[dp["z2d"]], writes=[d["sc8k"]])
                EM("vector", "vC", lambda e, gi=gi: e.tensor_scalar(
                    out=x2c_al, in0=z2c_al, scalar1=rstd2_t[:, gi:gi + 1],
                    scalar2=None, op0=ALU.mult),
                    reads=[d["sc8k"], dp["rstd2"]], writes=[d["sc8k"]])
                EM("vector", "vC", lambda e, gi=gi: e.tensor_scalar(
                    out=prb[:], in0=lgz_t[:, gi, :], scalar1=rstd2_t[:, gi:gi + 1],
                    scalar2=None, op0=ALU.mult),
                    reads=[dp["lgz"], dp["rstd2"]], writes=[dp["prb"]])
                EM("vector", "vC", lambda e: e.tensor_reduce(
                    out=msc[:, 3:4], in_=prb[:], axis=AX.X, op=ALU.max, negate=True),
                    reads=[dp["prb"]], writes=[dp["msc"]])
                EM("scalar", "sC", lambda e: e.activation(
                    out=prb[:], in_=prb[:], func=AF.Exp, bias=msc[:, 3:4]),
                    reads=[dp["prb"], dp["msc"]], writes=[dp["prb"]])
                EM("vector", "vC", lambda e: e.tensor_reduce(
                    out=msc[:, 4:5], in_=prb[:], axis=AX.X, op=ALU.add),
                    reads=[dp["prb"]], writes=[dp["msc"]])
                EM("vector", "vC", lambda e: e.reciprocal(out=msc[:, 5:6], in_=msc[:, 4:5]),
                   reads=[dp["msc"]], writes=[dp["msc"]])
                EM("vector", "vC", lambda e: e.tensor_scalar(
                    out=prb[:], in0=prb[:], scalar1=msc[:, 5:6], scalar2=None, op0=ALU.mult),
                    reads=[dp["prb"], dp["msc"]], writes=[dp["prb"]])
                EM("vector", "vC", lambda e: e.max(out=top8[:], in_=prb[:]),
                   reads=[dp["prb"]], writes=[dp["top"]])
                EM("vector", "vC", lambda e: e.max_index(out=idx8[:], in_max=top8[:], in_values=prb[:]),
                   reads=[dp["top"], dp["prb"]], writes=[dp["top"]])
                EM("vector", "vC", lambda e, gi=gi: e.tensor_copy(
                    out=ws_t[:, gi, :], in_=top8[:, 0:2]), reads=[dp["top"]], writes=[dp["route"]])
                EM("vector", "vC", lambda e, gi=gi: e.tensor_copy(
                    out=es_t[:, gi, :], in_=idx8[:, 0:2]), reads=[dp["top"]], writes=[dp["route"]])
                EM("vector", "vC", lambda e, gi=gi: e.tensor_scalar(
                    out=m1x[:], in0=io8_b[:], scalar1=es_t[:, gi, 0:1], scalar2=None,
                    op0=ALU.is_equal), reads=[dp["consts"], dp["route"]], writes=[dp["m12"]])
                EM("vector", "vC", lambda e, gi=gi: e.tensor_scalar(
                    out=m2x[:], in0=io8_b[:], scalar1=es_t[:, gi, 1:2], scalar2=None,
                    op0=ALU.is_equal), reads=[dp["consts"], dp["route"]], writes=[dp["m12"]])
                EM("vector", "vC", lambda e: e.tensor_tensor(
                    out=cab[:], in0=m1x[:], in1=m2x[:], op=ALU.add),
                    reads=[dp["m12"]], writes=[dp["cab"]])
                EM("vector", "vC", lambda e: e.tensor_copy(out=mm8[:], in_=cab[:]),
                   reads=[dp["cab"]], writes=[dp["mm8"]])
                EM("tensor", "tC", lambda e: e.matmul(
                    ps_co[:, 0, 0:E], utri[:], mm8[:], start=True, stop=True),
                    reads=[dp["consts"], dp["mm8"]], writes=[d_psco[0]])
                EM("tensor", "tC", lambda e: e.matmul(
                    ps_co[:, 1, 0:E], ones_f[:], run_row[:], start=True, stop=True),
                    reads=[dp["consts"], dp["run"]], writes=[d_psco[1]])
                EM("vector", "vC", lambda e: e.tensor_copy(out=runb_s[:], in_=ps_co[:, 1, 0:E]),
                   reads=[d_psco[1]], writes=[dp["runb"]])
                EM("vector", "vC", lambda e: e.tensor_tensor(
                    out=cab[:], in0=ps_co[:, 0, 0:E], in1=runb_s[:], op=ALU.add),
                    reads=[d_psco[0], dp["runb"], dp["cab"]], writes=[dp["cab"]])
                for j in range(2):
                    EM("vector", "vC", lambda e, gi=gi, j=j: e.scalar_tensor_tensor(
                        out=prb[:], in0=cab[:], scalar=1.0, in1=(m1x[:] if j == 0 else m2x[:]),
                        op0=ALU.mult, op1=ALU.mult, accum_out=rs_t[:, gi, j:j + 1]),
                        reads=[dp["cab"], dp["m12"]], writes=[dp["prb"], dp["route"]])
                EM("tensor", "tC", lambda e: e.matmul(
                    ps_co[0:1, 1, 0:E], ones_bcol[:], mm8[:], start=True, stop=True),
                    reads=[dp["consts"], dp["mm8"]], writes=[d_psco[1]])
                EM("vector", "vC", lambda e: e.tensor_tensor(
                    out=run_row[:], in0=run_row[:], in1=ps_co[0:1, 1, 0:E], op=ALU.add),
                    reads=[d_psco[1], dp["run"]], writes=[dp["run"]])
                for j in range(2):
                    EM("vector", "vC", lambda e, gi=gi, j=j: e.tensor_scalar(
                        out=msc[:, 8 + j:9 + j], in0=rs_t[:, gi, j:j + 1],
                        scalar1=float(PEXP - 1), scalar2=None, op0=ALU.min),
                        reads=[dp["route"]], writes=[dp["msc"]])
                    EM("vector", "vC", lambda e, gi=gi, j=j: e.scalar_tensor_tensor(
                        out=msc[:, 10 + j:11 + j], in0=es_t[:, gi, j:j + 1],
                        scalar=float(PEXP), in1=msc[:, 8 + j:9 + j],
                        op0=ALU.mult, op1=ALU.add),
                        reads=[dp["route"], dp["msc"]], writes=[dp["msc"]])
                    EM("vector", "vC", lambda e, j=j: e.tensor_copy(
                        out=idxu[:, j:j + 1], in_=msc[:, 10 + j:11 + j]),
                        reads=[dp["msc"]], writes=[dp["idxu"]])
                    EM("gpsimd", "qG", lambda e, j=j: e.indirect_dma_start(
                        out=xdisp_d, out_offset=IndirectOffsetOnAxis(idxu[:, j:j + 1], 0),
                        in_=x2c_al, in_offset=None),
                        reads=[dp["idxu"], d["sc8k"]], writes=[dp["xdispd"]])

            EM("tensor", "tC", lambda e: e.matmul(
                ps_co[:, 0, 0:E], rowmask[:], run_row[:], start=True, stop=True),
                reads=[dp["consts"], dp["run"]], writes=[d_psco[0]])
            EM("vector", "vC", lambda e: e.tensor_copy(out=cnt128[:, 0:E], in_=ps_co[:, 0, 0:E]),
               reads=[d_psco[0]], writes=[dp["cnt"]])
            EM("gpsimd", "qG", lambda e: e.dma_start(out=ccs_d, in_=cnt128[:]),
               reads=[dp["cnt"]], writes=[dp["ccsd"]])
            EM("gpsimd", "cc", lambda e: e.collective_compute(
                "AllReduce", ALU.add, replica_groups=[list(range(NC))],
                ins=[ccs_d.opt()], outs=[ccd_d.opt()]),
                reads=[dp["ccsd"]], writes=[dp["ccdd"]])
            EM("gpsimd", "qG", lambda e: e.dma_start(out=cnt128[:], in_=ccd_d),
               reads=[dp["ccdd"], dp["cnt"]], writes=[dp["cnt"]])
            EM("tensor", "tC", lambda e: e.matmul(
                ps_co[0:1, 0, 0:E], selcol[:], cnt128[:, 0:E], start=True, stop=True),
                reads=[dp["consts"], dp["cnt"]], writes=[d_psco[0]])
            EM("vector", "vC", lambda e: e.tensor_copy(out=cpx_b[0:1, :], in_=ps_co[0:1, 0, 0:E]),
               reads=[d_psco[0]], writes=[dp["cpx"]])
            EM("tensor", "tC", lambda e: e.matmul(
                ps_co[:, 1, 0:E], ones_f[:], cpx_b[0:1, :], start=True, stop=True),
                reads=[dp["consts"], dp["cpx"]], writes=[d_psco[1]])
            EM("vector", "vC", lambda e: e.tensor_copy(out=cpx_b[:], in_=ps_co[:, 1, 0:E]),
               reads=[d_psco[1]], writes=[dp["cpx"]])

        bld.barrier()
        with ExitStack() as ffn:
            FE = ffn.enter_context
            RHS = [(0, 1024), (1024, 1280)]
            RMAX = 1280
            xeT = FE(nc.sbuf_tensor("xeT", [128, 2, KT, RMAX], BF16))
            hT = FE(nc.sbuf_tensor("hT", [128, 8, RMAX], BF16))
            w1s = FE(nc.sbuf_tensor("w1s", [128, 2, KT, 1024], BF16))
            w2s = FE(nc.sbuf_tensor("w2s", [128, 2, 8, 1024], BF16))
            wst = FE(nc.sbuf_tensor("wst", [128, 2, 1024], F32))
            yac = FE(nc.sbuf_tensor("yac", [128, RMAX // 128, 1024], F32))
            ps_h = FE(nc.psum_tensor("ps_h", [128, 2, 1024], F32))
            ps_y = FE(nc.psum_tensor("ps_y", [128, 2, 1024], F32))

            d_xeT = [Dep(), Dep()]
            d_hT = Dep()
            d_w1 = [Dep(), Dep()]
            d_w2 = [Dep(), Dep()]
            d_wst = Dep()
            d_yac = Dep()
            d_psh = [Dep(), Dep()]
            d_psy = [Dep(), Dep()]

            def emit_xeT(idx):
                e_i, rh = idx // 2, idx % 2
                r0, rl = RHS[rh]
                base = e_i * PEXP + r0
                xp = idx % 2
                for kt in range(KT):
                    EM("sync", "qS", lambda e, base=base, kt=kt, xp=xp, rl=rl: e.dma_start(
                        out=xeT[:, xp, kt, 0:rl],
                        in_=xdisp_d[base:base + rl, kt * 128:(kt + 1) * 128],
                        transpose=True),
                        reads=[dp["xdispd"]], writes=[d_xeT[xp]])

            def emit_wload(we, wfb):
                wb = wfb % 2
                for (ws, dw, buf, dbuf) in ((w1_in, None, w1s, d_w1), (None, w2_in, w2s, d_w2)):
                    for q in range(4):
                        for k2 in range(2):
                            kt = q * 2 + k2
                            if ws is not None:
                                EM("sync", "qS", lambda e, we=we, wfb=wfb, kt=kt, k2=k2: e.dma_start(
                                    out=wst[:, k2, :],
                                    in_=w1_in[we, kt * 128:(kt + 1) * 128, wfb * 1024:(wfb + 1) * 1024]),
                                    writes=[d_wst])
                            else:
                                EM("sync", "qS", lambda e, we=we, wfb=wfb, kt=kt, k2=k2: e.dma_start(
                                    out=wst[:, k2, :],
                                    in_=w2_in[we, wfb * 1024 + kt * 128:wfb * 1024 + (kt + 1) * 128, :]),
                                    writes=[d_wst])
                        EM("vector", "vC", lambda e, wb=wb, q=q, buf=buf: e.tensor_copy(
                            out=buf[:, wb, q * 2:q * 2 + 2, :], in_=wst[:]),
                            reads=[d_wst], writes=[dbuf[wb]])

            emit_xeT(0)
            for idx in range(2 * E):
                e_i, rh = idx // 2, idx % 2
                r0, rl = RHS[rh]
                base = e_i * PEXP + r0
                xp = idx % 2
                nrt = rl // 128
                for fb in range(4):
                    wb = fb % 2
                    if idx == 0 and fb == 0:
                        emit_wload(0, 0)
                    if fb == 1 and idx + 1 < 2 * E:
                        emit_xeT(idx + 1)
                    if fb < 3:
                        emit_wload(e_i, fb + 1)
                    elif rh == 0:
                        emit_wload(e_i, 0)
                    elif e_i < E - 1:
                        emit_wload(e_i + 1, 0)
                    for ft in range(8):
                        hp = ft % 2
                        for rc in range(2):
                            for kt in range(KT):
                                EM("tensor", "tC", lambda e, ft=ft, rc=rc, kt=kt, wb=wb, hp=hp, xp=xp: e.matmul(
                                    ps_h[:, hp, rc * 512:(rc + 1) * 512],
                                    w1s[:, wb, kt, ft * 128:(ft + 1) * 128],
                                    xeT[:, xp, kt, rc * 512:(rc + 1) * 512],
                                    start=(kt == 0), stop=(kt == KT - 1)),
                                    reads=[d_w1[wb], d_xeT[xp]], writes=[d_psh[hp]])
                        EM("scalar", "sC", lambda e, ft=ft, hp=hp: e.activation(
                            out=hT[:, ft, 0:1024], in_=ps_h[:, hp, :],
                            func=AF.Gelu_apprx_tanh),
                            reads=[d_psh[hp]], writes=[d_hT])
                        if rl > 1024:
                            yp2 = ft % 2
                            for kt in range(KT):
                                EM("tensor", "tC", lambda e, ft=ft, kt=kt, wb=wb, yp2=yp2, xp=xp: e.matmul(
                                    ps_y[:, yp2, 0:256],
                                    w1s[:, wb, kt, ft * 128:(ft + 1) * 128],
                                    xeT[:, xp, kt, 1024:1280],
                                    start=(kt == 0), stop=(kt == KT - 1)),
                                    reads=[d_w1[wb], d_xeT[xp]], writes=[d_psy[yp2]])
                            EM("scalar", "sC", lambda e, ft=ft, yp2=yp2: e.activation(
                                out=hT[:, ft, 1024:1280], in_=ps_y[:, yp2, 0:256],
                                func=AF.Gelu_apprx_tanh),
                                reads=[d_psy[yp2]], writes=[d_hT])
                    for rt in range(nrt):
                        yp = rt % 2
                        for dch in range(2):
                            for ft in range(8):
                                EM("tensor", "tC", lambda e, rt=rt, dch=dch, ft=ft, wb=wb, yp=yp: e.matmul(
                                    ps_y[:, yp, dch * 512:(dch + 1) * 512],
                                    hT[:, ft, rt * 128:(rt + 1) * 128],
                                    w2s[:, wb, ft, dch * 512:(dch + 1) * 512],
                                    start=(ft == 0), stop=(ft == 7)),
                                    reads=[d_hT, d_w2[wb]], writes=[d_psy[yp]])
                        if fb == 0:
                            EM("vector", "vC", lambda e, rt=rt, yp=yp: e.tensor_copy(
                                out=yac[:, rt, :], in_=ps_y[:, yp, :]),
                                reads=[d_psy[yp]], writes=[d_yac])
                        else:
                            EM("vector", "vC", lambda e, rt=rt, yp=yp: e.tensor_tensor(
                                out=yac[:, rt, :], in0=yac[:, rt, :], in1=ps_y[:, yp, :],
                                op=ALU.add),
                                reads=[d_psy[yp], d_yac], writes=[d_yac])
                for rt in range(nrt):
                    EM("sync", "qS", lambda e, base=base, rt=rt: e.dma_start(
                        out=ybuf_d[base + rt * 128:base + (rt + 1) * 128, :],
                        in_=yac[:, rt, :]),
                        reads=[d_yac], writes=[dp["ybufd"]])

        bld.barrier()
        with ExitStack() as comb:
            CE = comb.enter_context
            z3 = CE(nc.sbuf_tensor("z3", [128, NTL, D], BF16))
            g1 = CE(nc.sbuf_tensor("g1", [128, D], F32))
            g2 = CE(nc.sbuf_tensor("g2", [128, D], F32))
            z2c = CE(nc.sbuf_tensor("z2c", [128, D], BF16))
            x2c = CE(nc.sbuf_tensor("x2c", [128, D], BF16))
            acc = CE(nc.sbuf_tensor("acc", [128, D], F32))
            xsq3 = CE(nc.sbuf_tensor("xsq3", [128, D], BF16))
            outf = CE(nc.sbuf_tensor("outf", [128, 2, D], F32))

            d_z3 = Dep()
            d_g = Dep()
            d_z2c = Dep()
            d_x2c = Dep()
            d_acc = Dep()
            d_xsq3 = Dep()
            d_outf = [Dep(), Dep()]

            for gi in range(NTL):
                EM("sync", "qS", lambda e, gi=gi: e.dma_start(
                    out=z2c[:], in_=z2_d[gi * 128:(gi + 1) * 128, :]),
                    reads=[dp["z2d"]], writes=[d_z2c])
                EM("vector", "vC", lambda e, gi=gi: e.tensor_scalar(
                    out=x2c[:], in0=z2c[:], scalar1=rstd2_t[:, gi:gi + 1],
                    scalar2=None, op0=ALU.mult),
                    reads=[d_z2c, dp["rstd2"]], writes=[d_x2c])
                for j in range(2):
                    mj = m1x if j == 0 else m2x
                    EM("vector", "vC", lambda e, gi=gi, j=j, mj=mj: e.tensor_scalar(
                        out=mj[:], in0=io8_b[:], scalar1=es_t[:, gi, j:j + 1], scalar2=None,
                        op0=ALU.is_equal), reads=[dp["consts"], dp["route"]], writes=[dp["m12"]])
                    EM("vector", "vC", lambda e, j=j, mj=mj: e.scalar_tensor_tensor(
                        out=prb[:], in0=cpx_b[:], scalar=1.0, in1=mj[:],
                        op0=ALU.mult, op1=ALU.mult, accum_out=msc[:, j:j + 1]),
                        reads=[dp["cpx"], dp["m12"]], writes=[dp["prb"], dp["msc"]])
                    EM("vector", "vC", lambda e, gi=gi, j=j: e.tensor_tensor(
                        out=msc[:, 2 + j:3 + j], in0=rs_t[:, gi, j:j + 1], in1=msc[:, j:j + 1],
                        op=ALU.add), reads=[dp["route"], dp["msc"]], writes=[dp["msc"]])
                    EM("vector", "vC", lambda e, j=j: e.tensor_scalar(
                        out=msc[:, 4 + j:5 + j], in0=msc[:, 2 + j:3 + j],
                        scalar1=CAP - 0.5, scalar2=None, op0=ALU.is_lt),
                        reads=[dp["msc"]], writes=[dp["msc"]])
                    EM("vector", "vC", lambda e, gi=gi, j=j: e.tensor_tensor(
                        out=msc[:, 6 + j:7 + j], in0=ws_t[:, gi, j:j + 1],
                        in1=msc[:, 4 + j:5 + j], op=ALU.mult),
                        reads=[dp["route"], dp["msc"]], writes=[dp["msc"]])
                    EM("vector", "vC", lambda e, gi=gi, j=j: e.tensor_scalar(
                        out=msc[:, 8 + j:9 + j], in0=rs_t[:, gi, j:j + 1],
                        scalar1=float(PEXP - 1), scalar2=None, op0=ALU.min),
                        reads=[dp["route"]], writes=[dp["msc"]])
                    EM("vector", "vC", lambda e, gi=gi, j=j: e.scalar_tensor_tensor(
                        out=msc[:, 10 + j:11 + j], in0=es_t[:, gi, j:j + 1],
                        scalar=float(PEXP), in1=msc[:, 8 + j:9 + j],
                        op0=ALU.mult, op1=ALU.add),
                        reads=[dp["route"], dp["msc"]], writes=[dp["msc"]])
                    EM("vector", "vC", lambda e, j=j: e.tensor_copy(
                        out=idxu[:, j:j + 1], in_=msc[:, 10 + j:11 + j]),
                        reads=[dp["msc"]], writes=[dp["idxu"]])
                    EM("gpsimd", "qG", lambda e, j=j: e.indirect_dma_start(
                        out=(g1 if j == 0 else g2)[:], out_offset=None,
                        in_=ybuf_d, in_offset=IndirectOffsetOnAxis(idxu[:, j:j + 1], 0)),
                        reads=[dp["idxu"], dp["ybufd"]], writes=[d_g])
                EM("vector", "vC", lambda e: e.scalar_tensor_tensor(
                    out=acc[:], in0=g1[:], scalar=msc[:, 6:7], in1=x2c[:],
                    op0=ALU.mult, op1=ALU.add),
                    reads=[d_g, dp["msc"], d_x2c], writes=[d_acc])
                EM("vector", "vC", lambda e: e.scalar_tensor_tensor(
                    out=acc[:], in0=g2[:], scalar=msc[:, 7:8], in1=acc[:],
                    op0=ALU.mult, op1=ALU.add),
                    reads=[d_g, dp["msc"], d_acc], writes=[d_acc])
                EM("vector", "vC", lambda e: e.tensor_tensor(
                    out=acc[:], in0=acc[:], in1=fb_b[:], op=ALU.add),
                    reads=[dp["consts"], d_acc], writes=[d_acc])
                EM("vector", "vC", lambda e: e.tensor_reduce(
                    out=msc[:, 12:13], in_=acc[:], axis=AX.X, op=ALU.add),
                    reads=[d_acc], writes=[dp["msc"]])
                EM("vector", "vC", lambda e: e.tensor_scalar(
                    out=msc[:, 12:13], in0=msc[:, 12:13], scalar1=1.0 / D,
                    scalar2=None, op0=ALU.mult), reads=[dp["msc"]], writes=[dp["msc"]])
                EM("scalar", "sC", lambda e, gi=gi: e.activation(
                    out=xsq3[:], in_=acc[:], func=AF.Square,
                    accum_out=var3_t[:, gi:gi + 1]),
                    reads=[d_acc], writes=[d_xsq3, dp["var3"]])
                EM("vector", "vC", lambda e: e.tensor_tensor(
                    out=msc[:, 13:14], in0=msc[:, 12:13], in1=msc[:, 12:13], op=ALU.mult),
                    reads=[dp["msc"]], writes=[dp["msc"]])
                EM("vector", "vC", lambda e, gi=gi: e.scalar_tensor_tensor(
                    out=var3_t[:, gi:gi + 1], in0=var3_t[:, gi:gi + 1], scalar=1.0 / D,
                    in1=msc[:, 13:14], op0=ALU.mult, op1=ALU.subtract),
                    reads=[dp["msc"], dp["var3"]], writes=[dp["var3"]])
                EM("vector", "vC", lambda e, gi=gi: e.tensor_scalar(
                    out=var3_t[:, gi:gi + 1], in0=var3_t[:, gi:gi + 1],
                    scalar1=LN_EPS, scalar2=None, op0=ALU.add),
                    reads=[dp["var3"]], writes=[dp["var3"]])
                EM("vector", "vC", lambda e, gi=gi: e.tensor_scalar(
                    out=z3[:, gi, :], in0=acc[:], scalar1=msc[:, 12:13],
                    scalar2=None, op0=ALU.subtract),
                    reads=[d_acc, dp["msc"]], writes=[d_z3])

            EM("scalar", "sC", lambda e: e.activation(out=rstd3_t[:], in_=var3_t[:], func=AF.Sqrt),
               reads=[dp["var3"]], writes=[dp["rstd3"]])
            EM("vector", "vC", lambda e: e.reciprocal(out=rstd3_t[:], in_=rstd3_t[:]),
               reads=[dp["rstd3"]], writes=[dp["rstd3"]])

            for gi in range(NTL):
                b, i = gi // NT, gi % NT
                op_ = gi % 2
                EM("vector", "vC", lambda e, gi=gi, op_=op_: e.tensor_scalar(
                    out=outf[:, op_, :], in0=z3[:, gi, :], scalar1=rstd3_t[:, gi:gi + 1],
                    scalar2=None, op0=ALU.mult),
                    reads=[d_z3, dp["rstd3"]], writes=[d_outf[op_]])
                EM("sync", "qS", lambda e, b=b, i=i, op_=op_: e.dma_start(
                    out=out_t[b, i * 128:(i + 1) * 128, :], in_=outf[:, op_, :]),
                    reads=[d_outf[op_]], writes=[dp["outd"]])

            snap = [(c.sem, c.val) for c in ctrs.values()]
            for pl in pools.values():
                snap += [(c.sem, c.val) for c in pl.subs]
            for ename in ("sync", "gpsimd", "vector", "scalar", "tensor"):
                self_q = bld.queues.setdefault(ename, [])
            bld.flush()
            for ename in ("sync", "gpsimd"):
                def fini(eng, snap=snap):
                    for (s, v) in snap:
                        eng.wait_ge(s, v)
                getattr(block, ename)(fini)

    return nc


def kernel(L0_attn_x, L0_attn_Wq, L0_attn_Wk, L0_attn_Wv, L0_attn_gq, L0_attn_gk,
           L0_attn_Wo, L0_attn_bo, L0_attn_ln_w, L0_attn_ln_b,
           L0_ffn_w_router, L0_ffn_w1, L0_ffn_w2, L0_ffn_bias):
    x = np.ascontiguousarray(np.asarray(L0_attn_x, dtype=np.float32))
    ident = np.eye(128, dtype=ml_dtypes.bfloat16)
    utri = np.triu(np.ones((128, 128), np.float32), 1).astype(ml_dtypes.bfloat16)
    iota8 = np.arange(E, dtype=np.float32).reshape(1, E)

    shared = {
        "wq_in": np.ascontiguousarray(np.asarray(L0_attn_Wq, np.float32)),
        "wk_in": np.ascontiguousarray(np.asarray(L0_attn_Wk, np.float32)),
        "wv_in": np.ascontiguousarray(np.asarray(L0_attn_Wv, np.float32)),
        "wo_in": np.ascontiguousarray(np.asarray(L0_attn_Wo, np.float32)),
        "gq_in": np.asarray(L0_attn_gq, np.float32).reshape(1, DH),
        "gk_in": np.asarray(L0_attn_gk, np.float32).reshape(1, DH),
        "bo_in": np.asarray(L0_attn_bo, np.float32).reshape(1, D),
        "lnw_in": np.asarray(L0_attn_ln_w, np.float32).reshape(D, 1),
        "lnb_in": np.asarray(L0_attn_ln_b, np.float32).reshape(D, 1),
        "wr_in": np.ascontiguousarray(np.asarray(L0_ffn_w_router, np.float32)),
        "w1_in": np.ascontiguousarray(np.asarray(L0_ffn_w1, np.float32)),
        "w2_in": np.ascontiguousarray(np.asarray(L0_ffn_w2, np.float32)),
        "fb_in": np.asarray(L0_ffn_bias, np.float32).reshape(1, D),
        "ident_in": ident,
        "utri_in": utri,
        "iota8_in": iota8,
        "e2map_in": np.kron(np.eye(2, dtype=np.float32), np.ones((1, 64), np.float32)).astype(ml_dtypes.bfloat16),
    }

    nc = build_nc()
    in_maps = []
    for c in range(NC):
        m = dict(shared)
        m["x_in"] = np.ascontiguousarray(x[c * BPC:(c + 1) * BPC])
        rm = np.zeros((1, 128), np.float32)
        rm[0, c] = 1.0
        m["rowmask_in"] = rm
        sc = np.zeros((128, 1), np.float32)
        sc[:c, 0] = 1.0
        m["selcol_in"] = sc
        in_maps.append(m)

    res = run_bass_kernel_spmd(nc, in_maps, core_ids=list(range(NC)))
    out = np.concatenate([np.asarray(r["out_t"]) for r in res.results], axis=0)
    return out.astype(np.float32)


# version 10 (final)
# speedup vs baseline: 1.2335x; 1.1526x over previous
import numpy as np
import ml_dtypes
from contextlib import ExitStack

import concourse.bass as bass
import concourse.mybir as mybir
from concourse.bass import IndirectOffsetOnAxis
from concourse.bass_utils import run_bass_kernel_spmd

F32 = mybir.dt.float32
BF16 = mybir.dt.bfloat16
U32 = mybir.dt.uint32
AF = mybir.ActivationFunctionType
ALU = mybir.AluOpType
AX = mybir.AxisListType

NC = 8
B, T, D, H, DH = 16, 4096, 1024, 16, 64
NPAIR = 8
BPC = B // NC
NT = T // 128
NLOC = BPC * T
NTL = NLOC // 128
E, FF = 8, 4096
CAP = 2 * B * T // E
PEXP = 2304
RH = PEXP // 2
NRH = RH // 128
THRESH, SHARP = 0.29514, 15.0
LN_EPS, NORM_EPS = 1e-5, 1e-12
KT = D // 128


class Ctr:
    def __init__(self, sem, step=1):
        self.sem, self.val, self.step = sem, 0, step

    def inc(self, instr, n=None):
        n = self.step if n is None else n
        instr.then_inc(self.sem, n)
        self.val += n
        return instr


class Dep:
    __slots__ = ("w", "r")
    def __init__(self):
        self.w, self.r = None, []


class DmaPool:
    def __init__(self, subs):
        self.subs = subs
        self.i = 0

    def next(self):
        c = self.subs[self.i % len(self.subs)]
        self.i += 1
        return c


class Builder:
    def __init__(self, block, ctrs, pools):
        self.block = block
        self.ctrs = ctrs
        self.pools = pools
        self.seen = {}
        self.queues = {}
        self.pending = {}

    def barrier(self):
        snap = [(c, c.val) for c in self.ctrs.values()]
        for pl in self.pools.values():
            snap += [(c, c.val) for c in pl.subs]
        for ename in ("sync", "scalar", "vector", "tensor", "gpsimd"):
            self.pending[ename] = list(snap)

    def _want(self, ename, ctr, val, waits, own):
        if val <= 0:
            return
        if ename == "tensor" and own is not None and ctr is own:
            return
        key = id(ctr.sem)
        seen = self.seen.setdefault(ename, {})
        if seen.get(key, -1) >= val:
            return
        seen[key] = val
        waits.append((ctr.sem, val))

    def emit(self, ename, cname, fn, reads=(), writes=()):
        if cname in self.pools:
            ctr = self.pools[cname].next()
        else:
            ctr = self.ctrs[cname]
        waits = []
        for (c, v) in self.pending.pop(ename, ()):
            self._want(ename, c, v, waits, None)
        if ctr.step == 16:
            self._want(ename, ctr, ctr.val, waits, None)
        for d in reads:
            if d.w is not None:
                self._want(ename, d.w[0], d.w[1], waits, ctr)
        for d in writes:
            if d.w is not None:
                self._want(ename, d.w[0], d.w[1], waits, ctr)
            for (c, v) in d.r:
                self._want(ename, c, v, waits, ctr)
        n = ctr.step
        self.queues.setdefault(ename, []).append((waits, fn, ctr.sem, n))
        ctr.val += n
        for d in reads:
            d.r.append((ctr, ctr.val))
        for d in writes:
            d.w = (ctr, ctr.val)
            d.r = []

    def flush(self):
        for ename, q in self.queues.items():
            def body(eng, q=q):
                for (waits, fn, sem, n) in q:
                    for (s, v) in waits:
                        eng.wait_ge(s, v)
                    fn(eng).then_inc(sem, n)
            getattr(self.block, ename)(body)
        self.queues = {}


def build_nc():
    nc = bass.Bass("TRN2", target_bir_lowering=False)

    x_in = nc.dram_tensor("x_in", [BPC, T, D], F32, kind="ExternalInput").ap()
    wq_in = nc.dram_tensor("wq_in", [D, D], F32, kind="ExternalInput").ap()
    wk_in = nc.dram_tensor("wk_in", [D, D], F32, kind="ExternalInput").ap()
    wv_in = nc.dram_tensor("wv_in", [D, D], F32, kind="ExternalInput").ap()
    wo_in = nc.dram_tensor("wo_in", [D, D], F32, kind="ExternalInput").ap()
    gq_in = nc.dram_tensor("gq_in", [1, DH], F32, kind="ExternalInput").ap()
    gk_in = nc.dram_tensor("gk_in", [1, DH], F32, kind="ExternalInput").ap()
    bo_in = nc.dram_tensor("bo_in", [1, D], F32, kind="ExternalInput").ap()
    lnw_in = nc.dram_tensor("lnw_in", [D, 1], F32, kind="ExternalInput").ap()
    lnb_in = nc.dram_tensor("lnb_in", [D, 1], F32, kind="ExternalInput").ap()
    wr_in = nc.dram_tensor("wr_in", [D, E], F32, kind="ExternalInput").ap()
    w1_in = nc.dram_tensor("w1_in", [E, D, FF], F32, kind="ExternalInput").ap()
    w2_in = nc.dram_tensor("w2_in", [E, FF, D], F32, kind="ExternalInput").ap()
    fb_in = nc.dram_tensor("fb_in", [1, D], F32, kind="ExternalInput").ap()
    ident_in = nc.dram_tensor("ident_in", [128, 128], BF16, kind="ExternalInput").ap()
    utri_in = nc.dram_tensor("utri_in", [128, 128], BF16, kind="ExternalInput").ap()
    iota8_in = nc.dram_tensor("iota8_in", [1, E], F32, kind="ExternalInput").ap()
    rowmask_in = nc.dram_tensor("rowmask_in", [1, 128], F32, kind="ExternalInput").ap()
    selcol_in = nc.dram_tensor("selcol_in", [128, 1], F32, kind="ExternalInput").ap()
    e2map_in = nc.dram_tensor("e2map_in", [2, 128], BF16, kind="ExternalInput").ap()

    out_t = nc.dram_tensor("out_t", [BPC, T, D], F32, kind="ExternalOutput").ap()

    wqkv_d = nc.dram_tensor("wqkv_d", [KT, 128, NPAIR, 384], BF16, kind="Internal").ap()
    collT_d = nc.dram_tensor("collT_d", [D, T], BF16, kind="Internal").ap()
    z2_d = nc.dram_tensor("z2_d", [NLOC, D], BF16, kind="Internal").ap()
    xdisp_d = nc.dram_tensor("xdisp_d", [E * PEXP, D], BF16, kind="Internal").ap()
    ybuf_d = nc.dram_tensor("ybuf_d", [E * PEXP, D], F32, kind="Internal").ap()
    ccs_d = nc.dram_tensor("ccs_d", [128, 128], F32, kind="Internal").ap()
    ccd_d = nc.dram_tensor("ccd_d", [128, 128], F32, kind="Internal").ap()

    with ExitStack() as top:
        TE = top.enter_context

        KPOOL = 8
        ctrs = {
            "tC": Ctr(TE(nc.semaphore("tC"))),
            "vC": Ctr(TE(nc.semaphore("vC"))),
            "sC": Ctr(TE(nc.semaphore("sC"))),
            "cc": Ctr(TE(nc.semaphore("ccsm"))),
        }
        pools = {
            q: DmaPool([Ctr(TE(nc.semaphore(f"{q}_{j}")), 16) for j in range(KPOOL)])
            for q in ("qS", "qA", "qG")
        }

        cq_b = TE(nc.sbuf_tensor("cq_b", [128, D], BF16))
        ck_b = TE(nc.sbuf_tensor("ck_b", [128, D], BF16))
        sv_b = TE(nc.sbuf_tensor("sv_b", [128, D], BF16))
        bo_b = TE(nc.sbuf_tensor("bo_b", [128, D], BF16))
        fb_b = TE(nc.sbuf_tensor("fb_b", [128, D], BF16))
        gq_b = TE(nc.sbuf_tensor("gq_b", [128, 128], F32))
        gk_b = TE(nc.sbuf_tensor("gk_b", [128, 128], F32))
        io8_b = TE(nc.sbuf_tensor("io8_b", [128, E], F32))
        ident = TE(nc.sbuf_tensor("ident", [128, 128], BF16))
        utri = TE(nc.sbuf_tensor("utri", [128, 128], BF16))
        ones_f = TE(nc.sbuf_tensor("ones_f", [1, 128], F32))
        ones_bf = TE(nc.sbuf_tensor("ones_bf", [1, 128], BF16))
        ones_bcol = TE(nc.sbuf_tensor("ones_bcol", [128, 1], BF16))
        selcol = TE(nc.sbuf_tensor("selcol", [128, 1], F32))
        rowmask = TE(nc.sbuf_tensor("rowmask", [1, 128], F32))
        sigb = TE(nc.sbuf_tensor("sigb", [128, 1], F32))
        lnw_s = TE(nc.sbuf_tensor("lnw_s", [128, KT], F32))
        lnb_s = TE(nc.sbuf_tensor("lnb_s", [128, KT], BF16))
        wr_s = TE(nc.sbuf_tensor("wr_s", [128, KT, E], BF16))
        es_t = TE(nc.sbuf_tensor("es_t", [128, NTL, 2], F32))
        ws_t = TE(nc.sbuf_tensor("ws_t", [128, NTL, 2], F32))
        rs_t = TE(nc.sbuf_tensor("rs_t", [128, NTL, 2], F32))
        run_row = TE(nc.sbuf_tensor("run_row", [1, E], F32))
        runb_s = TE(nc.sbuf_tensor("runb_s", [128, E], F32))
        cpx_b = TE(nc.sbuf_tensor("cpx_b", [128, E], F32))
        var2_t = TE(nc.sbuf_tensor("var2_t", [128, NTL], F32))
        rstd2_t = TE(nc.sbuf_tensor("rstd2_t", [128, NTL], F32))
        var3_t = TE(nc.sbuf_tensor("var3_t", [128, NTL], F32))
        rstd3_t = TE(nc.sbuf_tensor("rstd3_t", [128, NTL], F32))
        lgz_t = TE(nc.sbuf_tensor("lgz_t", [128, NTL, E], F32))
        msc = TE(nc.sbuf_tensor("msc", [128, 16], F32))
        m1x = TE(nc.sbuf_tensor("m1x", [128, E], F32))
        m2x = TE(nc.sbuf_tensor("m2x", [128, E], F32))
        cab = TE(nc.sbuf_tensor("cab", [128, E], F32))
        prb = TE(nc.sbuf_tensor("prb", [128, E], F32))
        top8 = TE(nc.sbuf_tensor("top8", [128, 8], F32))
        idx8 = TE(nc.sbuf_tensor("idx8", [128, 8], U32))
        idxu = TE(nc.sbuf_tensor("idxu", [128, 2], U32))
        mm8 = TE(nc.sbuf_tensor("mm8", [128, E], BF16))
        cnt128 = TE(nc.sbuf_tensor("cnt128", [128, 128], F32))

        dp = {n: Dep() for n in [
            "consts", "lnw", "wr", "route", "run", "runb", "cpx",
            "var2", "rstd2", "var3", "rstd3", "lgz",
            "wqkvd", "collTd", "z2d", "xdispd", "ybufd", "ccsd", "ccdd", "outd",
            "msc", "m12", "cab", "prb", "top", "idxu", "mm8", "cnt"]}

        block = TE(nc.Block())
        bld = Builder(block, ctrs, pools)
        EM = bld.emit

        def bcast_row_f32(dst, d_dst, row_ap, d_row, ps, d_ps, width=D):
            nch = (width + 511) // 512
            for ch in range(nch):
                w = min(512, width - ch * 512)
                sl = slice(ch * 512, ch * 512 + w)
                EM("tensor", "tC", lambda e, sl=sl, w=w: e.matmul(
                    ps[:, 0:w], ones_f[:], row_ap[:, sl], start=True, stop=True),
                    reads=[dp["consts"], d_row], writes=[d_ps])
                EM("vector", "vC", lambda e, sl=sl, w=w: e.tensor_copy(
                    out=dst[:, sl], in_=ps[:, 0:w]), reads=[d_ps], writes=[d_dst])

        with ExitStack() as pre:
            PP = pre.enter_context
            stage = PP(nc.sbuf_tensor("stage", [128, KT, D], BF16))
            row_s = PP(nc.sbuf_tensor("row_s", [1, D], F32))
            zro = PP(nc.sbuf_tensor("zro", [128, 8192], BF16))
            ps_p = PP(nc.psum_tensor("ps_p", [128, 512], F32))
            ps_q = PP(nc.psum_tensor("ps_q", [1, 512], F32))
            d_stage, d_row, d_psp, d_psq, d_zro = Dep(), Dep(), Dep(), Dep(), Dep()

            EM("sync", "qS", lambda e: e.dma_start(out=ident[:], in_=ident_in), writes=[dp["consts"]])
            EM("sync", "qS", lambda e: e.dma_start(out=utri[:], in_=utri_in), writes=[dp["consts"]])
            EM("sync", "qS", lambda e: e.dma_start(out=selcol[:], in_=selcol_in), writes=[dp["consts"]])
            EM("sync", "qS", lambda e: e.dma_start(out=rowmask[:], in_=rowmask_in), writes=[dp["consts"]])
            EM("sync", "qS", lambda e: e.dma_start(out=gq_b[0:1, 0:DH], in_=gq_in), writes=[dp["consts"]])
            EM("sync", "qS", lambda e: e.dma_start(out=gq_b[0:1, DH:128], in_=gq_in), writes=[dp["consts"]])
            EM("sync", "qS", lambda e: e.dma_start(out=gk_b[0:1, 0:DH], in_=gk_in), writes=[dp["consts"]])
            EM("sync", "qS", lambda e: e.dma_start(out=gk_b[0:1, DH:128], in_=gk_in), writes=[dp["consts"]])
            EM("sync", "qS", lambda e: e.dma_start(out=io8_b[0:1, :], in_=iota8_in), writes=[dp["consts"]])
            for kt in range(KT):
                EM("sync", "qS", lambda e, kt=kt: e.dma_start(
                    out=lnw_s[:, kt:kt + 1], in_=lnw_in[kt * 128:(kt + 1) * 128, :]), writes=[dp["lnw"]])
                EM("gpsimd", "qG", lambda e, kt=kt: e.dma_start(
                    out=lnb_s[:, kt:kt + 1], in_=lnb_in[kt * 128:(kt + 1) * 128, :]), writes=[dp["lnw"]])
                EM("gpsimd", "qG", lambda e, kt=kt: e.dma_start(
                    out=wr_s[:, kt, :], in_=wr_in[kt * 128:(kt + 1) * 128, :]), writes=[dp["wr"]])

            EM("vector", "vC", lambda e: e.memset(ones_f[:], 1.0), writes=[dp["consts"]])
            EM("vector", "vC", lambda e: e.memset(ones_bf[:], 1.0), writes=[dp["consts"]])
            EM("vector", "vC", lambda e: e.memset(ones_bcol[:], 1.0), writes=[dp["consts"]])
            EM("vector", "vC", lambda e: e.memset(sigb[:], -SHARP * THRESH), writes=[dp["consts"]])
            EM("vector", "vC", lambda e: e.memset(run_row[:], 0.0), writes=[dp["run"]])
            EM("vector", "vC", lambda e: e.memset(cnt128[:], 0.0), writes=[dp["cnt"]])
            EM("vector", "vC", lambda e: e.memset(zro[:], 0.0), writes=[d_zro])

            EM("sync", "qS", lambda e: e.dma_start(out=row_s[:], in_=bo_in), writes=[d_row])
            bcast_row_f32(bo_b, dp["consts"], row_s, d_row, ps_p, d_psp)
            EM("sync", "qS", lambda e: e.dma_start(out=row_s[:], in_=fb_in), writes=[d_row])
            bcast_row_f32(fb_b, dp["consts"], row_s, d_row, ps_p, d_psp)

            for (gb, w) in ((gq_b, 128), (gk_b, 128), (io8_b, E)):
                EM("tensor", "tC", lambda e, gb=gb, w=w: e.matmul(
                    ps_p[:, 0:w], ones_f[:], gb[0:1, 0:w], start=True, stop=True),
                    reads=[dp["consts"]], writes=[d_psp])
                EM("vector", "vC", lambda e, gb=gb, w=w: e.tensor_copy(
                    out=gb[:, 0:w], in_=ps_p[:, 0:w]), reads=[d_psp], writes=[dp["consts"]])

            xd_r = xdisp_d.rearrange("(n p) d -> n p d", p=128)
            for i in range(18):
                EM("gpsimd", "qG", lambda e, i=i: e.dma_start(
                    out=xd_r[8 * i:8 * (i + 1)].rearrange("n p d -> p n d"),
                    in_=zro[:].rearrange("p (n x) -> p n x", x=1024)),
                    reads=[d_zro], writes=[dp["xdispd"]])

            def stage_weight(w_src, off, fold, row_kind, dst_b):
                for kt in range(KT):
                    EM("gpsimd", "qG", lambda e, kt=kt: e.dma_start(
                        out=stage[:, kt, :], in_=w_src[kt * 128:(kt + 1) * 128, :]), writes=[d_stage])
                if row_kind is not None:
                    for ch in range(2):
                        sl = slice(ch * 512, (ch + 1) * 512)
                        for kt in range(KT):
                            lhs = (lnb_s[:, kt:kt + 1] if row_kind == "lnb" else ones_bcol[:])
                            EM("tensor", "tC", lambda e, kt=kt, sl=sl, lhs=lhs: e.matmul(
                                ps_q[:, 0:512], lhs, stage[:, kt, sl],
                                start=(kt == 0), stop=(kt == KT - 1)),
                                reads=[d_stage, dp["lnw"], dp["consts"]], writes=[d_psq])
                        EM("vector", "vC", lambda e, sl=sl: e.tensor_copy(
                            out=row_s[0:1, sl], in_=ps_q[0:1, 0:512]),
                            reads=[d_psq], writes=[d_row])
                    bcast_row_f32(dst_b, dp["consts"], row_s, d_row, ps_p, d_psp)
                if fold:
                    for kt in range(KT):
                        EM("vector", "vC", lambda e, kt=kt: e.tensor_scalar(
                            out=stage[:, kt, :], in0=stage[:, kt, :],
                            scalar1=lnw_s[:, kt:kt + 1], scalar2=None, op0=ALU.mult),
                            reads=[dp["lnw"], d_stage], writes=[d_stage])
                for kt in range(KT):
                    EM("gpsimd", "qG", lambda e, kt=kt, off=off: e.dma_start(
                        out=wqkv_d[kt, :, :, off:off + 128],
                        in_=stage[:, kt, :].rearrange("p (pr c) -> p pr c", c=128)),
                        reads=[d_stage], writes=[dp["wqkvd"]])

            stage_weight(wq_in, 0, True, "lnb", cq_b)
            stage_weight(wk_in, 128, True, "lnb", ck_b)
            stage_weight(wv_in, 256, False, "ones", sv_b)

        bld.barrier()
        with ExitStack() as att:
            AE = att.enter_context
            zT = AE(nc.sbuf_tensor("zT", [128, KT, T], BF16))
            qkv = AE(nc.sbuf_tensor("qkv", [128, NT, 384], BF16))
            vbuf = AE(nc.sbuf_tensor("vbuf", [128, 2, NT, 128], BF16))
            qnT = AE(nc.sbuf_tensor("qnT", [128, 2, T], BF16))
            knT = AE(nc.sbuf_tensor("knT", [128, 2, T], BF16))
            modT = AE(nc.sbuf_tensor("modT", [128, 3, 1024], BF16))
            sc8k = AE(nc.sbuf_tensor("sc8k", [128, 4096], BF16))
            gqbc = AE(nc.sbuf_tensor("gqbc", [128, T], BF16))
            xa2 = AE(nc.sbuf_tensor("xa2", [128, 2, D], F32))
            wpair = AE(nc.sbuf_tensor("wpair", [128, 2, KT, 384], BF16))
            mu_t = AE(nc.sbuf_tensor("mu_t", [128, NT], F32))
            rstd_t = AE(nc.sbuf_tensor("rstd_t", [128, NT], F32))
            var_t = AE(nc.sbuf_tensor("var_t", [128, NT], F32))
            ssq = AE(nc.sbuf_tensor("ssq", [128, NT, 2], F32))
            ssk = AE(nc.sbuf_tensor("ssk", [128, NT, 2], F32))
            gqv = AE(nc.sbuf_tensor("gqv", [128, NT, 2], F32))
            gkv = AE(nc.sbuf_tensor("gkv", [128, NT, 2], F32))
            gq16 = AE(nc.sbuf_tensor("gq16", [128, NT, 2], BF16))
            gqT = AE(nc.sbuf_tensor("gqT", [2, T], BF16))
            e2map = AE(nc.sbuf_tensor("e2map", [2, 128], BF16))

            ps_al = AE(nc.psum_tensor("ps_al", [128, 2, 1024], F32))
            ps_co = AE(nc.psum_tensor("ps_co", [128, 2, 512], F32))
            ps_pj = AE(nc.psum_tensor("ps_pj", [128, 512], F32))
            ps_tx = AE(nc.psum_tensor("ps_tx", [128, 128], BF16))

            d = {n: Dep() for n in [
                "zT", "mu", "qkv", "sc8k", "gqbc", "ss", "gv", "gqT", "e2",
                "psproj", "pstr"]}
            d_vbuf = [Dep(), Dep()]
            d_qnT = [Dep(), Dep()]
            d_knT = [Dep(), Dep()]
            d_modT = [Dep(), Dep(), Dep()]
            d_xa = [Dep(), Dep()]
            d_wpair = [Dep(), Dep()]
            d_psal = [Dep(), Dep(), Dep()]
            _dpsco = Dep()
            d_psco = [_dpsco, _dpsco]
            d_pspj = Dep()
            d_pstx = Dep()
            import os as _os
            if _os.environ.get("KPS", "new") == "new":
                tr_slot = [(ps_tx[:, :], d_pstx), (ps_pj[:, 0:64].bitcast(BF16), d_pspj)]
            else:
                tr_slot = [(ps_co[:, 0, 0:64].bitcast(BF16), d_psco[0]),
                           (ps_co[:, 1, 0:64].bitcast(BF16), d_psco[1])]

            EM("sync", "qS", lambda e: e.dma_start(out=e2map[:], in_=e2map_in), writes=[d["e2"]])
            import os as _os
            for _b in range(2 if _os.environ.get("KSIMPRETOUCH") == "1" else 0):
                EM("vector", "vC", lambda e, _b=_b: e.memset(ps_co[:, _b, :], 0.0),
                   writes=[d_psco[_b]])
                for _h in range(2):
                    EM("vector", "vC", lambda e, _b=_b, _h=_h: e.memset(
                        ps_al[:, _b, _h * 512:(_h + 1) * 512], 0.0), writes=[d_psal[_b]])
            if _os.environ.get("KSIMPRETOUCH") == "1":
                EM("vector", "vC", lambda e: e.memset(ps_pj[:], 0.0), writes=[d_pspj])
                EM("vector", "vC", lambda e: e.memset(ps_tx[:], 0.0), writes=[d_pstx])

            def phaseA(b):
                ztile = qkv[:].rearrange("p a c -> p (a c)")[:, 0:1024]
                for i in range(NT):
                    pp = i % 2
                    EM("sync", "qS", lambda e, i=i, pp=pp: e.dma_start(
                        out=xa2[:, pp, :], in_=x_in[b, i * 128:(i + 1) * 128, :]),
                        writes=[d_xa[pp]])
                    EM("vector", "vC", lambda e, i=i, pp=pp: e.tensor_reduce(
                        out=mu_t[:, i:i + 1], in_=xa2[:, pp, :], axis=AX.X, op=ALU.add),
                        reads=[d_xa[pp]], writes=[d["mu"]])
                    EM("scalar", "sC", lambda e, i=i, pp=pp: e.activation(
                        out=gqbc[:, 0:1024], in_=xa2[:, pp, :], func=AF.Square,
                        accum_out=var_t[:, i:i + 1]),
                        reads=[d_xa[pp]], writes=[d["gqbc"], d["mu"]])
                    EM("vector", "vC", lambda e, i=i: e.tensor_scalar(
                        out=mu_t[:, i:i + 1], in0=mu_t[:, i:i + 1], scalar1=1.0 / D,
                        scalar2=None, op0=ALU.mult), reads=[d["mu"]], writes=[d["mu"]])
                    EM("vector", "vC", lambda e, i=i: e.tensor_tensor(
                        out=msc[:, 0:1], in0=mu_t[:, i:i + 1], in1=mu_t[:, i:i + 1], op=ALU.mult),
                        reads=[d["mu"]], writes=[dp["msc"]])
                    EM("vector", "vC", lambda e, i=i: e.scalar_tensor_tensor(
                        out=var_t[:, i:i + 1], in0=var_t[:, i:i + 1], scalar=1.0 / D,
                        in1=msc[:, 0:1], op0=ALU.mult, op1=ALU.subtract),
                        reads=[d["mu"], dp["msc"]], writes=[d["mu"]])
                    EM("vector", "vC", lambda e, i=i: e.tensor_scalar(
                        out=var_t[:, i:i + 1], in0=var_t[:, i:i + 1], scalar1=LN_EPS,
                        scalar2=None, op0=ALU.add), reads=[d["mu"]], writes=[d["mu"]])
                    EM("vector", "vC", lambda e, i=i, pp=pp: e.tensor_scalar(
                        out=ztile, in0=xa2[:, pp, :], scalar1=mu_t[:, i:i + 1],
                        scalar2=None, op0=ALU.subtract),
                        reads=[d_xa[pp], d["mu"]], writes=[d["qkv"]])
                    for kt in range(KT):
                        trv, dtr = tr_slot[kt % 2]
                        EM("tensor", "tC", lambda e, kt=kt, trv=trv: e.transpose(
                            trv, ztile[:, kt * 128:(kt + 1) * 128], ident[:]),
                            reads=[d["qkv"], dp["consts"]], writes=[dtr])
                        EM("vector", "vC", lambda e, kt=kt, i=i, trv=trv: e.tensor_copy(
                            out=zT[:, kt, i * 128:(i + 1) * 128], in_=trv),
                            reads=[dtr], writes=[d["zT"]])
                EM("scalar", "sC", lambda e: e.activation(out=rstd_t[:], in_=var_t[:], func=AF.Sqrt),
                   reads=[d["mu"]], writes=[d["mu"]])
                EM("vector", "vC", lambda e: e.reciprocal(out=rstd_t[:], in_=rstd_t[:]),
                   reads=[d["mu"]], writes=[d["mu"]])

            def prep_pair(p):
                par = p % 2
                q3 = qkv
                import os as _os
                if _os.environ.get("KSST", "xa2") == "xa2":
                    sst = xa2[:].rearrange("p a c -> p (a c)").bitcast(BF16) \
                        .rearrange("p (n c) -> p n c", c=128)
                    d_sst = d_xa
                else:
                    sst = sc8k[:].rearrange("p (n c) -> p n c", c=128)
                    d_sst = [d["sc8k"]]
                yield lambda: EM("sync", "qS", lambda e: e.dma_start(
                    out=wpair[:, par],
                    in_=wqkv_d[:, :, p, :].rearrange("k p c -> p k c")),
                    reads=[dp["wqkvd"]], writes=[d_wpair[par]])
                for i in range(NT):
                    for kt in range(KT):
                        yield lambda i=i, kt=kt: EM("tensor", "tC", lambda e: e.matmul(
                            ps_pj[:, 0:384], zT[:, kt, i * 128:(i + 1) * 128],
                            wpair[:, par, kt, :], start=(kt == 0), stop=(kt == KT - 1)),
                            reads=[d["zT"], d_wpair[par]], writes=[d_pspj])
                    yield lambda i=i: EM("vector", "vC", lambda e: e.tensor_copy(
                        out=qkv[:, i, :], in_=ps_pj[:, 0:384]),
                        reads=[d_pspj], writes=[d["qkv"]])
                rstd3 = rstd_t[:].unsqueeze(2).to_broadcast([128, NT, 128])
                mu3 = mu_t[:].unsqueeze(2).to_broadcast([128, NT, 128])
                for (off, cb) in ((0, cq_b), (128, ck_b)):
                    yield lambda off=off, rstd3=rstd3: EM("vector", "vC", lambda e: e.tensor_tensor(
                        out=q3[:, :, off:off + 128], in0=q3[:, :, off:off + 128],
                        in1=rstd3, op=ALU.mult), reads=[d["qkv"], d["mu"]], writes=[d["qkv"]])
                    cb3 = cb[:, p * 128:(p + 1) * 128].unsqueeze(1).to_broadcast([128, NT, 128])
                    yield lambda off=off, cb3=cb3: EM("vector", "vC", lambda e: e.tensor_tensor(
                        out=q3[:, :, off:off + 128], in0=q3[:, :, off:off + 128],
                        in1=cb3, op=ALU.add), reads=[d["qkv"], dp["consts"]], writes=[d["qkv"]])
                sv3 = sv_b[:, p * 128:(p + 1) * 128].unsqueeze(1).to_broadcast([128, NT, 128])
                yield lambda sv3=sv3, mu3=mu3: EM("vector", "vC", lambda e: e.tensor_tensor(
                    out=sst, in0=sv3, in1=mu3, op=ALU.mult),
                    reads=[dp["consts"], d["mu"]], writes=d_sst)
                yield lambda: EM("vector", "vC", lambda e: e.tensor_tensor(
                    out=q3[:, :, 256:384], in0=q3[:, :, 256:384], in1=sst, op=ALU.add),
                    reads=d_sst + [d["qkv"]], writes=[d["qkv"]])
                for (off, ssx) in ((0, ssq), (128, ssk)):
                    yield lambda off=off: EM("vector", "vC", lambda e: e.tensor_tensor(
                        out=sst, in0=q3[:, :, off:off + 128], in1=q3[:, :, off:off + 128],
                        op=ALU.mult), reads=[d["qkv"]], writes=d_sst)
                    yield lambda ssx=ssx: EM("vector", "vC", lambda e: e.tensor_reduce(
                        out=ssx[:], in_=sst.rearrange("p n (h x) -> p n h x", x=DH),
                        axis=AX.X, op=ALU.add), reads=d_sst, writes=[d["ss"]])
                    yield lambda ssx=ssx: EM("scalar", "sC", lambda e: e.activation(
                        out=ssx[:].rearrange("p a c -> p (a c)"),
                        in_=ssx[:].rearrange("p a c -> p (a c)"), func=AF.Sqrt),
                        reads=[d["ss"]], writes=[d["ss"]])
                    yield lambda ssx=ssx: EM("vector", "vC", lambda e: e.tensor_scalar(
                        out=ssx[:].rearrange("p a c -> p (a c)"),
                        in0=ssx[:].rearrange("p a c -> p (a c)"),
                        scalar1=NORM_EPS, scalar2=None, op0=ALU.max),
                        reads=[d["ss"]], writes=[d["ss"]])
                    yield lambda ssx=ssx: EM("vector", "vC", lambda e: e.reciprocal(
                        out=ssx[:].rearrange("p a c -> p (a c)"),
                        in_=ssx[:].rearrange("p a c -> p (a c)")),
                        reads=[d["ss"]], writes=[d["ss"]])
                    rb = ssx[:].unsqueeze(3).to_broadcast([128, NT, 2, DH])
                    yield lambda off=off, rb=rb: EM("vector", "vC", lambda e: e.tensor_tensor(
                        out=q3[:, :, off:off + 128].rearrange("p n (h x) -> p n h x", x=DH),
                        in0=q3[:, :, off:off + 128].rearrange("p n (h x) -> p n h x", x=DH),
                        in1=rb, op=ALU.mult), reads=[d["qkv"], d["ss"]], writes=[d["qkv"]])
                gk4 = gk_b[:].rearrange("p (h x) -> p h x", x=DH).unsqueeze(1).to_broadcast([128, NT, 2, DH])
                gq4 = gq_b[:].rearrange("p (h x) -> p h x", x=DH).unsqueeze(1).to_broadcast([128, NT, 2, DH])
                yield lambda gk4=gk4: EM("vector", "vC", lambda e: e.tensor_tensor(
                    out=sst.rearrange("p n (h x) -> p n h x", x=DH),
                    in0=q3[:, :, 128:256].rearrange("p n (h x) -> p n h x", x=DH),
                    in1=gk4, op=ALU.mult), reads=[d["qkv"], dp["consts"]], writes=d_sst)
                yield lambda: EM("vector", "vC", lambda e: e.tensor_reduce(
                    out=gkv[:], in_=sst.rearrange("p n (h x) -> p n h x", x=DH),
                    axis=AX.X, op=ALU.add), reads=d_sst, writes=[d["gv"]])
                yield lambda gq4=gq4: EM("vector", "vC", lambda e: e.tensor_tensor(
                    out=sst.rearrange("p n (h x) -> p n h x", x=DH),
                    in0=q3[:, :, 0:128].rearrange("p n (h x) -> p n h x", x=DH),
                    in1=gq4, op=ALU.mult), reads=[d["qkv"], dp["consts"]], writes=d_sst)
                yield lambda: EM("vector", "vC", lambda e: e.tensor_reduce(
                    out=gqv[:], in_=sst.rearrange("p n (h x) -> p n h x", x=DH),
                    axis=AX.X, op=ALU.add), reads=d_sst, writes=[d["gv"]])
                gkb4 = gkv[:].unsqueeze(3).to_broadcast([128, NT, 2, DH])
                yield lambda gkb4=gkb4: EM("vector", "vC", lambda e: e.tensor_tensor(
                    out=vbuf[:, par].rearrange("p n (h x) -> p n h x", x=DH),
                    in0=q3[:, :, 256:384].rearrange("p n (h x) -> p n h x", x=DH),
                    in1=gkb4, op=ALU.mult), reads=[d["qkv"], d["gv"]], writes=[d_vbuf[par]])
                yield lambda: EM("vector", "vC", lambda e: e.tensor_copy(out=gq16[:], in_=gqv[:]),
                    reads=[d["gv"]], writes=[d["gv"]])
                for i in range(NT):
                    trv0, dtr0 = tr_slot[0]
                    trv1, dtr1 = tr_slot[1]
                    yield lambda i=i, trv0=trv0, dtr0=dtr0: EM("tensor", "tC", lambda e: e.transpose(
                        trv0, qkv[:, i, 0:128], ident[:]),
                        reads=[d["qkv"], dp["consts"]], writes=[dtr0])
                    yield lambda i=i, trv0=trv0, dtr0=dtr0: EM("vector", "vC", lambda e: e.tensor_copy(
                        out=qnT[:, par, i * 128:(i + 1) * 128], in_=trv0),
                        reads=[dtr0], writes=[d_qnT[par]])
                    yield lambda i=i, trv1=trv1, dtr1=dtr1: EM("tensor", "tC", lambda e: e.transpose(
                        trv1, qkv[:, i, 128:256], ident[:]),
                        reads=[d["qkv"], dp["consts"]], writes=[dtr1])
                    yield lambda i=i, trv1=trv1, dtr1=dtr1: EM("vector", "vC", lambda e: e.tensor_copy(
                        out=knT[:, par, i * 128:(i + 1) * 128], in_=trv1),
                        reads=[dtr1], writes=[d_knT[par]])
                    yield lambda i=i, trv0=trv0, dtr0=dtr0: EM("tensor", "tC", lambda e: e.transpose(
                        trv0[0:2, :], gq16[:, i, :], ident[:]),
                        reads=[d["gv"], dp["consts"]], writes=[dtr0])
                    yield lambda i=i, trv0=trv0, dtr0=dtr0: EM("vector", "vC", lambda e: e.tensor_copy(
                        out=gqT[0:2, i * 128:(i + 1) * 128], in_=trv0[0:2, :]),
                        reads=[dtr0], writes=[d["gqT"]])

            def prep_gqbc(p):
                for ch in range(T // 512):
                    EM("tensor", "tC", lambda e, ch=ch: e.matmul(
                        ps_pj[:, 0:512], e2map[:],
                        gqT[0:2, ch * 512:(ch + 1) * 512], start=True, stop=True),
                        reads=[d["gqT"], d["e2"]], writes=[d_pspj])
                    EM("vector", "vC", lambda e, ch=ch: e.tensor_copy(
                        out=gqbc[:, ch * 512:(ch + 1) * 512], in_=ps_pj[:, 0:512]),
                        reads=[d_pspj], writes=[d["gqbc"]])

            def core_pair(p):
                par = p % 2
                import os as _os
                lag = 0 if _os.environ.get("KLAG", "2") == "0" else 2
                for tch in range(T // 512):
                    cop = tch % 2
                    for s in range(NT + lag):
                        th = []
                        th2 = []
                        if s >= lag and s - lag < NT:
                            sc_ = s - lag
                            ac = sc_ % 3
                            th.append(lambda sc_=sc_, ac=ac, cop=cop, par=par: EM(
                                "tensor", "tC",
                                lambda e, sc_=sc_, ac=ac, cop=cop, par=par: e.matmul(
                                    ps_co[0:64, cop, :], vbuf[:, par, sc_, 0:64],
                                    modT[:, ac, 0:512], start=(sc_ == 0), stop=(sc_ == NT - 1),
                                    tile_position=(0, 0)),
                                reads=[d_vbuf[par], d_modT[ac]], writes=[d_psco[cop]]))
                            th.append(lambda sc_=sc_, ac=ac, cop=cop, par=par: EM(
                                "tensor", "tC",
                                lambda e, sc_=sc_, ac=ac, cop=cop, par=par: e.matmul(
                                    ps_co[64:128, cop, :], vbuf[:, par, sc_, 64:128],
                                    modT[:, ac, 512:1024], start=(sc_ == 0), stop=(sc_ == NT - 1),
                                    tile_position=(0, 64), skip_group_check=True),
                                reads=[d_vbuf[par], d_modT[ac]], writes=[d_psco[cop]]))
                        if lag == 0:
                            th, th2 = th2, th
                        if s < NT:
                            ap = s % 2
                            mp = s % 3
                            th.append(lambda s=s, ap=ap, tch=tch, par=par: EM(
                                "tensor", "tC",
                                lambda e, s=s, ap=ap, tch=tch, par=par: e.matmul(
                                    ps_al[:, ap, 0:512], knT[0:64, par, s * 128:(s + 1) * 128],
                                    qnT[0:64, par, tch * 512:(tch + 1) * 512],
                                    start=True, stop=True, tile_position=(0, 0)),
                                reads=[d_knT[par], d_qnT[par]], writes=[d_psal[ap]]))
                            th.append(lambda s=s, ap=ap, tch=tch, par=par: EM(
                                "tensor", "tC",
                                lambda e, s=s, ap=ap, tch=tch, par=par: e.matmul(
                                    ps_al[:, ap, 512:1024], knT[64:128, par, s * 128:(s + 1) * 128],
                                    qnT[64:128, par, tch * 512:(tch + 1) * 512],
                                    start=True, stop=True, tile_position=(64, 0)),
                                reads=[d_knT[par], d_qnT[par]], writes=[d_psal[ap]]))
                            th.append(lambda s=s, ap=ap, mp=mp: EM(
                                "scalar", "sC",
                                lambda e, ap=ap, mp=mp: e.activation(
                                    out=modT[:, mp, :], in_=ps_al[:, ap, :],
                                    func=AF.Sigmoid, scale=SHARP, bias=sigb[:]),
                                reads=[d_psal[ap], dp["consts"]], writes=[d_modT[mp]]))
                        yield th + th2
                    yield [lambda tch=tch, cop=cop: EM(
                        "vector", "vC",
                        lambda e, tch=tch, cop=cop: e.tensor_tensor(
                            out=sc8k[:, tch * 512:(tch + 1) * 512], in0=ps_co[:, cop, :],
                            in1=gqbc[:, tch * 512:(tch + 1) * 512], op=ALU.mult),
                        reads=[d_psco[cop], d["gqbc"]], writes=[d["sc8k"]])]
                yield [lambda p=p: EM("scalar", "qA", lambda e, p=p: e.dma_start(
                    out=collT_d[p * 128:(p + 1) * 128, :], in_=sc8k[:]),
                    reads=[d["sc8k"]], writes=[dp["collTd"]])]

            def phaseB(b):
                for t in prep_pair(0):
                    t()
                prep_gqbc(0)
                for p in range(NPAIR):
                    nxt = prep_pair(p + 1) if p + 1 < NPAIR else None
                    import os as _os
                    ilv = _os.environ.get("KILV", "1") == "1"
                    for step in core_pair(p):
                        for t in step:
                            t()
                        if ilv and nxt is not None:
                            for _ in range(2):
                                t = next(nxt, None)
                                if t is None:
                                    nxt = None
                                    break
                                t()
                    if nxt is not None:
                        for t in nxt:
                            t()
                    if p + 1 < NPAIR:
                        prep_gqbc(p + 1)

            def phaseC1(b):
                wo_al = zT[:, :, 0:1024]
                colsb = [qkv[:].rearrange("p a c -> p (a c)")[:, 0:4096]
                         .rearrange("p (k c) -> p k c", c=512),
                         qkv[:].rearrange("p a c -> p (a c)")[:, 4096:8192]
                         .rearrange("p (k c) -> p k c", c=512)]
                r2_al = vbuf[:].rearrange("p a n c -> p (a n c)").bitcast(F32)[:, 0:1024]
                z2t_al = sc8k[:, 0:1024]
                z2T_al = modT[:].rearrange("p a c -> p (a c)")[:, 0:1024] \
                    .rearrange("p (k c) -> p k c", c=128)
                xsq_al = gqbc[:, 0:1024]
                d_r2 = d_vbuf
                for kt in range(KT):
                    EM("gpsimd", "qG", lambda e, kt=kt: e.dma_start(
                        out=wo_al[:, kt, :], in_=wo_in[kt * 128:(kt + 1) * 128, :]),
                        writes=[d["zT"]])
                for ci in range(8):
                    cb = ci % 2
                    for dt in range(KT):
                        EM("sync", "qS", lambda e, ci=ci, cb=cb, dt=dt: e.dma_start(
                            out=colsb[cb][:, dt, :],
                            in_=collT_d[dt * 128:(dt + 1) * 128, ci * 512:(ci + 1) * 512]),
                            reads=[dp["collTd"]], writes=[d["qkv"]])
                    for tsub in range(4):
                        i = ci * 4 + tsub
                        gi = b * NT + i
                        pp = i % 2
                        xp = i % 2
                        EM("sync", "qS", lambda e, i=i, xp=xp: e.dma_start(
                            out=xa2[:, xp, :], in_=x_in[b, i * 128:(i + 1) * 128, :]),
                            writes=[d_xa[xp]])
                        for dch in range(2):
                            for dt in range(KT):
                                EM("tensor", "tC", lambda e, cb=cb, dt=dt, tsub=tsub, dch=dch, pp=pp: e.matmul(
                                    ps_al[:, pp, dch * 512:(dch + 1) * 512],
                                    colsb[cb][:, dt, tsub * 128:(tsub + 1) * 128],
                                    wo_al[:, dt, dch * 512:(dch + 1) * 512],
                                    start=(dt == 0), stop=(dt == KT - 1)),
                                    reads=[d["qkv"], d["zT"]], writes=[d_psal[pp]])
                        EM("vector", "vC", lambda e, pp=pp: e.tensor_tensor(
                            out=r2_al, in0=ps_al[:, pp, :], in1=bo_b[:], op=ALU.add),
                            reads=[d_psal[pp], dp["consts"]], writes=d_r2)
                        EM("vector", "vC", lambda e, xp=xp: e.tensor_tensor(
                            out=r2_al, in0=r2_al, in1=xa2[:, xp, :], op=ALU.add),
                            reads=[d_xa[xp]] + d_r2, writes=d_r2)
                        EM("vector", "vC", lambda e, gi=gi: e.tensor_reduce(
                            out=msc[:, 1:2], in_=r2_al, axis=AX.X, op=ALU.add),
                            reads=d_r2, writes=[dp["msc"]])
                        EM("vector", "vC", lambda e: e.tensor_scalar(
                            out=msc[:, 1:2], in0=msc[:, 1:2], scalar1=1.0 / D,
                            scalar2=None, op0=ALU.mult), reads=[dp["msc"]], writes=[dp["msc"]])
                        EM("scalar", "sC", lambda e, gi=gi: e.activation(
                            out=xsq_al, in_=r2_al, func=AF.Square,
                            accum_out=var2_t[:, gi:gi + 1]),
                            reads=d_r2, writes=[d["gqbc"], dp["var2"]])
                        EM("vector", "vC", lambda e: e.tensor_tensor(
                            out=msc[:, 2:3], in0=msc[:, 1:2], in1=msc[:, 1:2], op=ALU.mult),
                            reads=[dp["msc"]], writes=[dp["msc"]])
                        EM("vector", "vC", lambda e, gi=gi: e.scalar_tensor_tensor(
                            out=var2_t[:, gi:gi + 1], in0=var2_t[:, gi:gi + 1], scalar=1.0 / D,
                            in1=msc[:, 2:3], op0=ALU.mult, op1=ALU.subtract),
                            reads=[dp["msc"], dp["var2"]], writes=[dp["var2"]])
                        EM("vector", "vC", lambda e, gi=gi: e.tensor_scalar(
                            out=var2_t[:, gi:gi + 1], in0=var2_t[:, gi:gi + 1],
                            scalar1=LN_EPS, scalar2=None, op0=ALU.add),
                            reads=[dp["var2"]], writes=[dp["var2"]])
                        EM("vector", "vC", lambda e: e.tensor_scalar(
                            out=z2t_al, in0=r2_al, scalar1=msc[:, 1:2],
                            scalar2=None, op0=ALU.subtract),
                            reads=d_r2 + [dp["msc"]], writes=[d["sc8k"]])
                        EM("scalar", "qA", lambda e, gi=gi: e.dma_start(
                            out=z2_d[gi * 128:(gi + 1) * 128, :], in_=z2t_al),
                            reads=[d["sc8k"]], writes=[dp["z2d"]])
                        for kt in range(KT):
                            trv, dtr = tr_slot[kt % 2]
                            EM("tensor", "tC", lambda e, kt=kt, trv=trv: e.transpose(
                                trv, z2t_al[:, kt * 128:(kt + 1) * 128], ident[:]),
                                reads=[d["sc8k"], dp["consts"]], writes=[dtr])
                            EM("vector", "vC", lambda e, kt=kt, trv=trv: e.tensor_copy(
                                out=z2T_al[:, kt, :], in_=trv),
                                reads=[dtr], writes=d_modT)
                        for kt in range(KT):
                            EM("tensor", "tC", lambda e, kt=kt: e.matmul(
                                ps_pj[:, 0:E], z2T_al[:, kt, :], wr_s[:, kt, :],
                                start=(kt == 0), stop=(kt == KT - 1)),
                                reads=d_modT + [dp["wr"]], writes=[d_pspj])
                        EM("vector", "vC", lambda e, gi=gi: e.tensor_copy(
                            out=lgz_t[:, gi, :], in_=ps_pj[:, 0:E]),
                            reads=[d_pspj], writes=[dp["lgz"]])

            for b in range(BPC):
                phaseA(b)
                phaseB(b)
                phaseC1(b)

            EM("scalar", "sC", lambda e: e.activation(out=rstd2_t[:], in_=var2_t[:], func=AF.Sqrt),
               reads=[dp["var2"]], writes=[dp["rstd2"]])
            EM("vector", "vC", lambda e: e.reciprocal(out=rstd2_t[:], in_=rstd2_t[:]),
               reads=[dp["rstd2"]], writes=[dp["rstd2"]])

            z2c_al = sc8k[:, 0:1024]
            x2c_al = sc8k[:, 1024:2048]
            for gi in range(NTL):
                EM("sync", "qS", lambda e, gi=gi: e.dma_start(
                    out=z2c_al, in_=z2_d[gi * 128:(gi + 1) * 128, :]),
                    reads=[dp["z2d"]], writes=[d["sc8k"]])
                EM("vector", "vC", lambda e, gi=gi: e.tensor_scalar(
                    out=x2c_al, in0=z2c_al, scalar1=rstd2_t[:, gi:gi + 1],
                    scalar2=None, op0=ALU.mult),
                    reads=[d["sc8k"], dp["rstd2"]], writes=[d["sc8k"]])
                EM("vector", "vC", lambda e, gi=gi: e.tensor_scalar(
                    out=prb[:], in0=lgz_t[:, gi, :], scalar1=rstd2_t[:, gi:gi + 1],
                    scalar2=None, op0=ALU.mult),
                    reads=[dp["lgz"], dp["rstd2"]], writes=[dp["prb"]])
                EM("vector", "vC", lambda e: e.tensor_reduce(
                    out=msc[:, 3:4], in_=prb[:], axis=AX.X, op=ALU.max, negate=True),
                    reads=[dp["prb"]], writes=[dp["msc"]])
                EM("scalar", "sC", lambda e: e.activation(
                    out=prb[:], in_=prb[:], func=AF.Exp, bias=msc[:, 3:4]),
                    reads=[dp["prb"], dp["msc"]], writes=[dp["prb"]])
                EM("vector", "vC", lambda e: e.tensor_reduce(
                    out=msc[:, 4:5], in_=prb[:], axis=AX.X, op=ALU.add),
                    reads=[dp["prb"]], writes=[dp["msc"]])
                EM("vector", "vC", lambda e: e.reciprocal(out=msc[:, 5:6], in_=msc[:, 4:5]),
                   reads=[dp["msc"]], writes=[dp["msc"]])
                EM("vector", "vC", lambda e: e.tensor_scalar(
                    out=prb[:], in0=prb[:], scalar1=msc[:, 5:6], scalar2=None, op0=ALU.mult),
                    reads=[dp["prb"], dp["msc"]], writes=[dp["prb"]])
                EM("vector", "vC", lambda e: e.max(out=top8[:], in_=prb[:]),
                   reads=[dp["prb"]], writes=[dp["top"]])
                EM("vector", "vC", lambda e: e.max_index(out=idx8[:], in_max=top8[:], in_values=prb[:]),
                   reads=[dp["top"], dp["prb"]], writes=[dp["top"]])
                EM("vector", "vC", lambda e, gi=gi: e.tensor_copy(
                    out=ws_t[:, gi, :], in_=top8[:, 0:2]), reads=[dp["top"]], writes=[dp["route"]])
                EM("vector", "vC", lambda e, gi=gi: e.tensor_copy(
                    out=es_t[:, gi, :], in_=idx8[:, 0:2]), reads=[dp["top"]], writes=[dp["route"]])
                EM("vector", "vC", lambda e, gi=gi: e.tensor_scalar(
                    out=m1x[:], in0=io8_b[:], scalar1=es_t[:, gi, 0:1], scalar2=None,
                    op0=ALU.is_equal), reads=[dp["consts"], dp["route"]], writes=[dp["m12"]])
                EM("vector", "vC", lambda e, gi=gi: e.tensor_scalar(
                    out=m2x[:], in0=io8_b[:], scalar1=es_t[:, gi, 1:2], scalar2=None,
                    op0=ALU.is_equal), reads=[dp["consts"], dp["route"]], writes=[dp["m12"]])
                EM("vector", "vC", lambda e: e.tensor_tensor(
                    out=cab[:], in0=m1x[:], in1=m2x[:], op=ALU.add),
                    reads=[dp["m12"]], writes=[dp["cab"]])
                EM("vector", "vC", lambda e: e.tensor_copy(out=mm8[:], in_=cab[:]),
                   reads=[dp["cab"]], writes=[dp["mm8"]])
                EM("tensor", "tC", lambda e: e.matmul(
                    ps_co[:, 0, 0:E], utri[:], mm8[:], start=True, stop=True),
                    reads=[dp["consts"], dp["mm8"]], writes=[d_psco[0]])
                EM("tensor", "tC", lambda e: e.matmul(
                    ps_co[:, 1, 0:E], ones_f[:], run_row[:], start=True, stop=True),
                    reads=[dp["consts"], dp["run"]], writes=[d_psco[1]])
                EM("vector", "vC", lambda e: e.tensor_copy(out=runb_s[:], in_=ps_co[:, 1, 0:E]),
                   reads=[d_psco[1]], writes=[dp["runb"]])
                EM("vector", "vC", lambda e: e.tensor_tensor(
                    out=cab[:], in0=ps_co[:, 0, 0:E], in1=runb_s[:], op=ALU.add),
                    reads=[d_psco[0], dp["runb"], dp["cab"]], writes=[dp["cab"]])
                for j in range(2):
                    EM("vector", "vC", lambda e, gi=gi, j=j: e.scalar_tensor_tensor(
                        out=prb[:], in0=cab[:], scalar=1.0, in1=(m1x[:] if j == 0 else m2x[:]),
                        op0=ALU.mult, op1=ALU.mult, accum_out=rs_t[:, gi, j:j + 1]),
                        reads=[dp["cab"], dp["m12"]], writes=[dp["prb"], dp["route"]])
                EM("tensor", "tC", lambda e: e.matmul(
                    ps_co[0:1, 1, 0:E], ones_bcol[:], mm8[:], start=True, stop=True),
                    reads=[dp["consts"], dp["mm8"]], writes=[d_psco[1]])
                EM("vector", "vC", lambda e: e.tensor_tensor(
                    out=run_row[:], in0=run_row[:], in1=ps_co[0:1, 1, 0:E], op=ALU.add),
                    reads=[d_psco[1], dp["run"]], writes=[dp["run"]])
                for j in range(2):
                    EM("vector", "vC", lambda e, gi=gi, j=j: e.tensor_scalar(
                        out=msc[:, 8 + j:9 + j], in0=rs_t[:, gi, j:j + 1],
                        scalar1=float(PEXP - 1), scalar2=None, op0=ALU.min),
                        reads=[dp["route"]], writes=[dp["msc"]])
                    EM("vector", "vC", lambda e, gi=gi, j=j: e.scalar_tensor_tensor(
                        out=msc[:, 10 + j:11 + j], in0=es_t[:, gi, j:j + 1],
                        scalar=float(PEXP), in1=msc[:, 8 + j:9 + j],
                        op0=ALU.mult, op1=ALU.add),
                        reads=[dp["route"], dp["msc"]], writes=[dp["msc"]])
                    EM("vector", "vC", lambda e, j=j: e.tensor_copy(
                        out=idxu[:, j:j + 1], in_=msc[:, 10 + j:11 + j]),
                        reads=[dp["msc"]], writes=[dp["idxu"]])
                    EM("gpsimd", "qG", lambda e, j=j: e.indirect_dma_start(
                        out=xdisp_d, out_offset=IndirectOffsetOnAxis(idxu[:, j:j + 1], 0),
                        in_=x2c_al, in_offset=None),
                        reads=[dp["idxu"], d["sc8k"]], writes=[dp["xdispd"]])

            EM("tensor", "tC", lambda e: e.matmul(
                ps_co[:, 0, 0:E], rowmask[:], run_row[:], start=True, stop=True),
                reads=[dp["consts"], dp["run"]], writes=[d_psco[0]])
            EM("vector", "vC", lambda e: e.tensor_copy(out=cnt128[:, 0:E], in_=ps_co[:, 0, 0:E]),
               reads=[d_psco[0]], writes=[dp["cnt"]])
            EM("gpsimd", "qG", lambda e: e.dma_start(out=ccs_d, in_=cnt128[:]),
               reads=[dp["cnt"]], writes=[dp["ccsd"]])
            EM("gpsimd", "cc", lambda e: e.collective_compute(
                "AllReduce", ALU.add, replica_groups=[list(range(NC))],
                ins=[ccs_d.opt()], outs=[ccd_d.opt()]),
                reads=[dp["ccsd"]], writes=[dp["ccdd"]])
            EM("gpsimd", "qG", lambda e: e.dma_start(out=cnt128[:], in_=ccd_d),
               reads=[dp["ccdd"], dp["cnt"]], writes=[dp["cnt"]])
            EM("tensor", "tC", lambda e: e.matmul(
                ps_co[0:1, 0, 0:E], selcol[:], cnt128[:, 0:E], start=True, stop=True),
                reads=[dp["consts"], dp["cnt"]], writes=[d_psco[0]])
            EM("vector", "vC", lambda e: e.tensor_copy(out=cpx_b[0:1, :], in_=ps_co[0:1, 0, 0:E]),
               reads=[d_psco[0]], writes=[dp["cpx"]])
            EM("tensor", "tC", lambda e: e.matmul(
                ps_co[:, 1, 0:E], ones_f[:], cpx_b[0:1, :], start=True, stop=True),
                reads=[dp["consts"], dp["cpx"]], writes=[d_psco[1]])
            EM("vector", "vC", lambda e: e.tensor_copy(out=cpx_b[:], in_=ps_co[:, 1, 0:E]),
               reads=[d_psco[1]], writes=[dp["cpx"]])

        bld.barrier()
        with ExitStack() as ffn:
            FE = ffn.enter_context
            RHS = [(0, 1024), (1024, 1280)]
            RMAX = 1280
            xeT = FE(nc.sbuf_tensor("xeT", [128, 2, KT, RMAX], BF16))
            hT = FE(nc.sbuf_tensor("hT", [128, 8, RMAX], BF16))
            w1s = FE(nc.sbuf_tensor("w1s", [128, 2, KT, 1024], BF16))
            w2s = FE(nc.sbuf_tensor("w2s", [128, 2, 8, 1024], BF16))
            wst = FE(nc.sbuf_tensor("wst", [128, 2, 2, 1024], F32))
            yac = FE(nc.sbuf_tensor("yac", [128, RMAX // 128, 1024], F32))
            ps_h = FE(nc.psum_tensor("ps_h", [128, 2, 1024], F32))
            ps_y = FE(nc.psum_tensor("ps_y", [128, 2, 1024], F32))

            d_xeT = [Dep(), Dep()]
            d_hT = Dep()
            d_w1 = [Dep(), Dep()]
            d_w2 = [Dep(), Dep()]
            d_wst = [Dep(), Dep()]
            d_yac = Dep()
            d_psh = [Dep(), Dep()]
            d_psy = [Dep(), Dep()]

            def emit_xeT(idx):
                e_i, rh = idx // 2, idx % 2
                r0, rl = RHS[rh]
                base = e_i * PEXP + r0
                xp = idx % 2
                for kt in range(KT):
                    EM("sync", "qS", lambda e, base=base, kt=kt, xp=xp, rl=rl: e.dma_start(
                        out=xeT[:, xp, kt, 0:rl],
                        in_=xdisp_d[base:base + rl, kt * 128:(kt + 1) * 128],
                        transpose=True),
                        reads=[dp["xdispd"]], writes=[d_xeT[xp]])

            def emit_wload(we, wfb):
                wb = wfb % 2
                r = 0
                for (is_w1, buf, dbuf) in ((True, w1s, d_w1), (False, w2s, d_w2)):
                    for q in range(4):
                        sp = r % 2
                        for k2 in range(2):
                            kt = q * 2 + k2
                            if is_w1:
                                EM("sync", "qS", lambda e, we=we, wfb=wfb, kt=kt, k2=k2, sp=sp: e.dma_start(
                                    out=wst[:, sp, k2, :],
                                    in_=w1_in[we, kt * 128:(kt + 1) * 128, wfb * 1024:(wfb + 1) * 1024]),
                                    writes=[d_wst[sp]])
                            else:
                                EM("sync", "qS", lambda e, we=we, wfb=wfb, kt=kt, k2=k2, sp=sp: e.dma_start(
                                    out=wst[:, sp, k2, :],
                                    in_=w2_in[we, wfb * 1024 + kt * 128:wfb * 1024 + (kt + 1) * 128, :]),
                                    writes=[d_wst[sp]])
                        EM("vector", "vC", lambda e, wb=wb, q=q, buf=buf, sp=sp, dbuf=dbuf: e.tensor_copy(
                            out=buf[:, wb, q * 2:q * 2 + 2, :], in_=wst[:, sp]),
                            reads=[d_wst[sp]], writes=[dbuf[wb]])
                        r += 1

            emit_xeT(0)
            for idx in range(2 * E):
                e_i, rh = idx // 2, idx % 2
                r0, rl = RHS[rh]
                base = e_i * PEXP + r0
                xp = idx % 2
                nrt = rl // 128
                for fb in range(4):
                    wb = fb % 2
                    if idx == 0 and fb == 0:
                        emit_wload(0, 0)
                    if fb == 1 and idx + 1 < 2 * E:
                        emit_xeT(idx + 1)
                    if fb < 3:
                        emit_wload(e_i, fb + 1)
                    elif rh == 0:
                        emit_wload(e_i, 0)
                    elif e_i < E - 1:
                        emit_wload(e_i + 1, 0)
                    for ft in range(8):
                        hp = ft % 2
                        for rc in range(2):
                            for kt in range(KT):
                                EM("tensor", "tC", lambda e, ft=ft, rc=rc, kt=kt, wb=wb, hp=hp, xp=xp: e.matmul(
                                    ps_h[:, hp, rc * 512:(rc + 1) * 512],
                                    w1s[:, wb, kt, ft * 128:(ft + 1) * 128],
                                    xeT[:, xp, kt, rc * 512:(rc + 1) * 512],
                                    start=(kt == 0), stop=(kt == KT - 1)),
                                    reads=[d_w1[wb], d_xeT[xp]], writes=[d_psh[hp]])
                        EM("scalar", "sC", lambda e, ft=ft, hp=hp: e.activation(
                            out=hT[:, ft, 0:1024], in_=ps_h[:, hp, :],
                            func=AF.Gelu_apprx_tanh),
                            reads=[d_psh[hp]], writes=[d_hT])
                        if rl > 1024:
                            yp2 = ft % 2
                            for kt in range(KT):
                                EM("tensor", "tC", lambda e, ft=ft, kt=kt, wb=wb, yp2=yp2, xp=xp: e.matmul(
                                    ps_y[:, yp2, 0:256],
                                    w1s[:, wb, kt, ft * 128:(ft + 1) * 128],
                                    xeT[:, xp, kt, 1024:1280],
                                    start=(kt == 0), stop=(kt == KT - 1)),
                                    reads=[d_w1[wb], d_xeT[xp]], writes=[d_psy[yp2]])
                            EM("scalar", "sC", lambda e, ft=ft, yp2=yp2: e.activation(
                                out=hT[:, ft, 1024:1280], in_=ps_y[:, yp2, 0:256],
                                func=AF.Gelu_apprx_tanh),
                                reads=[d_psy[yp2]], writes=[d_hT])
                    for rt in range(nrt):
                        yp = rt % 2
                        for dch in range(2):
                            for ft in range(8):
                                EM("tensor", "tC", lambda e, rt=rt, dch=dch, ft=ft, wb=wb, yp=yp: e.matmul(
                                    ps_y[:, yp, dch * 512:(dch + 1) * 512],
                                    hT[:, ft, rt * 128:(rt + 1) * 128],
                                    w2s[:, wb, ft, dch * 512:(dch + 1) * 512],
                                    start=(ft == 0), stop=(ft == 7)),
                                    reads=[d_hT, d_w2[wb]], writes=[d_psy[yp]])
                        if fb == 0:
                            EM("vector", "vC", lambda e, rt=rt, yp=yp: e.tensor_copy(
                                out=yac[:, rt, :], in_=ps_y[:, yp, :]),
                                reads=[d_psy[yp]], writes=[d_yac])
                        else:
                            EM("vector", "vC", lambda e, rt=rt, yp=yp: e.tensor_tensor(
                                out=yac[:, rt, :], in0=yac[:, rt, :], in1=ps_y[:, yp, :],
                                op=ALU.add),
                                reads=[d_psy[yp], d_yac], writes=[d_yac])
                for rt in range(nrt):
                    EM("sync", "qS", lambda e, base=base, rt=rt: e.dma_start(
                        out=ybuf_d[base + rt * 128:base + (rt + 1) * 128, :],
                        in_=yac[:, rt, :]),
                        reads=[d_yac], writes=[dp["ybufd"]])

        bld.barrier()
        with ExitStack() as comb:
            CE = comb.enter_context
            z3 = CE(nc.sbuf_tensor("z3", [128, NTL, D], BF16))
            g1 = CE(nc.sbuf_tensor("g1", [128, D], F32))
            g2 = CE(nc.sbuf_tensor("g2", [128, D], F32))
            z2c = CE(nc.sbuf_tensor("z2c", [128, D], BF16))
            x2c = CE(nc.sbuf_tensor("x2c", [128, D], BF16))
            acc = CE(nc.sbuf_tensor("acc", [128, D], F32))
            xsq3 = CE(nc.sbuf_tensor("xsq3", [128, D], BF16))
            outf = CE(nc.sbuf_tensor("outf", [128, 2, D], F32))

            d_z3 = Dep()
            d_g = Dep()
            d_z2c = Dep()
            d_x2c = Dep()
            d_acc = Dep()
            d_xsq3 = Dep()
            d_outf = [Dep(), Dep()]

            for gi in range(NTL):
                EM("sync", "qS", lambda e, gi=gi: e.dma_start(
                    out=z2c[:], in_=z2_d[gi * 128:(gi + 1) * 128, :]),
                    reads=[dp["z2d"]], writes=[d_z2c])
                EM("vector", "vC", lambda e, gi=gi: e.tensor_scalar(
                    out=x2c[:], in0=z2c[:], scalar1=rstd2_t[:, gi:gi + 1],
                    scalar2=None, op0=ALU.mult),
                    reads=[d_z2c, dp["rstd2"]], writes=[d_x2c])
                for j in range(2):
                    mj = m1x if j == 0 else m2x
                    EM("vector", "vC", lambda e, gi=gi, j=j, mj=mj: e.tensor_scalar(
                        out=mj[:], in0=io8_b[:], scalar1=es_t[:, gi, j:j + 1], scalar2=None,
                        op0=ALU.is_equal), reads=[dp["consts"], dp["route"]], writes=[dp["m12"]])
                    EM("vector", "vC", lambda e, j=j, mj=mj: e.scalar_tensor_tensor(
                        out=prb[:], in0=cpx_b[:], scalar=1.0, in1=mj[:],
                        op0=ALU.mult, op1=ALU.mult, accum_out=msc[:, j:j + 1]),
                        reads=[dp["cpx"], dp["m12"]], writes=[dp["prb"], dp["msc"]])
                    EM("vector", "vC", lambda e, gi=gi, j=j: e.tensor_tensor(
                        out=msc[:, 2 + j:3 + j], in0=rs_t[:, gi, j:j + 1], in1=msc[:, j:j + 1],
                        op=ALU.add), reads=[dp["route"], dp["msc"]], writes=[dp["msc"]])
                    EM("vector", "vC", lambda e, j=j: e.tensor_scalar(
                        out=msc[:, 4 + j:5 + j], in0=msc[:, 2 + j:3 + j],
                        scalar1=CAP - 0.5, scalar2=None, op0=ALU.is_lt),
                        reads=[dp["msc"]], writes=[dp["msc"]])
                    EM("vector", "vC", lambda e, gi=gi, j=j: e.tensor_tensor(
                        out=msc[:, 6 + j:7 + j], in0=ws_t[:, gi, j:j + 1],
                        in1=msc[:, 4 + j:5 + j], op=ALU.mult),
                        reads=[dp["route"], dp["msc"]], writes=[dp["msc"]])
                    EM("vector", "vC", lambda e, gi=gi, j=j: e.tensor_scalar(
                        out=msc[:, 8 + j:9 + j], in0=rs_t[:, gi, j:j + 1],
                        scalar1=float(PEXP - 1), scalar2=None, op0=ALU.min),
                        reads=[dp["route"]], writes=[dp["msc"]])
                    EM("vector", "vC", lambda e, gi=gi, j=j: e.scalar_tensor_tensor(
                        out=msc[:, 10 + j:11 + j], in0=es_t[:, gi, j:j + 1],
                        scalar=float(PEXP), in1=msc[:, 8 + j:9 + j],
                        op0=ALU.mult, op1=ALU.add),
                        reads=[dp["route"], dp["msc"]], writes=[dp["msc"]])
                    EM("vector", "vC", lambda e, j=j: e.tensor_copy(
                        out=idxu[:, j:j + 1], in_=msc[:, 10 + j:11 + j]),
                        reads=[dp["msc"]], writes=[dp["idxu"]])
                    EM("gpsimd", "qG", lambda e, j=j: e.indirect_dma_start(
                        out=(g1 if j == 0 else g2)[:], out_offset=None,
                        in_=ybuf_d, in_offset=IndirectOffsetOnAxis(idxu[:, j:j + 1], 0)),
                        reads=[dp["idxu"], dp["ybufd"]], writes=[d_g])
                EM("vector", "vC", lambda e: e.scalar_tensor_tensor(
                    out=acc[:], in0=g1[:], scalar=msc[:, 6:7], in1=x2c[:],
                    op0=ALU.mult, op1=ALU.add),
                    reads=[d_g, dp["msc"], d_x2c], writes=[d_acc])
                EM("vector", "vC", lambda e: e.scalar_tensor_tensor(
                    out=acc[:], in0=g2[:], scalar=msc[:, 7:8], in1=acc[:],
                    op0=ALU.mult, op1=ALU.add),
                    reads=[d_g, dp["msc"], d_acc], writes=[d_acc])
                EM("vector", "vC", lambda e: e.tensor_tensor(
                    out=acc[:], in0=acc[:], in1=fb_b[:], op=ALU.add),
                    reads=[dp["consts"], d_acc], writes=[d_acc])
                EM("vector", "vC", lambda e: e.tensor_reduce(
                    out=msc[:, 12:13], in_=acc[:], axis=AX.X, op=ALU.add),
                    reads=[d_acc], writes=[dp["msc"]])
                EM("vector", "vC", lambda e: e.tensor_scalar(
                    out=msc[:, 12:13], in0=msc[:, 12:13], scalar1=1.0 / D,
                    scalar2=None, op0=ALU.mult), reads=[dp["msc"]], writes=[dp["msc"]])
                EM("scalar", "sC", lambda e, gi=gi: e.activation(
                    out=xsq3[:], in_=acc[:], func=AF.Square,
                    accum_out=var3_t[:, gi:gi + 1]),
                    reads=[d_acc], writes=[d_xsq3, dp["var3"]])
                EM("vector", "vC", lambda e: e.tensor_tensor(
                    out=msc[:, 13:14], in0=msc[:, 12:13], in1=msc[:, 12:13], op=ALU.mult),
                    reads=[dp["msc"]], writes=[dp["msc"]])
                EM("vector", "vC", lambda e, gi=gi: e.scalar_tensor_tensor(
                    out=var3_t[:, gi:gi + 1], in0=var3_t[:, gi:gi + 1], scalar=1.0 / D,
                    in1=msc[:, 13:14], op0=ALU.mult, op1=ALU.subtract),
                    reads=[dp["msc"], dp["var3"]], writes=[dp["var3"]])
                EM("vector", "vC", lambda e, gi=gi: e.tensor_scalar(
                    out=var3_t[:, gi:gi + 1], in0=var3_t[:, gi:gi + 1],
                    scalar1=LN_EPS, scalar2=None, op0=ALU.add),
                    reads=[dp["var3"]], writes=[dp["var3"]])
                EM("vector", "vC", lambda e, gi=gi: e.tensor_scalar(
                    out=z3[:, gi, :], in0=acc[:], scalar1=msc[:, 12:13],
                    scalar2=None, op0=ALU.subtract),
                    reads=[d_acc, dp["msc"]], writes=[d_z3])

            EM("scalar", "sC", lambda e: e.activation(out=rstd3_t[:], in_=var3_t[:], func=AF.Sqrt),
               reads=[dp["var3"]], writes=[dp["rstd3"]])
            EM("vector", "vC", lambda e: e.reciprocal(out=rstd3_t[:], in_=rstd3_t[:]),
               reads=[dp["rstd3"]], writes=[dp["rstd3"]])

            for gi in range(NTL):
                b, i = gi // NT, gi % NT
                op_ = gi % 2
                EM("vector", "vC", lambda e, gi=gi, op_=op_: e.tensor_scalar(
                    out=outf[:, op_, :], in0=z3[:, gi, :], scalar1=rstd3_t[:, gi:gi + 1],
                    scalar2=None, op0=ALU.mult),
                    reads=[d_z3, dp["rstd3"]], writes=[d_outf[op_]])
                EM("sync", "qS", lambda e, b=b, i=i, op_=op_: e.dma_start(
                    out=out_t[b, i * 128:(i + 1) * 128, :], in_=outf[:, op_, :]),
                    reads=[d_outf[op_]], writes=[dp["outd"]])

            snap = [(c.sem, c.val) for c in ctrs.values()]
            for pl in pools.values():
                snap += [(c.sem, c.val) for c in pl.subs]
            for ename in ("sync", "gpsimd", "vector", "scalar", "tensor"):
                self_q = bld.queues.setdefault(ename, [])
            bld.flush()
            for ename in ("sync", "gpsimd"):
                def fini(eng, snap=snap):
                    for (s, v) in snap:
                        eng.wait_ge(s, v)
                getattr(block, ename)(fini)

    return nc


def kernel(L0_attn_x, L0_attn_Wq, L0_attn_Wk, L0_attn_Wv, L0_attn_gq, L0_attn_gk,
           L0_attn_Wo, L0_attn_bo, L0_attn_ln_w, L0_attn_ln_b,
           L0_ffn_w_router, L0_ffn_w1, L0_ffn_w2, L0_ffn_bias):
    x = np.ascontiguousarray(np.asarray(L0_attn_x, dtype=np.float32))
    ident = np.eye(128, dtype=ml_dtypes.bfloat16)
    utri = np.triu(np.ones((128, 128), np.float32), 1).astype(ml_dtypes.bfloat16)
    iota8 = np.arange(E, dtype=np.float32).reshape(1, E)

    shared = {
        "wq_in": np.ascontiguousarray(np.asarray(L0_attn_Wq, np.float32)),
        "wk_in": np.ascontiguousarray(np.asarray(L0_attn_Wk, np.float32)),
        "wv_in": np.ascontiguousarray(np.asarray(L0_attn_Wv, np.float32)),
        "wo_in": np.ascontiguousarray(np.asarray(L0_attn_Wo, np.float32)),
        "gq_in": np.asarray(L0_attn_gq, np.float32).reshape(1, DH),
        "gk_in": np.asarray(L0_attn_gk, np.float32).reshape(1, DH),
        "bo_in": np.asarray(L0_attn_bo, np.float32).reshape(1, D),
        "lnw_in": np.asarray(L0_attn_ln_w, np.float32).reshape(D, 1),
        "lnb_in": np.asarray(L0_attn_ln_b, np.float32).reshape(D, 1),
        "wr_in": np.ascontiguousarray(np.asarray(L0_ffn_w_router, np.float32)),
        "w1_in": np.ascontiguousarray(np.asarray(L0_ffn_w1, np.float32)),
        "w2_in": np.ascontiguousarray(np.asarray(L0_ffn_w2, np.float32)),
        "fb_in": np.asarray(L0_ffn_bias, np.float32).reshape(1, D),
        "ident_in": ident,
        "utri_in": utri,
        "iota8_in": iota8,
        "e2map_in": np.kron(np.eye(2, dtype=np.float32), np.ones((1, 64), np.float32)).astype(ml_dtypes.bfloat16),
    }

    nc = build_nc()
    in_maps = []
    for c in range(NC):
        m = dict(shared)
        m["x_in"] = np.ascontiguousarray(x[c * BPC:(c + 1) * BPC])
        rm = np.zeros((1, 128), np.float32)
        rm[0, c] = 1.0
        m["rowmask_in"] = rm
        sc = np.zeros((128, 1), np.float32)
        sc[:c, 0] = 1.0
        m["selcol_in"] = sc
        in_maps.append(m)

    res = run_bass_kernel_spmd(nc, in_maps, core_ids=list(range(NC)))
    out = np.concatenate([np.asarray(r["out_t"]) for r in res.results], axis=0)
    return out.astype(np.float32)
